# version 02
# baseline (speedup 1.0000x reference)
import numpy as np
import ml_dtypes
from contextlib import ExitStack

import concourse.bass as bass
import concourse.mybir as mybir
from concourse.bass_utils import run_bass_kernel_spmd

F32 = mybir.dt.float32
BF16 = mybir.dt.bfloat16
AF = mybir.ActivationFunctionType
ALU = mybir.AluOpType
AX = mybir.AxisListType

S = 2048
D = 2048
NH = 16
DH = 128
RATIO = 16
NCMP = S // RATIO
WIN = 256
SCALE = DH ** -0.5
HID0 = 1792
NE = 8
HID1 = 8192
EPS = 1e-5
NT = S // 128
ND = D // 128
NT2 = S // 256
NB = D // 512

_BF = ml_dtypes.bfloat16
_CONST_CACHE = {}


def _consts():
    if _CONST_CACHE:
        return _CONST_CACHE
    c = {}
    c["ident"] = np.eye(128, dtype=_BF)
    c["identf"] = np.eye(128, dtype=np.float32)
    c["epsf"] = np.full((128, 1), EPS, dtype=np.float32)
    c["ones_rowS"] = np.ones((1, S), dtype=_BF)
    c["ones_row_f"] = np.ones((1, 128), dtype=np.float32)
    c["ident8f"] = np.eye(8, dtype=np.float32)
    c["ones_col"] = np.ones((128, 1), dtype=_BF)
    c["ones_row"] = np.ones((1, 128), dtype=_BF)
    A = np.zeros((S, NCMP), dtype=np.float32)
    A[np.arange(S), np.arange(S) // RATIO] = 1.0 / RATIO
    c["avg"] = A.astype(_BF)
    inv = 1.0 / (10000.0 ** (np.arange(0, DH, 2, dtype=np.float64) / DH))
    ang = np.outer(inv, np.arange(S, dtype=np.float64))
    cos = np.cos(ang).astype(np.float32)
    sin = np.sin(ang).astype(np.float32)
    c["cs1"] = np.concatenate([cos, sin], axis=0)
    c["cs2"] = np.concatenate([sin, cos], axis=0)
    i64 = np.eye(64, dtype=_BF)
    c["rl_e"] = np.concatenate([i64, -i64], axis=0)
    c["rl_o"] = np.concatenate([i64, i64], axis=0)
    p = np.arange(128)[:, None]
    f = np.arange(128)[None, :]
    c["m_w0"] = (f >= p).astype(_BF)
    c["m_w2"] = (f < p).astype(_BF)
    c["m_ones"] = np.ones((128, 128), dtype=_BF)
    c["m_zero"] = np.zeros((128, 128), dtype=_BF)
    v = np.arange(248)[:, None] - 120
    fq = np.arange(128)[None, :]
    c["m_cmp"] = (v < (fq + 1) // 16).astype(_BF)
    i = np.arange(128)
    pc = np.zeros((128, 128), dtype=np.int64)
    for b in range(7):
        pc += ((i[:, None] >> b) & 1) * ((i[None, :] >> b) & 1)
    c["h128"] = (np.where(pc % 2 == 0, 1.0, -1.0) / np.sqrt(8192.0)).astype(_BF)
    _CONST_CACHE.update(c)
    return c


def _rope_perm():
    perm = np.zeros(D, dtype=np.int64)
    for h in range(NH):
        b = h * DH
        perm[b:b + 64] = b + 2 * np.arange(64)
        perm[b + 64:b + 128] = b + 2 * np.arange(64) + 1
    return perm


class Slot:

    def __init__(self, bb, name):
        self.sem = bb.ctx.enter_context(bb.nc.semaphore(name))
        self.n = 0
        self.base = 0

    def rebase(self):
        self.base = self.n
        return self

    def put(self, eng, out, in_, **kw):
        eng.dma_start(out, in_, **kw).then_inc(self.sem, 16)
        self.n += 16
        return self.n

    def wait(self, eng, val=None):
        eng.wait_ge(self.sem, self.n if val is None else val)

    def wait_rel(self, eng, rel):
        eng.wait_ge(self.sem, self.base + rel)


class BB:
    def __init__(self, nc, ctx):
        self.nc = nc
        self.ctx = ctx
        self.sems = {}
        self.cnt = {}
        self.io = {}
        self._n = 0

    def u(self):
        self._n += 1
        return "u%d" % self._n

    def sem(self, name):
        if name not in self.sems:
            self.sems[name] = self.ctx.enter_context(self.nc.semaphore(name))
            self.cnt[name] = 0
        return self.sems[name]

    def inc(self, name, by=1):
        self.cnt[name] += by
        return self.cnt[name]

    def slot(self, name):
        if name not in self.sems:
            self.sems[name] = Slot(self, name)
        return self.sems[name].rebase()


def emit_preln_xt(bb, x_in, xt_scr, pfx):
    nc = bb.nc
    ss, sm, sv = "PNs", "PNm", "PNv"
    for _s in (ss, sm, sv):
        bb.sem(_s)
    bs, bm, bv = bb.cnt[ss], bb.cnt[sm], bb.cnt[sv]
    with ExitStack() as st:
        e = st.enter_context
        xs = [e(nc.sbuf_tensor(f"{pfx}xs{i}", [128, D], F32)) for i in range(2)]
        sq = [e(nc.sbuf_tensor(f"{pfx}sq{i}", [128, 2, D], BF16)) for i in range(2)]
        xln = [e(nc.sbuf_tensor(f"{pfx}xln{i}", [128, D], BF16)) for i in range(2)]
        xtrow = e(nc.sbuf_tensor(f"{pfx}xtrow", [128, ND, 256], BF16))
        idsb = e(nc.sbuf_tensor(f"{pfx}id", [128, 128], BF16))
        epsb = e(nc.sbuf_tensor(f"{pfx}eps", [128, 1], F32))
        stt = e(nc.sbuf_tensor(f"{pfx}stt", [128, 2, 8], F32))
        ptp = e(nc.psum_tensor(f"{pfx}ptp", [128, 2, 32, 128], BF16))
        sx = [bb.slot("PNdx%d" % i) for i in range(2)]
        sxt = bb.slot("PNdt")
        scn = bb.slot("PNdc")

        with nc.Block() as blk:
            @blk.gpsimd
            def _(g):
                scn.put(g, epsb[:, :], bb.io["c_epsf"][:, :])
                scn.put(g, idsb[:, :], bb.io["c_ident"][:, :])
                for m2 in range(NT2 + 1):
                    if m2 < NT2:
                        for sub in range(2):
                            m = 2 * m2 + sub
                            if m >= 2:
                                g.wait_ge(bb.sem(ss), bs + m - 1)
                            sx[m % 2].put(g, xs[m % 2][:, :],
                                          x_in[m * 128:(m + 1) * 128, :])
                    if m2 >= 1:
                        g.wait_ge(bb.sem(sv), bv + m2)
                        for dt in range(ND):
                            sxt.put(g, xt_scr[dt, m2 - 1], xtrow[:, dt, :])
                sxt.wait(g)

            @blk.scalar
            def _(s):
                scn.wait_rel(s, 32)
                for m in range(NT):
                    i = m % 2
                    s1, s2 = stt[:, i, 0:1], stt[:, i, 1:2]
                    mean, s2d = stt[:, i, 2:3], stt[:, i, 3:4]
                    tvd, var = stt[:, i, 4:5], stt[:, i, 5:6]
                    lnv, rstd = stt[:, i, 6:7], stt[:, i, 7:8]
                    sx[i].wait_rel(s, 16 * (m // 2 + 1))
                    s.activation(sq[i][:, 0, :], xs[i][:, :], AF.Copy,
                                 accum_out=s1)
                    s.activation(sq[i][:, 1, :], xs[i][:, :], AF.Square,
                                 accum_out=s2)
                    s.drain()
                    s.activation(mean, s1, AF.Copy, scale=1.0 / D)
                    s.activation(s2d, s2, AF.Copy, scale=1.0 / D)
                    s.drain()
                    s.activation(tvd, mean, AF.Square)
                    s.drain()
                    s.activation(var, tvd, AF.Identity, scale=-1.0, bias=s2d)
                    s.drain()
                    s.activation(lnv, var, AF.Ln, bias=epsb[:, :])
                    s.drain()
                    s.activation(rstd, lnv, AF.Exp, scale=-0.5)
                    s.drain()
                    s.activation(tvd, rstd, AF.Copy, scale=-1.0)
                    s.drain()
                    s.activation(s2d, mean, AF.Copy, scale=tvd)
                    s.drain()
                    if m >= 2:
                        s.wait_ge(bb.sem(sm), bm + m // 2)
                    s.activation(xln[i][:, :], xs[i][:, :], AF.Identity,
                                 scale=rstd, bias=s2d).then_inc(bb.sem(ss), 1)
                    bb.inc(ss)

            @blk.tensor
            def _(t):
                scn.wait_rel(t, 32)
                for m2 in range(NT2):
                    if m2 >= 2:
                        t.wait_ge(bb.sem(sv), bv + m2 - 1)
                    for sub in range(2):
                        m = 2 * m2 + sub
                        t.wait_ge(bb.sem(ss), bs + m + 1)
                        for dt in range(ND):
                            tp = t.transpose(ptp[:, m2 % 2, sub * ND + dt, :],
                                        xln[m % 2][:, dt * 128:(dt + 1) * 128],
                                        idsb[:, :])
                    tp.then_inc(bb.sem(sm), 1)
                    bb.inc(sm)

            @blk.vector
            def _(v):
                for m2 in range(NT2):
                    v.wait_ge(bb.sem(sm), bm + m2 + 1)
                    if m2 >= 1:
                        sxt.wait_rel(v, 256 * m2)
                    for sub in range(2):
                        for dt in range(ND):
                            cp = v.tensor_copy(
                                xtrow[:, dt, sub * 128:(sub + 1) * 128],
                                ptp[:, m2 % 2, sub * ND + dt, :])
                    cp.then_inc(bb.sem(sv), 1)
                    bb.inc(sv)


def emit_out_ln_residual(bb, x_in, x_out, pfx, emit_mm, mm_waits=None,
                         pre_g=None):
    nc = bb.nc
    ss, sm, sv = "OTs", "OTm", "OTv"
    for _s in (ss, sm, sv):
        bb.sem(_s)
    bs, bm, bv = bb.cnt[ss], bb.cnt[sm], bb.cnt[sv]
    with ExitStack() as st:
        e = st.enter_context
        xs = [e(nc.sbuf_tensor(f"{pfx}xs{i}", [128, D], F32)) for i in range(2)]
        sq = [e(nc.sbuf_tensor(f"{pfx}sq{i}", [128, 2, D], BF16)) for i in range(2)]
        tmp = [e(nc.sbuf_tensor(f"{pfx}tmp{i}", [128, D], F32)) for i in range(2)]
        xo = [e(nc.sbuf_tensor(f"{pfx}xo{i}", [128, D], F32)) for i in range(2)]
        epsb = e(nc.sbuf_tensor(f"{pfx}eps", [128, 1], F32))
        stt = e(nc.sbuf_tensor(f"{pfx}stt", [128, 2, 8], F32))
        pE = e(nc.psum_tensor(f"{pfx}pE", [128, 2, D], F32))
        sxi = [bb.slot("OTdi%d" % i) for i in range(2)]
        sxo = [bb.slot("OTdo%d" % i) for i in range(2)]
        sce = bb.slot("OTdce")

        with nc.Block() as blk:
            @blk.gpsimd
            def _(g):
                sce.put(g, epsb[:, :], bb.io["c_epsf"][:, :])
                for m in range(NT + 2):
                    if m < NT:
                        if pre_g is not None:
                            pre_g(g, m)
                        if m >= 2:
                            g.wait_ge(bb.sem(sv), bv + 2 * (m - 1))
                        sxi[m % 2].put(g, xs[m % 2][:, :],
                                       x_in[m * 128:(m + 1) * 128, :])
                    if m >= 2:
                        mm_ = m - 2
                        g.wait_ge(bb.sem(sv), bv + 2 * mm_ + 2)
                        sxo[mm_ % 2].put(g, x_out[mm_ * 128:(mm_ + 1) * 128, :],
                                         xo[mm_ % 2][:, :])
                sxo[0].wait(g)
                sxo[1].wait(g)

            @blk.tensor
            def _(t):
                for m in range(NT):
                    if m >= 2:
                        t.wait_ge(bb.sem(sv), bv + 2 * (m - 1) + 2)
                        t.wait_ge(bb.sem(ss), bs + m - 1)
                    if mm_waits is not None:
                        mm_waits(t, m)
                    last = emit_mm(t, m, pE[:, m % 2, :])
                    last.then_inc(bb.sem(sm), 1)
                    bb.inc(sm)

            @blk.scalar
            def _(s):
                sce.wait_rel(s, 16)
                for m in range(NT):
                    i = m % 2
                    s1, s2 = stt[:, i, 0:1], stt[:, i, 1:2]
                    mean, s2d = stt[:, i, 2:3], stt[:, i, 3:4]
                    tvd, var = stt[:, i, 4:5], stt[:, i, 5:6]
                    lnv, rstd = stt[:, i, 6:7], stt[:, i, 7:8]
                    s.wait_ge(bb.sem(sm), bm + m + 1)
                    s.activation(sq[i][:, 0, :], pE[:, i, :], AF.Copy,
                                 accum_out=s1)
                    s.activation(sq[i][:, 1, :], pE[:, i, :], AF.Square,
                                 accum_out=s2)
                    s.drain()
                    s.activation(mean, s1, AF.Copy, scale=1.0 / D)
                    s.activation(s2d, s2, AF.Copy, scale=1.0 / D)
                    s.drain()
                    s.activation(tvd, mean, AF.Square)
                    s.drain()
                    s.activation(var, tvd, AF.Identity, scale=-1.0, bias=s2d)
                    s.drain()
                    s.activation(lnv, var, AF.Ln, bias=epsb[:, :])
                    s.drain()
                    s.activation(rstd, lnv, AF.Exp, scale=-0.5
                                 ).then_inc(bb.sem(ss), 1)
                    bb.inc(ss)

            @blk.vector
            def _(v):
                for m in range(NT):
                    i = m % 2
                    mean, rstd = stt[:, i, 2:3], stt[:, i, 7:8]
                    v.wait_ge(bb.sem(ss), bs + m + 1)
                    v.tensor_scalar(tmp[i][:, :], pE[:, i, :], mean, rstd,
                                    op0=ALU.subtract, op1=ALU.mult
                                    ).then_inc(bb.sem(sv), 1)
                    bb.inc(sv)
                    v.drain()
                    sxi[i].wait_rel(v, 16 * (m // 2 + 1))
                    if m >= 2:
                        sxo[i].wait_rel(v, 16 * (m // 2))
                    v.tensor_tensor(xo[i][:, :], tmp[i][:, :], xs[i][:, :],
                                    op=ALU.add).then_inc(bb.sem(sv), 1)
                    bb.inc(sv)


def emit_l0_attn(bb, x_in, x_out, pfx="a0"):
    nc = bb.nc
    io = bb.io
    xt = io[pfx + "_xt"]
    v_scr = io[pfx + "_vscr"]

    emit_preln_xt(bb, x_in, xt, pfx + "A")

    with ExitStack() as st0:
        e0 = st0.enter_context
        OT = e0(nc.sbuf_tensor(pfx + "OT", [128, ND, S], BF16))
        onesc = e0(nc.sbuf_tensor(pfx + "onesc", [128, 1], BF16))
        onesr = e0(nc.sbuf_tensor(pfx + "onesr", [1, 128], BF16))
        idsb = e0(nc.sbuf_tensor(pfx + "id", [128, 128], BF16))
        scst = bb.slot("A0dc")

        with nc.Block() as blk:
            @blk.gpsimd
            def _(g):
                scst.put(g, onesc[:, :], io["c_ones_col"][:, :])
                scst.put(g, onesr[:, :], io["c_ones_row"][:, :])
                scst.put(g, idsb[:, :], io["c_ident"][:, :])
                scst.wait(g)

        _l0_v_phase(bb, xt, v_scr, pfx)

        with ExitStack() as stC:
            eC = stC.enter_context
            avgb = eC(nc.sbuf_tensor(pfx + "avg", [128, NT, 128], BF16))
            maskq = eC(nc.sbuf_tensor(pfx + "mq", [128, NT, 512], BF16))
            cs1b = eC(nc.sbuf_tensor(pfx + "cs1", [128, S], BF16))
            cs2b = eC(nc.sbuf_tensor(pfx + "cs2", [128, S], BF16))
            rleb = eC(nc.sbuf_tensor(pfx + "rle", [128, 64], BF16))
            rlob = eC(nc.sbuf_tensor(pfx + "rlo", [128, 64], BF16))
            sck = bb.slot("A0dk")
            svh = bb.slot("A0dv")
            with nc.Block() as blk:
                @blk.gpsimd
                def _(g):
                    sck.put(g, cs1b[:, :], io["c_cs1"][:, :])
                    sck.put(g, cs2b[:, :], io["c_cs2"][:, :])
                    sck.put(g, rleb[:, :], io["c_rl_e"][:, :])
                    sck.put(g, rlob[:, :], io["c_rl_o"][:, :])
                    for j in range(NT):
                        sck.put(g, avgb[:, j, :],
                                io["c_avg"][j * 128:(j + 1) * 128, :])
                    for qt in range(NT):
                        sck.put(g, maskq[:, qt, 0:128],
                                io["c_m_cmp"][120 - 8 * qt:248 - 8 * qt, :])
                        src2 = io["c_m_w2"] if qt >= 2 else io["c_m_zero"]
                        src1 = io["c_m_ones"] if qt >= 1 else io["c_m_zero"]
                        sck.put(g, maskq[:, qt, 128:256], src2[:, :])
                        sck.put(g, maskq[:, qt, 256:384], src1[:, :])
                        sck.put(g, maskq[:, qt, 384:512], io["c_m_w0"][:, :])
                    sck.wait(g)

            for grp in range(NH // 2):
                _l0_attn_group(bb, grp, xt, v_scr, OT, onesc, onesr, idsb,
                               avgb, maskq, cs1b, cs2b, rleb, rlob, svh, pfx)

        with ExitStack() as st:
            wob = st.enter_context(nc.sbuf_tensor(pfx + "wob", [128, ND, D], BF16))
            swo = bb.slot("A0dw")
            with nc.Block() as blk:
                @blk.gpsimd
                def _(g):
                    for k in range(ND):
                        swo.put(g, wob[:, k, :],
                                io[pfx + "_woT"][k * 128:(k + 1) * 128, :])
                    swo.wait(g)

            def emit_mm(t, m, psum):
                for k in range(ND):
                    for nb in range(NB):
                        mm = t.matmul(psum[:, nb * 512:(nb + 1) * 512],
                                 OT[:, k, m * 128:(m + 1) * 128],
                                 wob[:, k, nb * 512:(nb + 1) * 512],
                                 start=(k == 0), stop=(k == ND - 1))
                return mm

            emit_out_ln_residual(bb, x_in, x_out, pfx + "E", emit_mm)


def _l0_v_phase(bb, xt, v_scr, pfx):
    nc = bb.nc
    io = bb.io
    with ExitStack() as st:
        e = st.enter_context
        wvb = e(nc.sbuf_tensor(pfx + "wvb", [128, ND, D], BF16))
        xtb = e(nc.sbuf_tensor(pfx + "xtb", [128, 2, ND, 256], BF16))
        vout = [e(nc.sbuf_tensor(f"{pfx}vo{i}", [128, D], BF16)) for i in range(2)]
        pV = e(nc.psum_tensor(pfx + "pV", [128, 2, D], F32))
        swv = bb.slot("Vdwv")
        sxtb = [bb.slot("Vdxb%d" % i) for i in range(2)]
        svo = [bb.slot("Vdvo%d" % i) for i in range(2)]
        sm, sv = "BVm", "BVv"
        bb.sem(sm), bb.sem(sv)
        bm, bv = bb.cnt[sm], bb.cnt[sv]

        with nc.Block() as blk:
            @blk.gpsimd
            def _(g):
                for k in range(ND):
                    swv.put(g, wvb[:, k, :],
                            io[pfx + "_wvT"][k * 128:(k + 1) * 128, :])
                for m2 in range(NT2 + 1):
                    if m2 < NT2:
                        if m2 >= 2:
                            g.wait_ge(bb.sem(sm), bm + 2 * (m2 - 1))
                        for k in range(ND):
                            sxtb[m2 % 2].put(g, xtb[:, m2 % 2, k, :], xt[k, m2])
                    if m2 >= 1:
                        for sub in range(2):
                            m = 2 * (m2 - 1) + sub
                            g.wait_ge(bb.sem(sv), bv + m + 1)
                            for h in range(NH):
                                svo[m % 2].put(g, v_scr[h, m],
                                               vout[m % 2][:, h * 128:(h + 1) * 128])
                svo[0].wait(g)
                svo[1].wait(g)

            @blk.tensor
            def _(t):
                swv.wait(t)
                for m2 in range(NT2):
                    sxtb[m2 % 2].wait_rel(t, 16 * ND * (m2 // 2 + 1))
                    for sub in range(2):
                        m = 2 * m2 + sub
                        if m >= 2:
                            t.wait_ge(bb.sem(sv), bv + m - 1)
                        for k in range(ND):
                            for nb in range(NB):
                                mm = t.matmul(pV[:, m % 2, nb * 512:(nb + 1) * 512],
                                         xtb[:, m2 % 2, k, sub * 128:(sub + 1) * 128],
                                         wvb[:, k, nb * 512:(nb + 1) * 512],
                                         start=(k == 0), stop=(k == ND - 1))
                        mm.then_inc(bb.sem(sm), 1)
                        bb.inc(sm)

            @blk.scalar
            def _(s):
                for m in range(NT):
                    s.wait_ge(bb.sem(sm), bm + m + 1)
                    if m >= 2:
                        svo[m % 2].wait_rel(s, 16 * NH * (m // 2))
                    s.copy(vout[m % 2][:, :], pV[:, m % 2, :]
                           ).then_inc(bb.sem(sv), 1)
                    bb.inc(sv)


def _l0_attn_group(bb, grp, xt, v_scr, OT, onesc, onesr, idsb, avgb, maskq,
                   cs1b, cs2b, rleb, rlob, svh, pfx):
    nc = bb.nc
    io = bb.io
    g0 = grp * 2
    un = pfx + bb.u()
    with ExitStack() as st:
        e = st.enter_context
        wqg = e(nc.sbuf_tensor(un + "wqg", [128, ND, 256], BF16))
        wkg = e(nc.sbuf_tensor(un + "wkg", [128, ND, 256], BF16))
        xtq = e(nc.sbuf_tensor(un + "xtq", [128, 2, ND, 512], BF16))
        QT = e(nc.sbuf_tensor(un + "QT", [128, 2, S], BF16))
        KT = e(nc.sbuf_tensor(un + "KT", [128, 2, S], BF16))
        QTr = e(nc.sbuf_tensor(un + "QTr", [128, 2, S], BF16))
        KTr = e(nc.sbuf_tensor(un + "KTr", [128, 2, S], BF16))
        rP = e(nc.sbuf_tensor(un + "rP", [128, S], BF16))
        rP2 = e(nc.sbuf_tensor(un + "rP2", [128, S], BF16))
        swg = bb.slot("Gdw")
        sxq = [bb.slot("Gdq%d" % i) for i in range(2)]

        sm, ss = "BGm", "BGs"
        bb.sem(sm), bb.sem(ss)
        bm, bs = bb.cnt[sm], bb.cnt[ss]

        with ExitStack() as st1:
            pqk = st1.enter_context(
                nc.psum_tensor(un + "pqk", [128, 2, 4, 512], F32))
            with nc.Block() as blk:
                @blk.gpsimd
                def _(g):
                    for k in range(ND):
                        swg.put(g, wqg[:, k, :],
                                io[pfx + "_wqT"][k * 128:(k + 1) * 128,
                                                 g0 * 128:(g0 + 2) * 128])
                    for k in range(ND):
                        swg.put(g, wkg[:, k, :],
                                io[pfx + "_wkT"][k * 128:(k + 1) * 128,
                                                 g0 * 128:(g0 + 2) * 128])
                    for tq in range(4):
                        if tq >= 2:
                            g.wait_ge(bb.sem(sm), bm + tq - 1)
                        for k in range(ND):
                            sxq[tq % 2].put(g, xtq[:, tq % 2, k, 0:256],
                                            xt[k, 2 * tq])
                            sxq[tq % 2].put(g, xtq[:, tq % 2, k, 256:512],
                                            xt[k, 2 * tq + 1])

                @blk.tensor
                def _(t):
                    swg.wait(t)
                    for tq in range(4):
                        sxq[tq % 2].wait_rel(t, 32 * ND * (tq // 2 + 1))
                        if tq >= 2:
                            t.wait_ge(bb.sem(ss), bs + 4 * (tq - 1))
                        for k in range(ND):
                            for hi in range(2):
                                t.matmul(pqk[:, tq % 2, hi, :],
                                         wqg[:, k, hi * 128:(hi + 1) * 128],
                                         xtq[:, tq % 2, k, :],
                                         start=(k == 0), stop=(k == ND - 1))
                                mm = t.matmul(pqk[:, tq % 2, 2 + hi, :],
                                         wkg[:, k, hi * 128:(hi + 1) * 128],
                                         xtq[:, tq % 2, k, :],
                                         start=(k == 0), stop=(k == ND - 1))
                        mm.then_inc(bb.sem(sm), 1)
                        bb.inc(sm)

                @blk.scalar
                def _(s):
                    for tq in range(4):
                        s.wait_ge(bb.sem(sm), bm + tq + 1)
                        s.copy(QT[:, 0, tq * 512:(tq + 1) * 512],
                               pqk[:, tq % 2, 0, :])
                        s.copy(QT[:, 1, tq * 512:(tq + 1) * 512],
                               pqk[:, tq % 2, 1, :])
                        s.copy(KT[:, 0, tq * 512:(tq + 1) * 512],
                               pqk[:, tq % 2, 2, :])
                        s.copy(KT[:, 1, tq * 512:(tq + 1) * 512],
                               pqk[:, tq % 2, 3, :]).then_inc(bb.sem(ss), 4)
                        bb.inc(ss, 4)

        with ExitStack() as str_:
            prr = str_.enter_context(nc.psum_tensor(un + "prr", [128, S], F32))
            smr, svr, ssr = "BRm", "BRv", "BRs"
            bb.sem(smr), bb.sem(svr), bb.sem(ssr)
            bmr, bvr, bsr = bb.cnt[smr], bb.cnt[svr], bb.cnt[ssr]
            cases = [(hi, src, dst) for hi in range(2)
                     for (src, dst) in ((QT, QTr), (KT, KTr))]
            with nc.Block() as blk:
                @blk.vector
                def _(v):
                    for i, (hi, src, dst) in enumerate(cases):
                        if i >= 1:
                            v.wait_ge(bb.sem(smr), bmr + i)
                        v.tensor_tensor(rP[0:64, :], src[0:64, hi, :],
                                        cs1b[0:64, :], op=ALU.mult)
                        v.tensor_tensor(rP[64:128, :], src[64:128, hi, :],
                                        cs1b[64:128, :], op=ALU.mult)
                        v.tensor_tensor(rP2[0:64, :], src[0:64, hi, :],
                                        cs2b[0:64, :], op=ALU.mult)
                        v.tensor_tensor(rP2[64:128, :], src[64:128, hi, :],
                                        cs2b[64:128, :], op=ALU.mult
                                        ).then_inc(bb.sem(svr), 1)
                        bb.inc(svr)

                @blk.tensor
                def _(t):
                    for i, (hi, src, dst) in enumerate(cases):
                        t.wait_ge(bb.sem(svr), bvr + i + 1)
                        if i >= 1:
                            t.wait_ge(bb.sem(ssr), bsr + i)
                        for nb in range(NB):
                            t.matmul(prr[0:64, nb * 512:(nb + 1) * 512],
                                     rleb[:, :], rP[:, nb * 512:(nb + 1) * 512],
                                     start=True, stop=True, skip_group_check=True)
                            mm = t.matmul(prr[64:128, nb * 512:(nb + 1) * 512],
                                     rlob[:, :], rP2[:, nb * 512:(nb + 1) * 512],
                                     start=True, stop=True, skip_group_check=True)
                        mm.then_inc(bb.sem(smr), 1)
                        bb.inc(smr)

                @blk.scalar
                def _(s):
                    for i, (hi, src, dst) in enumerate(cases):
                        s.wait_ge(bb.sem(smr), bmr + i + 1)
                        s.copy(dst[:, hi, :], prr[:, :]).then_inc(bb.sem(ssr), 1)
                        bb.inc(ssr)

        for hi in range(2):
            _l0_attn_head(bb, g0 + hi, hi, QTr, KTr, v_scr, OT, onesc, onesr,
                          idsb, avgb, maskq, svh, pfx)


def _l0_attn_head(bb, h, hi, QTr, KTr, v_scr, OT, onesc, onesr, idsb, avgb,
                  maskq, svh, pfx):
    nc = bb.nc
    sm, ss, sv = "BHm", "BHs", "BHv"
    for _s in (sm, ss, sv):
        bb.sem(_s)
    un = pfx + bb.u()
    with ExitStack() as st:
        e = st.enter_context
        vh = e(nc.sbuf_tensor(un + "vh", [128, NT, 128], BF16))
        ktok = e(nc.sbuf_tensor(un + "ktok", [128, NT, 128], BF16))
        ckb = e(nc.sbuf_tensor(un + "ckb", [128, 128], BF16))
        cvb = e(nc.sbuf_tensor(un + "cvb", [128, 128], BF16))
        sqs = e(nc.sbuf_tensor(un + "sqs", [128, S], BF16))
        nrm = e(nc.sbuf_tensor(un + "nrm", [1, 8], F32))
        c11b = e(nc.sbuf_tensor(un + "c11b", [1, 1], BF16))
        negC = e(nc.sbuf_tensor(un + "negC", [128, 1], F32))

        with nc.Block() as blk:
            @blk.gpsimd
            def _(g):
                for m in range(NT):
                    svh.put(g, vh[:, m, :], v_scr[h, m])
                svh.wait(g)

        with ExitStack() as sta:
            ptk = sta.enter_context(nc.psum_tensor(un + "ptk", [128, NT, 128],
                                                   BF16))
            with nc.Block() as blk:
                @blk.tensor
                def _(t):
                    for j in range(NT):
                        tp = t.transpose(ptk[:, j, :],
                                    KTr[:, hi, j * 128:(j + 1) * 128], idsb[:, :])
                    tp.then_inc(bb.sem(sm), 1)
                    bb.inc(sm)

                @blk.vector
                def _(v):
                    v.wait_ge(bb.sem(sm), bb.cnt[sm])
                    for j in range(NT):
                        cp = v.tensor_copy(ktok[:, j, :], ptk[:, j, :])
                    cp.then_inc(bb.sem(sv), 1)
                    bb.inc(sv)

        with ExitStack() as st1:
            e1 = st1.enter_context
            pck = e1(nc.psum_tensor(un + "pck", [128, 128], F32))
            pcv = e1(nc.psum_tensor(un + "pcv", [128, 128], F32))
            pnr = e1(nc.psum_tensor(un + "pnr", [1, S], F32))
            pb1 = e1(nc.psum_tensor(un + "pb1", [128, 1], F32))

            with nc.Block() as blk:
                @blk.tensor
                def _(t):
                    for j in range(NT):
                        t.matmul(pck[:, :], ktok[:, j, :], avgb[:, j, :],
                                 start=(j == 0), stop=(j == NT - 1))
                    for j in range(NT):
                        mm = t.matmul(pcv[:, :], avgb[:, j, :], vh[:, j, :],
                                 start=(j == 0), stop=(j == NT - 1))
                    mm.then_inc(bb.sem(sm), 1)
                    bb.inc(sm)

                @blk.scalar
                def _(s):
                    s.wait_ge(bb.sem(sm), bb.cnt[sm])
                    s.copy(ckb[:, :], pck[:, :])
                    s.copy(cvb[:, :], pcv[:, :]).then_inc(bb.sem(ss), 1)
                    bb.inc(ss)

            for (srcT, slot) in ((QTr, 0), (KTr, 1)):
                with nc.Block() as blk:
                    @blk.scalar
                    def _(s, srcT=srcT):
                        s.activation(sqs[:, :], srcT[:, hi, :], AF.Square
                                     ).then_inc(bb.sem(ss), 1)
                        bb.inc(ss)

                    @blk.tensor
                    def _(t):
                        t.wait_ge(bb.sem(ss), bb.cnt[ss])
                        for nb in range(NB):
                            mm = t.matmul(pnr[0:1, nb * 512:(nb + 1) * 512],
                                     onesc[:, :], sqs[:, nb * 512:(nb + 1) * 512],
                                     start=True, stop=True)
                        mm.then_inc(bb.sem(sm), 1)
                        bb.inc(sm)

                    @blk.vector
                    def _(v, slot=slot):
                        v.wait_ge(bb.sem(sm), bb.cnt[sm])
                        v.reduce_max(nrm[0:1, slot:slot + 1], pnr[0:1, :],
                                     axis=AX.X).then_inc(bb.sem(sv), 1)
                        bb.inc(sv)

            with nc.Block() as blk:
                @blk.scalar
                def _(s):
                    s.activation(sqs[:, 0:128], ckb[:, :], AF.Square
                                 ).then_inc(bb.sem(ss), 1)
                    bb.inc(ss)

                @blk.tensor
                def _(t):
                    t.wait_ge(bb.sem(ss), bb.cnt[ss])
                    t.matmul(pnr[0:1, 0:128], onesc[:, :], sqs[:, 0:128],
                             start=True, stop=True).then_inc(bb.sem(sm), 1)
                    bb.inc(sm)

                @blk.vector
                def _(v):
                    v.wait_ge(bb.sem(sm), bb.cnt[sm])
                    v.reduce_max(nrm[0:1, 2:3], pnr[0:1, 0:128], axis=AX.X)
                    v.drain()
                    v.tensor_tensor(nrm[0:1, 3:4], nrm[0:1, 1:2], nrm[0:1, 2:3],
                                    op=ALU.max)
                    v.drain()
                    v.tensor_tensor(nrm[0:1, 4:5], nrm[0:1, 0:1], nrm[0:1, 3:4],
                                    op=ALU.mult).then_inc(bb.sem(sv), 1)
                    bb.inc(sv)

                @blk.scalar
                def _(s2):
                    s2.wait_ge(bb.sem(sv), bb.cnt[sv])
                    s2.activation(nrm[0:1, 5:6], nrm[0:1, 4:5], AF.Sqrt)
                    s2.drain()
                    s2.activation(nrm[0:1, 6:7], nrm[0:1, 5:6], AF.Copy,
                                  scale=-SCALE).then_inc(bb.sem(ss), 1)
                    bb.inc(ss)

                @blk.vector
                def _(v2):
                    v2.wait_ge(bb.sem(ss), bb.cnt[ss])
                    v2.tensor_copy(c11b[:, :], nrm[0:1, 6:7]
                                   ).then_inc(bb.sem(sv), 1)
                    bb.inc(sv)

            with nc.Block() as blk:
                @blk.tensor
                def _(t):
                    t.matmul(pb1[:, :], onesr[0:1, :], c11b[:, :],
                             start=True, stop=True).then_inc(bb.sem(sm), 1)
                    bb.inc(sm)

                @blk.scalar
                def _(s):
                    s.wait_ge(bb.sem(sm), bb.cnt[sm])
                    s.copy(negC[:, :], pb1[:, :]).then_inc(bb.sem(ss), 1)
                    bb.inc(ss)

        for half in range(2):
            _l0_attn_half(bb, h, hi, half, QTr, KTr, vh, OT, onesc, onesr,
                          ckb, cvb, maskq, negC, pfx)


def _l0_attn_half(bb, h, hi, half, QTr, KTr, vh, OT, onesc, onesr, ckb, cvb,
                  maskq, negC, pfx):
    nc = bb.nc
    sm, ss, sv = "BWm", "BWs", "BWv"
    for _s in (sm, ss, sv):
        bb.sem(_s)
    un = pfx + bb.u()
    with ExitStack() as st:
        e = st.enter_context
        pe_sb = e(nc.sbuf_tensor(un + "pe", [128, 2, 512], BF16))
        pm_sb = e(nc.sbuf_tensor(un + "pm", [128, 2, 512], BF16))
        rrow = e(nc.sbuf_tensor(un + "rrow", [1, 1024], F32))
        rrowb = e(nc.sbuf_tensor(un + "rrowb", [1, 1024], BF16))
        rb_sb = e(nc.sbuf_tensor(un + "rb", [128, 1024], BF16))
        ps = e(nc.psum_tensor(un + "ps", [128, 2, 512], F32))
        pOT = e(nc.psum_tensor(un + "pOT", [128, 1024], F32))
        prow = e(nc.psum_tensor(un + "prow", [1, 1024], F32))
        pbc = e(nc.psum_tensor(un + "pbc", [128, 1024], F32))

        qts = [half * 8 + i for i in range(8)]
        bm, bs, bv = bb.cnt[sm], bb.cnt[ss], bb.cnt[sv]

        def sm_scores(i):
            return bm + (i + 1 if i < 2 else 2 * i - 1)

        def sm_consume(i):
            return bm + (2 * i + 4 if i <= 5 else 9 + i)

        with nc.Block() as blk:
            @blk.tensor
            def _(t):
                for j in range(10):
                    if j < 8:
                        i = j
                        qt = qts[i]
                        if i >= 2:
                            t.wait_ge(bb.sem(ss), bs + i - 1)
                        q_ap = QTr[:, hi, qt * 128:(qt + 1) * 128]
                        t.matmul(ps[:, i % 2, 0:128], ckb[:, :], q_ap,
                                 start=True, stop=False, skip_group_check=True)
                        for d, jt in enumerate((max(qt - 2, 0), max(qt - 1, 0),
                                                qt)):
                            mm = t.matmul(
                                ps[:, i % 2, 128 * (d + 1):128 * (d + 2)],
                                KTr[:, hi, jt * 128:(jt + 1) * 128], q_ap,
                                start=False, stop=(d == 2),
                                skip_group_check=True)
                        mm.then_inc(bb.sem(sm), 1)
                        bb.inc(sm)
                    if j >= 2:
                        i = j - 2
                        qt = qts[i]
                        t.wait_ge(bb.sem(sv), bv + i + 1)
                        col = (i % 4) * 128
                        bank = i // 4
                        first = (i % 4 == 0)
                        reg = slice(bank * 512 + col, bank * 512 + col + 128)
                        t.matmul(prow[0:1, reg], onesc[:, :],
                                 pm_sb[:, i % 2, 0:128],
                                 start=first, stop=False, skip_group_check=True)
                        t.matmul(pOT[:, reg], cvb[:, :], pm_sb[:, i % 2, 0:128],
                                 start=first, stop=False, skip_group_check=True)
                        for d, jt in enumerate((max(qt - 2, 0), max(qt - 1, 0),
                                                qt)):
                            pm_ap = pm_sb[:, i % 2, 128 * (d + 1):128 * (d + 2)]
                            t.matmul(prow[0:1, reg], onesc[:, :], pm_ap,
                                     start=False, stop=False,
                                     skip_group_check=True)
                            mm = t.matmul(pOT[:, reg], vh[:, jt, :], pm_ap,
                                     start=False, stop=False,
                                     skip_group_check=True)
                        mm.then_inc(bb.sem(sm), 1)
                        bb.inc(sm)

            @blk.scalar
            def _(s):
                for i in range(8):
                    s.wait_ge(bb.sem(sm), sm_scores(i))
                    if i >= 2:
                        s.wait_ge(bb.sem(sv), bv + i - 1)
                    s.activation(pe_sb[:, i % 2, :], ps[:, i % 2, :], AF.Exp,
                                 scale=SCALE, bias=negC[:, :]
                                 ).then_inc(bb.sem(ss), 1)
                    bb.inc(ss)

            @blk.vector
            def _(v):
                for i in range(8):
                    v.wait_ge(bb.sem(ss), bs + i + 1)
                    if i >= 2:
                        v.wait_ge(bb.sem(sm), sm_consume(i - 2))
                    v.tensor_tensor(pm_sb[:, i % 2, :], pe_sb[:, i % 2, :],
                                    maskq[:, qts[i], :], op=ALU.mult
                                    ).then_inc(bb.sem(sv), 1)
                    bb.inc(sv)

        bm, bs, bv = bb.cnt[sm], bb.cnt[ss], bb.cnt[sv]
        with nc.Block() as blk:
            @blk.vector
            def _(v):
                v.reciprocal(rrow[:, :], prow[0:1, :])
                v.drain()
                v.tensor_copy(rrowb[:, :], rrow[:, :]).then_inc(bb.sem(sv), 1)
                bb.inc(sv)

            @blk.tensor
            def _(t):
                t.wait_ge(bb.sem(sv), bv + 1)
                t.matmul(pbc[:, 0:512], onesr[0:1, :], rrowb[0:1, 0:512],
                         start=True, stop=True)
                t.matmul(pbc[:, 512:1024], onesr[0:1, :], rrowb[0:1, 512:1024],
                         start=True, stop=True).then_inc(bb.sem(sm), 1)
                bb.inc(sm)

            @blk.scalar
            def _(s):
                s.wait_ge(bb.sem(sm), bm + 1)
                s.copy(rb_sb[:, :], pbc[:, :]).then_inc(bb.sem(ss), 1)
                bb.inc(ss)

            @blk.vector
            def _(v2):
                v2.wait_ge(bb.sem(ss), bs + 1)
                v2.tensor_tensor(OT[0:128, h, half * 1024:(half + 1) * 1024],
                                 pOT[:, :], rb_sb[:, :], op=ALU.mult)


def _prep_l0_attn(inputs, perm):
    wq = np.asarray(inputs["L0_attn_wq"], np.float32)
    wk = np.asarray(inputs["L0_attn_wk"], np.float32)
    wv = np.asarray(inputs["L0_attn_wv"], np.float32)
    wo = np.asarray(inputs["L0_attn_wo"], np.float32)
    return {
        "a0_wqT": np.ascontiguousarray(wq.T[:, perm]),
        "a0_wkT": np.ascontiguousarray(wk.T[:, perm]),
        "a0_wvT": np.ascontiguousarray(wv.T),
        "a0_woT": np.ascontiguousarray(wo.T),
    }


def _declare_common(nc, bb):
    for k, arr in _consts().items():
        dt = BF16 if arr.dtype == _BF else F32
        bb.io["c_" + k] = nc.dram_tensor("c_" + k, list(arr.shape), dt,
                                         kind="ExternalInput").ap()


def build_l0_attn():
    nc = bass.Bass("TRN2", target_bir_lowering=False)
    with ExitStack() as ctx:
        bb = BB(nc, ctx)
        x_in = nc.dram_tensor("x", [S, D], F32, kind="ExternalInput").ap()
        x_out = nc.dram_tensor("y", [S, D], F32, kind="ExternalOutput").ap()
        for nm in ["a0_wqT", "a0_wkT", "a0_wvT", "a0_woT"]:
            bb.io[nm] = nc.dram_tensor(nm, [D, D], F32, kind="ExternalInput").ap()
        _declare_common(nc, bb)
        bb.io["a0_xt"] = nc.dram_tensor("a0_xt", [ND, NT2, 128, 256], BF16,
                                        kind="Internal").ap()
        bb.io["a0_vscr"] = nc.dram_tensor("a0_vscr", [NH, NT, 128, 128], BF16,
                                          kind="Internal").ap()
        emit_l0_attn(bb, x_in, x_out, "a0")
    return nc


def kernel(**inputs):
    raise NotImplementedError("full module not assembled yet")


def emit_scr_out_phase(bb, x_in, x_out, acc_scr, pfx):
    nc = bb.nc
    un = pfx + bb.u()
    with ExitStack() as st:
        e = st.enter_context
        otile = [e(nc.sbuf_tensor(f"{un}ot{i}", [128, ND, 128], F32))
                 for i in range(2)]
        idf = e(nc.sbuf_tensor(un + "idf", [128, 128], F32))
        sot = [bb.slot("SCdot%d" % i) for i in range(2)]
        sid = bb.slot("SCdid")

        with nc.Block() as blk:
            @blk.gpsimd
            def _(g):
                sid.put(g, idf[:, :], bb.io["c_identf"][:, :])
                sid.wait(g)

        sm = "OTm"
        bb.sem(sm)
        bm0 = bb.cnt[sm]

        def pre_g(g, m):
            if m >= 2:
                g.wait_ge(bb.sem(sm), bm0 + m - 1)
            for dt in range(ND):
                sot[m % 2].put(g, otile[m % 2][:, dt, :],
                               acc_scr[dt, :, m * 128:(m + 1) * 128])

        def mm_waits(t, m):
            sot[m % 2].wait_rel(t, 16 * ND * (m // 2 + 1))

        def emit_mm(t, m, psum):
            for dt in range(ND):
                tp = t.transpose(psum[:, dt * 128:(dt + 1) * 128],
                                 otile[m % 2][:, dt, :], idf[:, :])
            return tp

        emit_out_ln_residual(bb, x_in, x_out, pfx, emit_mm, mm_waits,
                             pre_g=pre_g)


def emit_l1_mix(bb, x_in, x_out, pfx="m1"):
    nc = bb.nc
    io = bb.io
    xt = io[pfx + "_xt"]
    emit_preln_xt(bb, x_in, xt, pfx + "A")

    with ExitStack() as st0:
        e0 = st0.enter_context
        vsb = e0(nc.sbuf_tensor(pfx + "vsb", [128, S], BF16))
        onesr = e0(nc.sbuf_tensor(pfx + "onesr", [1, 128], BF16))
        onesS = e0(nc.sbuf_tensor(pfx + "onesS", [1, S], BF16))
        bvb = e0(nc.sbuf_tensor(pfx + "bvb", [1, S], BF16))
        bob = e0(nc.sbuf_tensor(pfx + "bob", [1, S], BF16))

        with ExitStack() as st:
            e = st.enter_context
            wvb = e(nc.sbuf_tensor(pfx + "wvb", [128, ND, D], BF16))
            xrow = [e(nc.sbuf_tensor(f"{pfx}xr{i}", [128, S], BF16))
                    for i in range(2)]
            pvs = e(nc.psum_tensor(pfx + "pvs", [128, S], F32))
            swv = bb.slot("M1dwv")
            sxr = [bb.slot("M1dxr%d" % i) for i in range(2)]
            sm, sv = "BVm", "BVv"
            bb.sem(sm), bb.sem(sv)
            bm, bv_ = bb.cnt[sm], bb.cnt[sv]

            with nc.Block() as blk:
                @blk.gpsimd
                def _(g):
                    swv.put(g, onesr[:, :], io["c_ones_row"][:, :])
                    swv.put(g, onesS[:, :], io["c_ones_rowS"][:, :])
                    swv.put(g, bvb[:, :], io[pfx + "_bv"][:, :])
                    swv.put(g, bob[:, :], io[pfx + "_bo"][:, :])
                    for k in range(ND):
                        swv.put(g, wvb[:, k, :],
                                io[pfx + "_wvT"][k * 128:(k + 1) * 128, :])
                    for k in range(ND):
                        if k >= 2:
                            g.wait_ge(bb.sem(sm), bm + k - 1)
                        for t2 in range(NT2):
                            sxr[k % 2].put(g, xrow[k % 2][:, t2 * 256:(t2 + 1) * 256],
                                           xt[k, t2])

                @blk.tensor
                def _(t):
                    swv.wait(t)
                    for k in range(ND):
                        sxr[k % 2].wait_rel(t, 16 * NT2 * (k // 2 + 1))
                        for kh in range(NH):
                            for nb in range(NB):
                                mm = t.matmul(
                                    pvs[:, nb * 512:(nb + 1) * 512],
                                    wvb[:, k, kh * 128:(kh + 1) * 128],
                                    xrow[k % 2][:, nb * 512:(nb + 1) * 512],
                                    start=(k == 0 and kh == 0), stop=False,
                                    skip_group_check=True)
                        mm.then_inc(bb.sem(sm), 1)
                        bb.inc(sm)
                    for kh in range(NH):
                        for nb in range(NB):
                            mm = t.matmul(pvs[:, nb * 512:(nb + 1) * 512],
                                          bvb[0:1, kh * 128:(kh + 1) * 128],
                                          onesS[0:1, nb * 512:(nb + 1) * 512],
                                          start=False,
                                          stop=(kh == NH - 1 and nb == NB - 1),
                                          skip_group_check=True)
                    mm.then_inc(bb.sem(sm), 1)
                    bb.inc(sm)

                @blk.scalar
                def _(s):
                    s.wait_ge(bb.sem(sm), bm + ND + 1)
                    s.copy(vsb[:, :], pvs[:, :]).then_inc(bb.sem(sv), 1)
                    bb.inc(sv)

        with ExitStack() as st:
            wob = st.enter_context(nc.sbuf_tensor(pfx + "wob", [128, ND, D],
                                                  BF16))
            swo = bb.slot("M1dwo")
            with nc.Block() as blk:
                @blk.gpsimd
                def _(g):
                    for k in range(ND):
                        swo.put(g, wob[:, k, :],
                                io[pfx + "_woT"][k * 128:(k + 1) * 128, :])
                    swo.wait(g)

            def emit_mm(t, m, psum):
                for qh in range(NH):
                    for nb in range(NB):
                        t.matmul(psum[:, nb * 512:(nb + 1) * 512],
                                 vsb[:, m * 128:(m + 1) * 128],
                                 wob[:, qh, nb * 512:(nb + 1) * 512],
                                 start=(qh == 0), stop=False,
                                 skip_group_check=True)
                for nb in range(NB):
                    mm = t.matmul(psum[:, nb * 512:(nb + 1) * 512],
                                  onesr[0:1, 0:128],
                                  bob[0:1, nb * 512:(nb + 1) * 512],
                                  start=False, stop=(nb == NB - 1),
                                  skip_group_check=True)
                return mm

            emit_out_ln_residual(bb, x_in, x_out, pfx + "E", emit_mm)


def emit_l1_ffn(bb, x_in, x_out, pfx="f1"):
    nc = bb.nc
    io = bb.io
    xt = io[pfx + "_xt"]
    acc = io[pfx + "_acc"]
    NHM = HID1 // 128
    TC = 512
    NCHUNK = S // TC

    emit_preln_xt(bb, x_in, xt, pfx + "A")

    with ExitStack() as st0:
        e0 = st0.enter_context
        h128b = e0(nc.sbuf_tensor(pfx + "h128", [128, 128], BF16))
        hA = e0(nc.sbuf_tensor(pfx + "hA", [128, NHM, TC], BF16))
        hB = e0(nc.sbuf_tensor(pfx + "hB", [128, NHM, TC], BF16))
        xtc = e0(nc.sbuf_tensor(pfx + "xtc", [128, ND, TC], BF16))
        sic = bb.slot("F1dic")
        with nc.Block() as blk:
            @blk.gpsimd
            def _(g):
                sic.put(g, h128b[:, :], io["c_h128"][:, :])
                sic.wait(g)

        for chunk in range(NCHUNK):
            _l1_ffn_chunk(bb, chunk, xt, acc, h128b, hA, hB, xtc, TC, NHM, pfx)

    emit_scr_out_phase(bb, x_in, x_out, acc, pfx + "E")


def _l1_ffn_chunk(bb, chunk, xt, acc, h128b, hA, hB, xtc, TC, NHM, pfx):
    nc = bb.nc
    io = bb.io
    un = pfx + bb.u()
    NTC = TC // 512

    with ExitStack() as st:
        sxc = bb.slot("F1dxc")
        with nc.Block() as blk:
            @blk.gpsimd
            def _(g):
                for k in range(ND):
                    for t2 in range(TC // 256):
                        sxc.put(g, xtc[:, k, t2 * 256:(t2 + 1) * 256],
                                xt[k, chunk * (TC // 256) + t2])
                sxc.wait(g)

    with ExitStack() as st:
        e = st.enter_context
        w1c = [e(nc.sbuf_tensor(f"{un}w1c{i}", [128, ND, 128], BF16))
               for i in range(2)]
        w3c = [e(nc.sbuf_tensor(f"{un}w3c{i}", [128, ND, 128], BF16))
               for i in range(2)]
        s1b = [e(nc.sbuf_tensor(f"{un}s1b{i}", [128, TC], BF16))
               for i in range(2)]
        pu = e(nc.psum_tensor(un + "pu", [128, 2, 2, TC], F32))
        sw1 = [bb.slot("F1dw1%d" % i) for i in range(2)]
        sw3 = [bb.slot("F1dw3%d" % i) for i in range(2)]
        sm, ss, sv = "BUm", "BUs", "BUv"
        for _x in (sm, ss, sv):
            bb.sem(_x)
        bm, bs, bv = bb.cnt[sm], bb.cnt[ss], bb.cnt[sv]

        with nc.Block() as blk:
            @blk.gpsimd
            def _(g):
                for hm in range(NHM):
                    if hm >= 2:
                        g.wait_ge(bb.sem(sm), bm + 2 * (hm - 1))
                    i = hm % 2
                    for k in range(ND):
                        sw1[i].put(g, w1c[i][:, k, :],
                                   io[pfx + "_w1T"][k * 128:(k + 1) * 128,
                                                    hm * 128:(hm + 1) * 128])
                    for k in range(ND):
                        sw3[i].put(g, w3c[i][:, k, :],
                                   io[pfx + "_w3T"][k * 128:(k + 1) * 128,
                                                    hm * 128:(hm + 1) * 128])

            @blk.tensor
            def _(t):
                for hm in range(NHM):
                    i = hm % 2
                    sw1[i].wait_rel(t, 16 * ND * (hm // 2 + 1))
                    sw3[i].wait_rel(t, 16 * ND * (hm // 2 + 1))
                    if hm >= 2:
                        t.wait_ge(bb.sem(sv), bv + hm - 1)
                    for k in range(ND):
                        mm = t.matmul(pu[:, i, 0, :], w1c[i][:, k, :],
                                      xtc[:, k, :],
                                      start=(k == 0), stop=(k == ND - 1))
                    mm.then_inc(bb.sem(sm), 1)
                    bb.inc(sm)
                    for k in range(ND):
                        mm = t.matmul(pu[:, i, 1, :], w3c[i][:, k, :],
                                      xtc[:, k, :],
                                      start=(k == 0), stop=(k == ND - 1))
                    mm.then_inc(bb.sem(sm), 1)
                    bb.inc(sm)

            @blk.scalar
            def _(s):
                for hm in range(NHM):
                    i = hm % 2
                    s.wait_ge(bb.sem(sm), bm + 2 * hm + 1)
                    s.activation(s1b[i][:, :], pu[:, i, 0, :], AF.Silu
                                 ).then_inc(bb.sem(ss), 1)
                    bb.inc(ss)

            @blk.vector
            def _(v):
                for hm in range(NHM):
                    i = hm % 2
                    v.wait_ge(bb.sem(ss), bs + hm + 1)
                    v.wait_ge(bb.sem(sm), bm + 2 * hm + 2)
                    v.tensor_tensor(hA[:, hm, :], s1b[i][:, :], pu[:, i, 1, :],
                                    op=ALU.mult).then_inc(bb.sem(sv), 1)
                    bb.inc(sv)

    with ExitStack() as st:
        pf = st.enter_context(nc.psum_tensor(un + "pf", [128, 4, TC], F32))
        sm, sv = "BFm", "BFv"
        bb.sem(sm), bb.sem(sv)
        bm, bv = bb.cnt[sm], bb.cnt[sv]
        with nc.Block() as blk:
            @blk.tensor
            def _(t):
                for pm in range(NHM):
                    if pm >= 4:
                        t.wait_ge(bb.sem(sv), bv + pm - 3)
                    t.matmul(pf[:, pm % 4, :], h128b[:, :], hA[:, pm, :],
                             start=True, stop=True).then_inc(bb.sem(sm), 1)
                    bb.inc(sm)

            @blk.vector
            def _(v):
                for pm in range(NHM):
                    v.wait_ge(bb.sem(sm), bm + pm + 1)
                    v.tensor_copy(hB[:, pm, :], pf[:, pm % 4, :]
                                  ).then_inc(bb.sem(sv), 1)
                    bb.inc(sv)

        cur, nxt = hB, hA
        with nc.Block() as blk:
            @blk.vector
            def _(v):
                c, n = cur, nxt
                for s_ in range(6):
                    Dd = 1 << s_
                    for a in range(NHM):
                        if a & Dd:
                            continue
                        b = a + Dd
                        v.tensor_tensor(n[:, a, :], c[:, a, :], c[:, b, :],
                                        op=ALU.add)
                        v.tensor_tensor(n[:, b, :], c[:, a, :], c[:, b, :],
                                        op=ALU.subtract)
                    v.drain()
                    c, n = n, c

        hfin = hB if 6 % 2 == 0 else hA

    with ExitStack() as st:
        e = st.enter_context
        w2c = [e(nc.sbuf_tensor(f"{un}w2c{i}", [128, NHM, 128], BF16))
               for i in range(2)]
        orow = [e(nc.sbuf_tensor(f"{un}or{i}", [128, TC], F32))
                for i in range(2)]
        pd = e(nc.psum_tensor(un + "pd", [128, 2, TC], F32))
        sw2 = [bb.slot("F1dw2%d" % i) for i in range(2)]
        sor = [bb.slot("F1dor%d" % i) for i in range(2)]
        sm, ss = "BDm", "BDs"
        bb.sem(sm), bb.sem(ss)
        bm, bs = bb.cnt[sm], bb.cnt[ss]

        with nc.Block() as blk:
            @blk.gpsimd
            def _(g):
                for md in range(ND + 2):
                    if md < ND:
                        if md >= 2:
                            g.wait_ge(bb.sem(sm), bm + md - 1)
                        i = md % 2
                        for k in range(NHM):
                            sw2[i].put(g, w2c[i][:, k, :],
                                       io[pfx + "_w2T"][k * 128:(k + 1) * 128,
                                                        md * 128:(md + 1) * 128])
                    if md >= 2:
                        md_ = md - 2
                        g.wait_ge(bb.sem(ss), bs + md_ + 1)
                        i = md_ % 2
                        sor[i].put(g, acc[md_, :, chunk * TC:(chunk + 1) * TC],
                                   orow[i][:, :])
                sor[0].wait(g)
                sor[1].wait(g)

            @blk.tensor
            def _(t):
                for md in range(ND):
                    i = md % 2
                    sw2[i].wait_rel(t, 16 * NHM * (md // 2 + 1))
                    if md >= 2:
                        t.wait_ge(bb.sem(ss), bs + md - 1)
                    for k in range(NHM):
                        mm = t.matmul(pd[:, i, :], w2c[i][:, k, :],
                                      hB[:, k, :],
                                      start=(k == 0), stop=(k == NHM - 1))
                    mm.then_inc(bb.sem(sm), 1)
                    bb.inc(sm)

            @blk.scalar
            def _(s):
                for md in range(ND):
                    i = md % 2
                    s.wait_ge(bb.sem(sm), bm + md + 1)
                    if md >= 2:
                        sor[i].wait_rel(s, 16 * (md // 2))
                    s.copy(orow[i][:, :], pd[:, i, :]).then_inc(bb.sem(ss), 1)
                    bb.inc(ss)


def emit_l0_moe(bb, x_in, x_out, pfx="f0"):
    nc = bb.nc
    io = bb.io
    xt = io[pfx + "_xt"]
    acc = io[pfx + "_acc"]
    NHT = HID0 // 128

    emit_preln_xt(bb, x_in, xt, pfx + "A")

    with ExitStack() as st0:
        e0 = st0.enter_context
        gT = e0(nc.sbuf_tensor(pfx + "gT", [8, S], F32))
        onesf = e0(nc.sbuf_tensor(pfx + "onesf", [1, 128], F32))
        idf = e0(nc.sbuf_tensor(pfx + "idf", [128, 128], F32))
        id8 = e0(nc.sbuf_tensor(pfx + "id8", [8, 8], F32))
        sor = [bb.slot("F0dacc%d" % i) for i in range(2)]
        scn = bb.slot("F0dcn")
        with nc.Block() as blk:
            @blk.gpsimd
            def _(g):
                scn.put(g, onesf[:, :], io["c_ones_row_f"][:, :])
                scn.put(g, idf[:, :], io["c_identf"][:, :])
                scn.put(g, id8[:, :], io["c_ident8f"][:, :])
                scn.wait(g)

        _moe_gate(bb, xt, gT, idf, pfx)

        for ex in range(NE + 1):
            _moe_expert(bb, ex, xt, gT, onesf, id8, acc, sor, pfx, NHT)

    emit_scr_out_phase(bb, x_in, x_out, acc, pfx + "E")


def _moe_gate(bb, xt, gT, idf, pfx):
    nc = bb.nc
    io = bb.io
    un = pfx + bb.u()
    with ExitStack() as st:
        e = st.enter_context
        gwb = e(nc.sbuf_tensor(un + "gwb", [128, ND, 8], BF16))
        ebias = e(nc.sbuf_tensor(un + "ebias", [128, 8], F32))
        xg = [e(nc.sbuf_tensor(f"{un}xg{i}", [128, ND, 128], BF16))
              for i in range(2)]
        ge = e(nc.sbuf_tensor(un + "ge", [128, 2, 8], F32))
        gst = e(nc.sbuf_tensor(un + "gst", [128, 2, 8], F32))
        gsc = e(nc.sbuf_tensor(un + "gsc", [128, 2, 8], F32))
        gpb = e(nc.sbuf_tensor(un + "gpb", [128, 2, 8], F32))
        gw2 = e(nc.sbuf_tensor(un + "gw2", [128, 2, 8], F32))
        gall = e(nc.sbuf_tensor(un + "gall", [128, NT, 8], F32))
        pg = e(nc.psum_tensor(un + "pg", [128, 2, 8], F32))
        pgt = e(nc.psum_tensor(un + "pgt", [8, 128], F32))
        sgw = bb.slot("F0dgw")
        sxg = [bb.slot("F0dxg%d" % i) for i in range(2)]
        sm, ss, sv = "BGm", "BGs", pfx + "Gv"
        for _x in (sm, ss, sv):
            bb.sem(_x)
        bm, bs, bv = bb.cnt[sm], bb.cnt[ss], bb.cnt[sv]

        with nc.Block() as blk:
            @blk.gpsimd
            def _(g):
                sgw.put(g, ebias[:, :], io[pfx + "_ebias"][:, :])
                for k in range(ND):
                    sgw.put(g, gwb[:, k, :],
                            io[pfx + "_gate"][k * 128:(k + 1) * 128, :])
                for m in range(NT):
                    if m >= 2:
                        g.wait_ge(bb.sem(sm), bm + m - 1)
                    for k in range(ND):
                        sxg[m % 2].put(
                            g, xg[m % 2][:, k, :],
                            xt[k, m // 2][:, (m % 2) * 128:(m % 2) * 128 + 128])

            @blk.tensor
            def _(t):
                sgw.wait(t)
                for m in range(NT):
                    sxg[m % 2].wait_rel(t, 16 * ND * (m // 2 + 1))
                    if m >= 2:
                        t.wait_ge(bb.sem(ss), bs + m - 1)
                    for k in range(ND):
                        mm = t.matmul(pg[:, m % 2, :], xg[m % 2][:, k, :],
                                      gwb[:, k, :],
                                      start=(k == 0), stop=(k == ND - 1))
                    mm.then_inc(bb.sem(sm), 1)
                    bb.inc(sm)

            @blk.scalar
            def _(s):
                for m in range(NT):
                    i = m % 2
                    s.wait_ge(bb.sem(sm), bm + m + 1)
                    if m >= 2:
                        s.wait_ge(bb.sem(sv), bv + m - 1)
                    s.activation(ge[:, i, :], pg[:, i, :], AF.Exp,
                                 accum_out=gst[:, i, 0:1]).then_inc(bb.sem(ss), 1)
                    bb.inc(ss)

            @blk.vector
            def _(v):
                for m in range(NT):
                    i = m % 2
                    v.wait_ge(bb.sem(ss), bs + m + 1)
                    v.reciprocal(gst[:, i, 1:2], gst[:, i, 0:1])
                    v.drain()
                    v.tensor_scalar(gsc[:, i, :], ge[:, i, :], gst[:, i, 1:2],
                                    None, op0=ALU.mult)
                    v.drain()
                    v.tensor_tensor(gpb[:, i, :], gsc[:, i, :], ebias[:, :],
                                    op=ALU.add)
                    v.drain()
                    v.reduce_max(gst[:, i, 2:3], gpb[:, i, :], axis=AX.X)
                    v.drain()
                    v.tensor_scalar(gw2[:, i, :], gpb[:, i, :], gst[:, i, 2:3],
                                    None, op0=ALU.is_equal)
                    v.drain()
                    v.scalar_tensor_tensor(gw2[:, i, :], gw2[:, i, :], -1e30,
                                           gpb[:, i, :], op0=ALU.mult,
                                           op1=ALU.add)
                    v.drain()
                    v.reduce_max(gst[:, i, 3:4], gw2[:, i, :], axis=AX.X)
                    v.drain()
                    v.tensor_scalar(gw2[:, i, :], gpb[:, i, :], gst[:, i, 3:4],
                                    None, op0=ALU.is_ge)
                    v.drain()
                    v.tensor_tensor(gall[:, m, :], gsc[:, i, :], gw2[:, i, :],
                                    op=ALU.mult).then_inc(bb.sem(sv), 1)
                    bb.inc(sv)

        bm, bv = bb.cnt[sm], bb.cnt[sv]
        with nc.Block() as blk:
            @blk.tensor
            def _(t):
                for m in range(NT):
                    if m >= 1:
                        t.wait_ge(bb.sem(sv), bv + m)
                    t.transpose(pgt[:, :], gall[:, m, :], idf[:, :]
                                ).then_inc(bb.sem(sm), 1)
                    bb.inc(sm)

            @blk.vector
            def _(v):
                for m in range(NT):
                    v.wait_ge(bb.sem(sm), bm + m + 1)
                    v.tensor_copy(gT[0:8, m * 128:(m + 1) * 128], pgt[0:8, :]
                                  ).then_inc(bb.sem(sv), 1)
                    bb.inc(sv)


def _moe_expert(bb, ex, xt, gT, onesf, id8, acc, sor, pfx, NHT):
    nc = bb.nc
    io = bb.io
    un = pfx + bb.u()
    shared = (ex == NE)

    def w1_ap(hm):
        if shared:
            return io[pfx + "_sw1"][:, hm * 128:(hm + 1) * 128]
        return io[pfx + "_w1"][ex, :, hm * 128:(hm + 1) * 128]

    def w3_ap(hm):
        if shared:
            return io[pfx + "_sw3"][:, hm * 128:(hm + 1) * 128]
        return io[pfx + "_w3"][ex, :, hm * 128:(hm + 1) * 128]

    def w2_ap(md):
        if shared:
            return io[pfx + "_sw2"][:, md * 128:(md + 1) * 128]
        return io[pfx + "_w2"][ex, :, md * 128:(md + 1) * 128]

    with ExitStack() as st:
        e = st.enter_context
        xeT = e(nc.sbuf_tensor(un + "xeT", [128, ND, S], BF16))
        heT = e(nc.sbuf_tensor(un + "heT", [128, NHT, S], BF16))

        with ExitStack() as stg:
            eg = stg.enter_context
            gb = eg(nc.sbuf_tensor(un + "gb", [128, S], F32))
            grow = eg(nc.sbuf_tensor(un + "grow", [1, S], F32))
            xrow = [eg(nc.sbuf_tensor(f"{un}xr{i}", [128, S], BF16))
                    for i in range(2)]
            pgb = eg(nc.psum_tensor(un + "pgb", [128, S], F32))
            sxr = [bb.slot("F0dxr%d" % i) for i in range(2)]
            sm, ss, sv = "BXm", "BXs", "BXv"
            for _x in (sm, ss, sv):
                bb.sem(_x)
            bm, bs, bv = bb.cnt[sm], bb.cnt[ss], bb.cnt[sv]

            if not shared:
                with nc.Block() as blk:
                    @blk.tensor
                    def _(t):
                        for nb in range(NB):
                            mm = t.matmul(pgb[0:1, nb * 512:(nb + 1) * 512],
                                          id8[:, ex:ex + 1],
                                          gT[0:8, nb * 512:(nb + 1) * 512],
                                          start=True, stop=True,
                                          skip_group_check=True)
                        mm.then_inc(bb.sem(sm), 1)
                        bb.inc(sm)

                    @blk.scalar
                    def _(s):
                        s.wait_ge(bb.sem(sm), bb.cnt[sm])
                        s.copy(grow[:, :], pgb[0:1, :]).then_inc(bb.sem(ss), 1)
                        bb.inc(ss)

                with nc.Block() as blk:
                    @blk.tensor
                    def _(t):
                        for nb in range(NB):
                            mm = t.matmul(pgb[:, nb * 512:(nb + 1) * 512],
                                          onesf[0:1, :],
                                          grow[0:1, nb * 512:(nb + 1) * 512],
                                          start=True, stop=True,
                                          skip_group_check=True)
                        mm.then_inc(bb.sem(sm), 1)
                        bb.inc(sm)

                    @blk.scalar
                    def _(s):
                        s.wait_ge(bb.sem(sm), bb.cnt[sm])
                        s.copy(gb[:, :], pgb[:, :]).then_inc(bb.sem(ss), 1)
                        bb.inc(ss)

                with nc.Block() as blk:
                    @blk.gpsimd
                    def _(g):
                        for k in range(ND):
                            if k >= 2:
                                g.wait_ge(bb.sem(sv), bv + k - 1)
                            for t2 in range(NT2):
                                sxr[k % 2].put(
                                    g, xrow[k % 2][:, t2 * 256:(t2 + 1) * 256],
                                    xt[k, t2])

                    @blk.vector
                    def _(v):
                        for k in range(ND):
                            sxr[k % 2].wait_rel(v, 16 * NT2 * (k // 2 + 1))
                            v.tensor_tensor(xeT[:, k, :], xrow[k % 2][:, :],
                                            gb[:, :], op=ALU.mult
                                            ).then_inc(bb.sem(sv), 1)
                            bb.inc(sv)
            else:
                with nc.Block() as blk:
                    @blk.gpsimd
                    def _(g):
                        for k in range(ND):
                            for t2 in range(NT2):
                                sxr[0].put(g, xeT[:, k, t2 * 256:(t2 + 1) * 256],
                                           xt[k, t2])
                        sxr[0].wait(g)

        with ExitStack() as stu:
            eu = stu.enter_context
            w1c = [eu(nc.sbuf_tensor(f"{un}w1c{i}", [128, ND, 128], BF16))
                   for i in range(2)]
            w3c = [eu(nc.sbuf_tensor(f"{un}w3c{i}", [128, ND, 128], BF16))
                   for i in range(2)]
            s1b = [eu(nc.sbuf_tensor(f"{un}s1b{i}", [128, S], BF16))
                   for i in range(2)]
            pu = eu(nc.psum_tensor(un + "pu", [128, 2, S], F32))
            sw1 = [bb.slot("F0dw1%d" % i) for i in range(2)]
            sw3 = [bb.slot("F0dw3%d" % i) for i in range(2)]
            sm, ss, sv = "BUm", "BUs", "BUv"
            for _x in (sm, ss, sv):
                bb.sem(_x)
            bm, bs, bv = bb.cnt[sm], bb.cnt[ss], bb.cnt[sv]

            with nc.Block() as blk:
                @blk.gpsimd
                def _(g):
                    for hm in range(NHT):
                        if hm >= 2:
                            g.wait_ge(bb.sem(sm), bm + 2 * (hm - 1))
                        i = hm % 2
                        w1a, w3a = w1_ap(hm), w3_ap(hm)
                        for k in range(ND):
                            sw1[i].put(g, w1c[i][:, k, :],
                                       w1a[k * 128:(k + 1) * 128, :])
                        for k in range(ND):
                            sw3[i].put(g, w3c[i][:, k, :],
                                       w3a[k * 128:(k + 1) * 128, :])

                @blk.tensor
                def _(t):
                    for hm in range(NHT):
                        i = hm % 2
                        sw1[i].wait_rel(t, 16 * ND * (hm // 2 + 1))
                        sw3[i].wait_rel(t, 16 * ND * (hm // 2 + 1))
                        if hm >= 1:
                            t.wait_ge(bb.sem(sv), bv + hm)
                        for k in range(ND):
                            for nb in range(NB):
                                mm = t.matmul(
                                    pu[:, 0, nb * 512:(nb + 1) * 512],
                                    w1c[i][:, k, :],
                                    xeT[:, k, nb * 512:(nb + 1) * 512],
                                    start=(k == 0), stop=(k == ND - 1),
                                    skip_group_check=True)
                        mm.then_inc(bb.sem(sm), 1)
                        bb.inc(sm)
                        for k in range(ND):
                            for nb in range(NB):
                                mm = t.matmul(
                                    pu[:, 1, nb * 512:(nb + 1) * 512],
                                    w3c[i][:, k, :],
                                    xeT[:, k, nb * 512:(nb + 1) * 512],
                                    start=(k == 0), stop=(k == ND - 1),
                                    skip_group_check=True)
                        mm.then_inc(bb.sem(sm), 1)
                        bb.inc(sm)

                @blk.scalar
                def _(s):
                    for hm in range(NHT):
                        i = hm % 2
                        s.wait_ge(bb.sem(sm), bm + 2 * hm + 1)
                        if hm >= 2:
                            s.wait_ge(bb.sem(sv), bv + hm - 1)
                        s.activation(s1b[i][:, :], pu[:, 0, :], AF.Silu
                                     ).then_inc(bb.sem(ss), 1)
                        bb.inc(ss)

                @blk.vector
                def _(v):
                    for hm in range(NHT):
                        i = hm % 2
                        v.wait_ge(bb.sem(ss), bs + hm + 1)
                        v.wait_ge(bb.sem(sm), bm + 2 * hm + 2)
                        v.tensor_tensor(heT[:, hm, :], s1b[i][:, :],
                                        pu[:, 1, :], op=ALU.mult
                                        ).then_inc(bb.sem(sv), 1)
                        bb.inc(sv)

        with ExitStack() as std:
            ed = std.enter_context
            w2c = [ed(nc.sbuf_tensor(f"{un}w2c{i}", [128, NHT, 128], BF16))
                   for i in range(2)]
            orow = [ed(nc.sbuf_tensor(f"{un}or{i}", [128, S], F32))
                    for i in range(2)]
            pd = ed(nc.psum_tensor(un + "pd", [128, 2, S], F32))
            sw2 = [bb.slot("F0dw2%d" % i) for i in range(2)]
            sm, ss = "BDm", "BDs"
            bb.sem(sm), bb.sem(ss)
            bm, bs = bb.cnt[sm], bb.cnt[ss]
            acc_base = [sor[0].n, sor[1].n]

            with nc.Block() as blk:
                @blk.gpsimd
                def _(g):
                    sor[0].wait(g, acc_base[0])
                    sor[1].wait(g, acc_base[1])
                    for md in range(ND + 2):
                        if md < ND:
                            if md >= 2:
                                g.wait_ge(bb.sem(sm), bm + md - 1)
                            i = md % 2
                            w2a = w2_ap(md)
                            for k in range(NHT):
                                sw2[i].put(
                                    g, w2c[i][:, k, :],
                                    w2a[k * 128:(k + 1) * 128, :])
                        if md >= 2:
                            md_ = md - 2
                            g.wait_ge(bb.sem(ss), bs + md_ + 1)
                            i = md_ % 2
                            if ex == 0:
                                sor[i].put(g, acc[md_], orow[i][:, :])
                            else:
                                sor[i].put(g, acc[md_], orow[i][:, :],
                                           accum_op=ALU.add)
                    sor[0].wait(g)
                    sor[1].wait(g)

                @blk.tensor
                def _(t):
                    for md in range(ND):
                        i = md % 2
                        sw2[i].wait_rel(t, 16 * NHT * (md // 2 + 1))
                        if md >= 2:
                            t.wait_ge(bb.sem(ss), bs + md - 1)
                        for k in range(NHT):
                            for nb in range(NB):
                                mm = t.matmul(
                                    pd[:, i, nb * 512:(nb + 1) * 512],
                                    w2c[i][:, k, :],
                                    heT[:, k, nb * 512:(nb + 1) * 512],
                                    start=(k == 0), stop=(k == NHT - 1),
                                    skip_group_check=True)
                        mm.then_inc(bb.sem(sm), 1)
                        bb.inc(sm)

                @blk.scalar
                def _(s):
                    for md in range(ND):
                        i = md % 2
                        s.wait_ge(bb.sem(sm), bm + md + 1)
                        if md >= 2:
                            sor[i].wait(s, acc_base[i] + 16 * (md // 2))
                        s.copy(orow[i][:, :], pd[:, i, :]).then_inc(bb.sem(ss), 1)
                        bb.inc(ss)


def _prep_all(inputs):
    perm = _rope_perm()
    w = _prep_l0_attn(inputs, perm)
    f32 = lambda k: np.asarray(inputs[k], np.float32)
    w["f0_gate"] = np.ascontiguousarray(f32("L0_ffn_gate_w"))
    w["f0_w1"] = np.ascontiguousarray(f32("L0_ffn_w1"))
    w["f0_w2"] = np.ascontiguousarray(f32("L0_ffn_w2"))
    w["f0_w3"] = np.ascontiguousarray(f32("L0_ffn_w3"))
    w["f0_sw1"] = np.ascontiguousarray(f32("L0_ffn_sw1"))
    w["f0_sw2"] = np.ascontiguousarray(f32("L0_ffn_sw2"))
    w["f0_sw3"] = np.ascontiguousarray(f32("L0_ffn_sw3"))
    w["f0_ebias"] = np.ascontiguousarray(
        np.broadcast_to(f32("L0_ffn_expert_bias")[None, :], (128, NE)))
    w["m1_wvT"] = np.ascontiguousarray(f32("L1_attn_Wv").T)
    w["m1_woT"] = np.ascontiguousarray(f32("L1_attn_Wo").T)
    w["m1_bv"] = np.ascontiguousarray(f32("L1_attn_bv")[None, :])
    w["m1_bo"] = np.ascontiguousarray(f32("L1_attn_bo")[None, :])
    w["f1_w1T"] = np.ascontiguousarray(f32("L1_ffn_W1").T)
    w["f1_w3T"] = np.ascontiguousarray(f32("L1_ffn_W3").T)
    w["f1_w2T"] = np.ascontiguousarray(f32("L1_ffn_W2").T)
    return w


_SHAPES = {
    "a0_wqT": [D, D], "a0_wkT": [D, D], "a0_wvT": [D, D], "a0_woT": [D, D],
    "f0_gate": [D, NE], "f0_w1": [NE, D, HID0], "f0_w2": [NE, HID0, D],
    "f0_w3": [NE, D, HID0], "f0_sw1": [D, HID0], "f0_sw2": [HID0, D],
    "f0_sw3": [D, HID0], "f0_ebias": [128, NE],
    "m1_wvT": [D, D], "m1_woT": [D, D], "m1_bv": [1, D], "m1_bo": [1, D],
    "f1_w1T": [D, HID1], "f1_w3T": [D, HID1], "f1_w2T": [HID1, D],
}

_BLOCK_WEIGHTS = {
    "a0": ["a0_wqT", "a0_wkT", "a0_wvT", "a0_woT"],
    "f0": ["f0_gate", "f0_w1", "f0_w2", "f0_w3", "f0_sw1", "f0_sw2", "f0_sw3",
           "f0_ebias"],
    "m1": ["m1_wvT", "m1_woT", "m1_bv", "m1_bo"],
    "f1": ["f1_w1T", "f1_w3T", "f1_w2T"],
}

_EMITTERS = {"a0": emit_l0_attn, "f0": emit_l0_moe, "m1": emit_l1_mix,
             "f1": emit_l1_ffn}


def _declare_block_io(nc, bb, pfx):
    for nm in _BLOCK_WEIGHTS[pfx]:
        bb.io[nm] = nc.dram_tensor(nm, _SHAPES[nm], F32,
                                   kind="ExternalInput").ap()
    bb.io[pfx + "_xt"] = nc.dram_tensor(pfx + "_xt", [ND, NT2, 128, 256], BF16,
                                        kind="Internal").ap()
    if pfx == "a0":
        bb.io["a0_vscr"] = nc.dram_tensor("a0_vscr", [NH, NT, 128, 128], BF16,
                                          kind="Internal").ap()
    if pfx in ("f0", "f1"):
        bb.io[pfx + "_acc"] = nc.dram_tensor(pfx + "_acc", [ND, 128, S], F32,
                                             kind="Internal").ap()


def build_block(pfx):
    nc = bass.Bass("TRN2", target_bir_lowering=False)
    with ExitStack() as ctx:
        bb = BB(nc, ctx)
        x_in = nc.dram_tensor("x", [S, D], F32, kind="ExternalInput").ap()
        x_out = nc.dram_tensor("y", [S, D], F32, kind="ExternalOutput").ap()
        _declare_common(nc, bb)
        _declare_block_io(nc, bb, pfx)
        _EMITTERS[pfx](bb, x_in, x_out, pfx)
    return nc


_NC_CACHE = {}


def kernel_unfused(**inputs):
    x = np.asarray(inputs["L0_attn_x"], np.float32)
    ncores = x.shape[0]
    wmap = _prep_all(inputs)
    cmap = {"c_" + k: v for k, v in _consts().items()}

    cur = [np.ascontiguousarray(x[c]) for c in range(ncores)]
    for pfx in ["a0", "f0", "m1", "f1"]:
        if pfx not in _NC_CACHE:
            _NC_CACHE[pfx] = build_block(pfx)
        nc = _NC_CACHE[pfx]
        in_maps = []
        for c in range(ncores):
            m = {"x": cur[c]}
            for nm in _BLOCK_WEIGHTS[pfx]:
                m[nm] = wmap[nm]
            m.update(cmap)
            in_maps.append(m)
        res = run_bass_kernel_spmd(nc, in_maps, core_ids=list(range(ncores)))
        cur = [np.asarray(res.results[c]["y"]) for c in range(ncores)]
    return np.stack(cur).astype(np.float32)


def build_fused():
    nc = bass.Bass("TRN2", target_bir_lowering=False)
    with ExitStack() as ctx:
        bb = BB(nc, ctx)
        x_in = nc.dram_tensor("x", [S, D], F32, kind="ExternalInput").ap()
        y = nc.dram_tensor("y", [S, D], F32, kind="ExternalOutput").ap()
        x1 = nc.dram_tensor("x1", [S, D], F32, kind="Internal").ap()
        x2 = nc.dram_tensor("x2", [S, D], F32, kind="Internal").ap()
        x3 = nc.dram_tensor("x3", [S, D], F32, kind="Internal").ap()
        _declare_common(nc, bb)
        for pfx in ("a0", "f0", "m1", "f1"):
            _declare_block_io(nc, bb, pfx)
        emit_l0_attn(bb, x_in, x1, "a0")
        emit_l0_moe(bb, x1, x2, "f0")
        emit_l1_mix(bb, x2, x3, "m1")
        emit_l1_ffn(bb, x3, y, "f1")
    return nc


def kernel(**inputs):
    x = np.asarray(inputs["L0_attn_x"], np.float32)
    ncores = x.shape[0]
    wmap = _prep_all(inputs)
    cmap = {"c_" + k: v for k, v in _consts().items()}

    if "fused" not in _NC_CACHE:
        _NC_CACHE["fused"] = build_fused()
    nc = _NC_CACHE["fused"]
    in_maps = []
    for c in range(ncores):
        m = {"x": np.ascontiguousarray(x[c])}
        m.update(wmap)
        m.update(cmap)
        in_maps.append(m)
    res = run_bass_kernel_spmd(nc, in_maps, core_ids=list(range(ncores)))
    return np.stack([np.asarray(res.results[c]["y"]) for c in range(ncores)]
                    ).astype(np.float32)


# version 05
# speedup vs baseline: 1.7451x; 1.7451x over previous
import numpy as np
import ml_dtypes
from contextlib import ExitStack

import concourse.bass as bass
import concourse.mybir as mybir
from concourse.bass_utils import run_bass_kernel_spmd

F32 = mybir.dt.float32
BF16 = mybir.dt.bfloat16
AF = mybir.ActivationFunctionType
ALU = mybir.AluOpType
AX = mybir.AxisListType

S = 2048
D = 2048
NH = 16
DH = 128
RATIO = 16
NCMP = S // RATIO
WIN = 256
SCALE = DH ** -0.5
HID0 = 1792
NE = 8
HID1 = 8192
EPS = 1e-5
NT = S // 128
ND = D // 128
NT2 = S // 256
NB = D // 512

_BF = ml_dtypes.bfloat16
_CONST_CACHE = {}


def _consts():
    if _CONST_CACHE:
        return _CONST_CACHE
    c = {}
    c["ident"] = np.eye(128, dtype=_BF)
    c["identf"] = np.eye(128, dtype=np.float32)
    c["epsf"] = np.full((128, 1), EPS, dtype=np.float32)
    c["ones_rowS"] = np.ones((1, S), dtype=_BF)
    c["ones_row_f"] = np.ones((1, 128), dtype=np.float32)
    c["ident8f"] = np.eye(8, dtype=np.float32)
    c["ones_col"] = np.ones((128, 1), dtype=_BF)
    c["ones_row"] = np.ones((1, 128), dtype=_BF)
    A = np.zeros((S, NCMP), dtype=np.float32)
    A[np.arange(S), np.arange(S) // RATIO] = 1.0 / RATIO
    c["avg"] = A.astype(_BF)
    inv = 1.0 / (10000.0 ** (np.arange(0, DH, 2, dtype=np.float64) / DH))
    ang = np.outer(inv, np.arange(S, dtype=np.float64))
    cos = np.cos(ang).astype(np.float32)
    sin = np.sin(ang).astype(np.float32)
    c["cs1"] = np.concatenate([cos, sin], axis=0).astype(_BF)
    c["cs2"] = np.concatenate([sin, cos], axis=0).astype(_BF)
    i64 = np.eye(64, dtype=_BF)
    c["rl_e"] = np.concatenate([i64, -i64], axis=0)
    c["rl_o"] = np.concatenate([i64, i64], axis=0)
    p = np.arange(128)[:, None]
    f = np.arange(128)[None, :]
    c["m_w0"] = (f >= p).astype(_BF)
    c["m_w2"] = (f < p).astype(_BF)
    c["m_ones"] = np.ones((128, 128), dtype=_BF)
    c["m_zero"] = np.zeros((128, 128), dtype=_BF)
    v = np.arange(248)[:, None] - 120
    fq = np.arange(128)[None, :]
    c["m_cmp"] = (v < (fq + 1) // 16).astype(_BF)
    i = np.arange(128)
    pc = np.zeros((128, 128), dtype=np.int64)
    for b in range(7):
        pc += ((i[:, None] >> b) & 1) * ((i[None, :] >> b) & 1)
    c["h128"] = (np.where(pc % 2 == 0, 1.0, -1.0) / np.sqrt(8192.0)).astype(_BF)
    _CONST_CACHE.update(c)
    return c


def _rope_perm():
    perm = np.zeros(D, dtype=np.int64)
    for h in range(NH):
        b = h * DH
        perm[b:b + 64] = b + 2 * np.arange(64)
        perm[b + 64:b + 128] = b + 2 * np.arange(64) + 1
    return perm


class Slot:

    def __init__(self, bb, name):
        self.sem = bb.ctx.enter_context(bb.nc.semaphore(name))
        self.n = 0
        self.base = 0

    def rebase(self):
        self.base = self.n
        return self

    def put(self, eng, out, in_, **kw):
        eng.dma_start(out, in_, **kw).then_inc(self.sem, 16)
        self.n += 16
        return self.n

    def wait(self, eng, val=None):
        eng.wait_ge(self.sem, self.n if val is None else val)

    def wait_rel(self, eng, rel):
        eng.wait_ge(self.sem, self.base + rel)


class BB:
    def __init__(self, nc, ctx):
        self.nc = nc
        self.ctx = ctx
        self.sems = {}
        self.cnt = {}
        self.io = {}
        self._n = 0

    def u(self):
        self._n += 1
        return "u%d" % self._n

    def sem(self, name):
        if name not in self.sems:
            self.sems[name] = self.ctx.enter_context(self.nc.semaphore(name))
            self.cnt[name] = 0
        return self.sems[name]

    def inc(self, name, by=1):
        self.cnt[name] += by
        return self.cnt[name]

    def slot(self, name):
        if name not in self.sems:
            self.sems[name] = Slot(self, name)
        return self.sems[name].rebase()


def emit_preln_xt(bb, x_in, xt_scr, pfx):
    nc = bb.nc
    ss, sm, sv = "PNs", "PNm", "PNv"
    for _s in (ss, sm, sv):
        bb.sem(_s)
    bs, bm, bv = bb.cnt[ss], bb.cnt[sm], bb.cnt[sv]
    with ExitStack() as st:
        e = st.enter_context
        xs = [e(nc.sbuf_tensor(f"{pfx}xs{i}", [128, D], F32)) for i in range(2)]
        sq = [e(nc.sbuf_tensor(f"{pfx}sq{i}", [128, 2, D], BF16)) for i in range(2)]
        xln = [e(nc.sbuf_tensor(f"{pfx}xln{i}", [128, D], BF16)) for i in range(2)]
        xtrow = e(nc.sbuf_tensor(f"{pfx}xtrow", [128, ND, 256], BF16))
        idsb = e(nc.sbuf_tensor(f"{pfx}id", [128, 128], BF16))
        epsb = e(nc.sbuf_tensor(f"{pfx}eps", [128, 1], F32))
        stt = e(nc.sbuf_tensor(f"{pfx}stt", [128, 2, 8], F32))
        ptp = e(nc.psum_tensor(f"{pfx}ptp", [128, 2, 32, 128], BF16))
        sx = [bb.slot("PNdx%d" % i) for i in range(2)]
        sxt = bb.slot("PNdt")
        scn = bb.slot("PNdc")

        with nc.Block() as blk:
            @blk.sync
            def _(g):
                scn.put(g, epsb[:, :], bb.io["c_epsf"][:, :])
                scn.put(g, idsb[:, :], bb.io["c_ident"][:, :])
                for m2 in range(NT2 + 1):
                    if m2 < NT2:
                        for sub in range(2):
                            m = 2 * m2 + sub
                            if m >= 2:
                                g.wait_ge(bb.sem(ss), bs + m - 1)
                            sx[m % 2].put(g, xs[m % 2][:, :],
                                          x_in[m * 128:(m + 1) * 128, :])
                    if m2 >= 1:
                        g.wait_ge(bb.sem(sv), bv + m2)
                        sxt.put(g, xt_scr[:, m2 - 1].rearrange("d p f -> p d f"),
                                xtrow[:, :, :])
                sxt.wait(g)

            @blk.scalar
            def _(s):
                scn.wait_rel(s, 32)
                for m in range(NT):
                    i = m % 2
                    s1, s2 = stt[:, i, 0:1], stt[:, i, 1:2]
                    mean, s2d = stt[:, i, 2:3], stt[:, i, 3:4]
                    tvd, var = stt[:, i, 4:5], stt[:, i, 5:6]
                    lnv, rstd = stt[:, i, 6:7], stt[:, i, 7:8]
                    sx[i].wait_rel(s, 16 * (m // 2 + 1))
                    s.activation(sq[i][:, 0, :], xs[i][:, :], AF.Copy,
                                 accum_out=s1)
                    s.activation(sq[i][:, 1, :], xs[i][:, :], AF.Square,
                                 accum_out=s2)
                    s.drain()
                    s.activation(mean, s1, AF.Copy, scale=1.0 / D)
                    s.activation(s2d, s2, AF.Copy, scale=1.0 / D)
                    s.drain()
                    s.activation(tvd, mean, AF.Square)
                    s.drain()
                    s.activation(var, tvd, AF.Identity, scale=-1.0, bias=s2d)
                    s.drain()
                    s.activation(lnv, var, AF.Ln, bias=epsb[:, :])
                    s.drain()
                    s.activation(rstd, lnv, AF.Exp, scale=-0.5)
                    s.drain()
                    s.activation(tvd, rstd, AF.Copy, scale=-1.0)
                    s.drain()
                    s.activation(s2d, mean, AF.Copy, scale=tvd)
                    s.drain()
                    if m >= 2:
                        s.wait_ge(bb.sem(sm), bm + m // 2)
                    s.activation(xln[i][:, :], xs[i][:, :], AF.Identity,
                                 scale=rstd, bias=s2d).then_inc(bb.sem(ss), 1)
                    bb.inc(ss)

            @blk.tensor
            def _(t):
                scn.wait_rel(t, 32)
                for m2 in range(NT2):
                    if m2 >= 2:
                        t.wait_ge(bb.sem(sv), bv + m2 - 1)
                    for sub in range(2):
                        m = 2 * m2 + sub
                        t.wait_ge(bb.sem(ss), bs + m + 1)
                        for dt in range(ND):
                            tp = t.transpose(ptp[:, m2 % 2, sub * ND + dt, :],
                                        xln[m % 2][:, dt * 128:(dt + 1) * 128],
                                        idsb[:, :])
                    tp.then_inc(bb.sem(sm), 1)
                    bb.inc(sm)

            @blk.vector
            def _(v):
                for m2 in range(NT2):
                    v.wait_ge(bb.sem(sm), bm + m2 + 1)
                    if m2 >= 1:
                        sxt.wait_rel(v, 16 * m2)
                    for sub in range(2):
                        for dt in range(ND):
                            cp = v.tensor_copy(
                                xtrow[:, dt, sub * 128:(sub + 1) * 128],
                                ptp[:, m2 % 2, sub * ND + dt, :])
                    cp.then_inc(bb.sem(sv), 1)
                    bb.inc(sv)


def emit_out_ln_residual(bb, x_in, x_out, pfx, emit_mm, mm_waits=None,
                         pre_g=None):
    nc = bb.nc
    ss, sm, sv = "OTs", "OTm", "OTv"
    for _s in (ss, sm, sv):
        bb.sem(_s)
    bs, bm, bv = bb.cnt[ss], bb.cnt[sm], bb.cnt[sv]
    with ExitStack() as st:
        e = st.enter_context
        xs = [e(nc.sbuf_tensor(f"{pfx}xs{i}", [128, D], F32)) for i in range(2)]
        sq = [e(nc.sbuf_tensor(f"{pfx}sq{i}", [128, 2, D], BF16)) for i in range(2)]
        tmp = [e(nc.sbuf_tensor(f"{pfx}tmp{i}", [128, D], F32)) for i in range(2)]
        xo = [e(nc.sbuf_tensor(f"{pfx}xo{i}", [128, D], F32)) for i in range(2)]
        epsb = e(nc.sbuf_tensor(f"{pfx}eps", [128, 1], F32))
        stt = e(nc.sbuf_tensor(f"{pfx}stt", [128, 2, 8], F32))
        pE = e(nc.psum_tensor(f"{pfx}pE", [128, 2, D], F32))
        sxi = [bb.slot("OTdi%d" % i) for i in range(2)]
        sxo = [bb.slot("OTdo%d" % i) for i in range(2)]
        sce = bb.slot("OTdce")

        with nc.Block() as blk:
            @blk.sync
            def _(g):
                sce.put(g, epsb[:, :], bb.io["c_epsf"][:, :])
                for m in range(NT + 2):
                    if m < NT:
                        if pre_g is not None:
                            pre_g(g, m)
                        if m >= 2:
                            g.wait_ge(bb.sem(sv), bv + 2 * (m - 1))
                        sxi[m % 2].put(g, xs[m % 2][:, :],
                                       x_in[m * 128:(m + 1) * 128, :])
                    if m >= 2:
                        mm_ = m - 2
                        g.wait_ge(bb.sem(sv), bv + 2 * mm_ + 2)
                        sxo[mm_ % 2].put(g, x_out[mm_ * 128:(mm_ + 1) * 128, :],
                                         xo[mm_ % 2][:, :])
                sxo[0].wait(g)
                sxo[1].wait(g)

            @blk.tensor
            def _(t):
                for m in range(NT):
                    if m >= 2:
                        t.wait_ge(bb.sem(sv), bv + 2 * (m - 1) + 2)
                        t.wait_ge(bb.sem(ss), bs + m - 1)
                    if mm_waits is not None:
                        mm_waits(t, m)
                    last = emit_mm(t, m, pE[:, m % 2, :])
                    last.then_inc(bb.sem(sm), 1)
                    bb.inc(sm)

            @blk.scalar
            def _(s):
                sce.wait_rel(s, 16)
                for m in range(NT):
                    i = m % 2
                    s1, s2 = stt[:, i, 0:1], stt[:, i, 1:2]
                    mean, s2d = stt[:, i, 2:3], stt[:, i, 3:4]
                    tvd, var = stt[:, i, 4:5], stt[:, i, 5:6]
                    lnv, rstd = stt[:, i, 6:7], stt[:, i, 7:8]
                    s.wait_ge(bb.sem(sm), bm + m + 1)
                    s.activation(sq[i][:, 0, :], pE[:, i, :], AF.Copy,
                                 accum_out=s1)
                    s.activation(sq[i][:, 1, :], pE[:, i, :], AF.Square,
                                 accum_out=s2)
                    s.drain()
                    s.activation(mean, s1, AF.Copy, scale=1.0 / D)
                    s.activation(s2d, s2, AF.Copy, scale=1.0 / D)
                    s.drain()
                    s.activation(tvd, mean, AF.Square)
                    s.drain()
                    s.activation(var, tvd, AF.Identity, scale=-1.0, bias=s2d)
                    s.drain()
                    s.activation(lnv, var, AF.Ln, bias=epsb[:, :])
                    s.drain()
                    s.activation(rstd, lnv, AF.Exp, scale=-0.5
                                 ).then_inc(bb.sem(ss), 1)
                    bb.inc(ss)

            @blk.vector
            def _(v):
                for m in range(NT):
                    i = m % 2
                    mean, rstd = stt[:, i, 2:3], stt[:, i, 7:8]
                    v.wait_ge(bb.sem(ss), bs + m + 1)
                    v.tensor_scalar(tmp[i][:, :], pE[:, i, :], mean, rstd,
                                    op0=ALU.subtract, op1=ALU.mult
                                    ).then_inc(bb.sem(sv), 1)
                    bb.inc(sv)
                    v.drain()
                    sxi[i].wait_rel(v, 16 * (m // 2 + 1))
                    if m >= 2:
                        sxo[i].wait_rel(v, 16 * (m // 2))
                    v.tensor_tensor(xo[i][:, :], tmp[i][:, :], xs[i][:, :],
                                    op=ALU.add).then_inc(bb.sem(sv), 1)
                    bb.inc(sv)


def emit_l0_attn(bb, x_in, x_out, pfx="a0"):
    nc = bb.nc
    io = bb.io
    xt = io[pfx + "_xt"]
    v_scr = io[pfx + "_vscr"]

    emit_preln_xt(bb, x_in, xt, pfx + "A")

    with ExitStack() as st0:
        e0 = st0.enter_context
        OT = e0(nc.sbuf_tensor(pfx + "OT", [128, ND, S], BF16))
        onesc = e0(nc.sbuf_tensor(pfx + "onesc", [128, 1], BF16))
        onesr = e0(nc.sbuf_tensor(pfx + "onesr", [1, 128], BF16))
        idsb = e0(nc.sbuf_tensor(pfx + "id", [128, 128], BF16))
        scst = bb.slot("A0dc")

        with nc.Block() as blk:
            @blk.sync
            def _(g):
                scst.put(g, onesc[:, :], io["c_ones_col"][:, :])
                scst.put(g, onesr[:, :], io["c_ones_row"][:, :])
                scst.put(g, idsb[:, :], io["c_ident"][:, :])
                scst.wait(g)

        _l0_v_phase(bb, xt, v_scr, pfx)

        with ExitStack() as stC:
            eC = stC.enter_context
            avgb = eC(nc.sbuf_tensor(pfx + "avg", [128, NT, 128], BF16))
            maskq = eC(nc.sbuf_tensor(pfx + "mq", [128, NT, 512], BF16))
            cs1b = eC(nc.sbuf_tensor(pfx + "cs1", [128, S], BF16))
            cs2b = eC(nc.sbuf_tensor(pfx + "cs2", [128, S], BF16))
            rleb = eC(nc.sbuf_tensor(pfx + "rle", [128, 64], BF16))
            rlob = eC(nc.sbuf_tensor(pfx + "rlo", [128, 64], BF16))
            sck = bb.slot("A0dk")
            svh = bb.slot("A0dv")
            with nc.Block() as blk:
                @blk.sync
                def _(g):
                    sck.put(g, cs1b[:, :], io["c_cs1"][:, :])
                    sck.put(g, cs2b[:, :], io["c_cs2"][:, :])
                    sck.put(g, rleb[:, :], io["c_rl_e"][:, :])
                    sck.put(g, rlob[:, :], io["c_rl_o"][:, :])
                    for j in range(NT):
                        sck.put(g, avgb[:, j, :],
                                io["c_avg"][j * 128:(j + 1) * 128, :])
                    for qt in range(NT):
                        sck.put(g, maskq[:, qt, 0:128],
                                io["c_m_cmp"][120 - 8 * qt:248 - 8 * qt, :])
                        src2 = io["c_m_w2"] if qt >= 2 else io["c_m_zero"]
                        src1 = io["c_m_ones"] if qt >= 1 else io["c_m_zero"]
                        sck.put(g, maskq[:, qt, 128:256], src2[:, :])
                        sck.put(g, maskq[:, qt, 256:384], src1[:, :])
                        sck.put(g, maskq[:, qt, 384:512], io["c_m_w0"][:, :])
                    sck.wait(g)

            for grp in range(NH // 2):
                _l0_attn_group(bb, grp, xt, v_scr, OT, onesc, onesr, idsb,
                               avgb, maskq, cs1b, cs2b, rleb, rlob, svh, pfx)

        with ExitStack() as st:
            wob = st.enter_context(nc.sbuf_tensor(pfx + "wob", [128, ND, D], BF16))
            swo = bb.slot("A0dw")
            with nc.Block() as blk:
                @blk.gpsimd
                def _(g):
                    for k in range(ND):
                        swo.put(g, wob[:, k, :],
                                io[pfx + "_woT"][k * 128:(k + 1) * 128, :])
                    swo.wait(g)

            def emit_mm(t, m, psum):
                for k in range(ND):
                    for nb in range(NB):
                        mm = t.matmul(psum[:, nb * 512:(nb + 1) * 512],
                                 OT[:, k, m * 128:(m + 1) * 128],
                                 wob[:, k, nb * 512:(nb + 1) * 512],
                                 start=(k == 0), stop=(k == ND - 1))
                return mm

            emit_out_ln_residual(bb, x_in, x_out, pfx + "E", emit_mm)


def _l0_v_phase(bb, xt, v_scr, pfx):
    nc = bb.nc
    io = bb.io
    with ExitStack() as st:
        e = st.enter_context
        wvb = e(nc.sbuf_tensor(pfx + "wvb", [128, ND, D], BF16))
        xtb = e(nc.sbuf_tensor(pfx + "xtb", [128, 2, ND, 256], BF16))
        vout = [e(nc.sbuf_tensor(f"{pfx}vo{i}", [128, D], BF16)) for i in range(2)]
        pV = e(nc.psum_tensor(pfx + "pV", [128, 2, D], F32))
        swv = bb.slot("Vdwv")
        sxtb = [bb.slot("Vdxb%d" % i) for i in range(2)]
        svo = [bb.slot("Vdvo%d" % i) for i in range(2)]
        sm, sv = "BVm", "BVv"
        bb.sem(sm), bb.sem(sv)
        bm, bv = bb.cnt[sm], bb.cnt[sv]

        with nc.Block() as blk:
            @blk.gpsimd
            def _(g):
                for k in range(ND):
                    swv.put(g, wvb[:, k, :],
                            io[pfx + "_wvT"][k * 128:(k + 1) * 128, :])

            @blk.sync
            def _(g):
                for m2 in range(NT2 + 1):
                    if m2 < NT2:
                        if m2 >= 2:
                            g.wait_ge(bb.sem(sm), bm + 2 * (m2 - 1))
                        sxtb[m2 % 2].put(g, xtb[:, m2 % 2, :, :],
                                         xt[:, m2].rearrange("k p f -> p k f"))
                    if m2 >= 1:
                        for sub in range(2):
                            m = 2 * (m2 - 1) + sub
                            g.wait_ge(bb.sem(sv), bv + m + 1)
                            svo[m % 2].put(
                                g, v_scr[:, m].rearrange("h p f -> p h f"),
                                vout[m % 2][:, :])
                svo[0].wait(g)
                svo[1].wait(g)

            @blk.tensor
            def _(t):
                swv.wait(t)
                for m2 in range(NT2):
                    sxtb[m2 % 2].wait_rel(t, 16 * (m2 // 2 + 1))
                    for sub in range(2):
                        m = 2 * m2 + sub
                        if m >= 2:
                            t.wait_ge(bb.sem(sv), bv + m - 1)
                        for k in range(ND):
                            for nb in range(NB):
                                mm = t.matmul(pV[:, m % 2, nb * 512:(nb + 1) * 512],
                                         xtb[:, m2 % 2, k, sub * 128:(sub + 1) * 128],
                                         wvb[:, k, nb * 512:(nb + 1) * 512],
                                         start=(k == 0), stop=(k == ND - 1))
                        mm.then_inc(bb.sem(sm), 1)
                        bb.inc(sm)

            @blk.scalar
            def _(s):
                for m in range(NT):
                    s.wait_ge(bb.sem(sm), bm + m + 1)
                    if m >= 2:
                        svo[m % 2].wait_rel(s, 16 * (m // 2))
                    s.copy(vout[m % 2][:, :], pV[:, m % 2, :]
                           ).then_inc(bb.sem(sv), 1)
                    bb.inc(sv)


def _l0_attn_group(bb, grp, xt, v_scr, OT, onesc, onesr, idsb, avgb, maskq,
                   cs1b, cs2b, rleb, rlob, svh, pfx):
    nc = bb.nc
    io = bb.io
    g0 = grp * 2
    un = pfx + bb.u()
    with ExitStack() as st:
        e = st.enter_context
        wqg = e(nc.sbuf_tensor(un + "wqg", [128, ND, 256], BF16))
        wkg = e(nc.sbuf_tensor(un + "wkg", [128, ND, 256], BF16))
        xtq = e(nc.sbuf_tensor(un + "xtq", [128, 2, ND, 512], BF16))
        QT = e(nc.sbuf_tensor(un + "QT", [128, 2, S], BF16))
        KT = e(nc.sbuf_tensor(un + "KT", [128, 2, S], BF16))
        QTr = e(nc.sbuf_tensor(un + "QTr", [128, 2, S], BF16))
        KTr = e(nc.sbuf_tensor(un + "KTr", [128, 2, S], BF16))
        rP = e(nc.sbuf_tensor(un + "rP", [128, S], BF16))
        rP2 = e(nc.sbuf_tensor(un + "rP2", [128, S], BF16))
        swg = bb.slot("Gdw")
        sxq = [bb.slot("Gdq%d" % i) for i in range(2)]

        sm, ss = "BGm", "BGs"
        bb.sem(sm), bb.sem(ss)
        bm, bs = bb.cnt[sm], bb.cnt[ss]

        with ExitStack() as st1:
            pqk = st1.enter_context(
                nc.psum_tensor(un + "pqk", [128, 2, 4, 512], F32))
            with nc.Block() as blk:
                @blk.gpsimd
                def _(g):
                    swg.put(g, wqg[:, :, :],
                            io[pfx + "_wqT"][:, g0 * 128:(g0 + 2) * 128
                                             ].rearrange("(kt p) n -> p kt n",
                                                         p=128))
                    swg.put(g, wkg[:, :, :],
                            io[pfx + "_wkT"][:, g0 * 128:(g0 + 2) * 128
                                             ].rearrange("(kt p) n -> p kt n",
                                                         p=128))

                @blk.sync
                def _(g):
                    for tq in range(4):
                        if tq >= 2:
                            g.wait_ge(bb.sem(sm), bm + tq - 1)
                        sxq[tq % 2].put(g, xtq[:, tq % 2, :, 0:256],
                                        xt[:, 2 * tq].rearrange("k p f -> p k f"))
                        sxq[tq % 2].put(g, xtq[:, tq % 2, :, 256:512],
                                        xt[:, 2 * tq + 1].rearrange(
                                            "k p f -> p k f"))

                @blk.tensor
                def _(t):
                    swg.wait(t)
                    for tq in range(4):
                        sxq[tq % 2].wait_rel(t, 32 * (tq // 2 + 1))
                        if tq >= 2:
                            t.wait_ge(bb.sem(ss), bs + 4 * (tq - 1))
                        for k in range(ND):
                            for hi in range(2):
                                t.matmul(pqk[:, tq % 2, hi, :],
                                         wqg[:, k, hi * 128:(hi + 1) * 128],
                                         xtq[:, tq % 2, k, :],
                                         start=(k == 0), stop=(k == ND - 1))
                                mm = t.matmul(pqk[:, tq % 2, 2 + hi, :],
                                         wkg[:, k, hi * 128:(hi + 1) * 128],
                                         xtq[:, tq % 2, k, :],
                                         start=(k == 0), stop=(k == ND - 1))
                        mm.then_inc(bb.sem(sm), 1)
                        bb.inc(sm)

                @blk.scalar
                def _(s):
                    for tq in range(4):
                        s.wait_ge(bb.sem(sm), bm + tq + 1)
                        s.copy(QT[:, 0, tq * 512:(tq + 1) * 512],
                               pqk[:, tq % 2, 0, :])
                        s.copy(QT[:, 1, tq * 512:(tq + 1) * 512],
                               pqk[:, tq % 2, 1, :])
                        s.copy(KT[:, 0, tq * 512:(tq + 1) * 512],
                               pqk[:, tq % 2, 2, :])
                        s.copy(KT[:, 1, tq * 512:(tq + 1) * 512],
                               pqk[:, tq % 2, 3, :]).then_inc(bb.sem(ss), 4)
                        bb.inc(ss, 4)

        with ExitStack() as str_:
            prr = str_.enter_context(nc.psum_tensor(un + "prr", [128, S], F32))
            smr, svr, ssr = "BRm", "BRv", "BRs"
            bb.sem(smr), bb.sem(svr), bb.sem(ssr)
            bmr, bvr, bsr = bb.cnt[smr], bb.cnt[svr], bb.cnt[ssr]
            cases = [(hi, src, dst) for hi in range(2)
                     for (src, dst) in ((QT, QTr), (KT, KTr))]
            with nc.Block() as blk:
                @blk.vector
                def _(v):
                    for i, (hi, src, dst) in enumerate(cases):
                        if i >= 1:
                            v.wait_ge(bb.sem(smr), bmr + i)
                        v.tensor_tensor(rP[0:64, :], src[0:64, hi, :],
                                        cs1b[0:64, :], op=ALU.mult)
                        v.tensor_tensor(rP[64:128, :], src[64:128, hi, :],
                                        cs1b[64:128, :], op=ALU.mult)
                        v.tensor_tensor(rP2[0:64, :], src[0:64, hi, :],
                                        cs2b[0:64, :], op=ALU.mult)
                        v.tensor_tensor(rP2[64:128, :], src[64:128, hi, :],
                                        cs2b[64:128, :], op=ALU.mult
                                        ).then_inc(bb.sem(svr), 1)
                        bb.inc(svr)

                @blk.tensor
                def _(t):
                    for i, (hi, src, dst) in enumerate(cases):
                        t.wait_ge(bb.sem(svr), bvr + i + 1)
                        if i >= 1:
                            t.wait_ge(bb.sem(ssr), bsr + i)
                        for nb in range(NB):
                            t.matmul(prr[0:64, nb * 512:(nb + 1) * 512],
                                     rleb[:, :], rP[:, nb * 512:(nb + 1) * 512],
                                     start=True, stop=True, skip_group_check=True)
                            mm = t.matmul(prr[64:128, nb * 512:(nb + 1) * 512],
                                     rlob[:, :], rP2[:, nb * 512:(nb + 1) * 512],
                                     start=True, stop=True, skip_group_check=True)
                        mm.then_inc(bb.sem(smr), 1)
                        bb.inc(smr)

                @blk.scalar
                def _(s):
                    for i, (hi, src, dst) in enumerate(cases):
                        s.wait_ge(bb.sem(smr), bmr + i + 1)
                        s.copy(dst[:, hi, :], prr[:, :]).then_inc(bb.sem(ssr), 1)
                        bb.inc(ssr)

        for hi in range(2):
            _l0_attn_head(bb, g0 + hi, hi, QTr, KTr, v_scr, OT, onesc, onesr,
                          idsb, avgb, maskq, svh, pfx)


def _l0_attn_head(bb, h, hi, QTr, KTr, v_scr, OT, onesc, onesr, idsb, avgb,
                  maskq, svh, pfx):
    nc = bb.nc
    sm, ss, sv = "BHm", "BHs", "BHv"
    for _s in (sm, ss, sv):
        bb.sem(_s)
    un = pfx + bb.u()
    with ExitStack() as st:
        e = st.enter_context
        vh = e(nc.sbuf_tensor(un + "vh", [128, NT, 128], BF16))
        ktok = e(nc.sbuf_tensor(un + "ktok", [128, NT, 128], BF16))
        ckb = e(nc.sbuf_tensor(un + "ckb", [128, 128], BF16))
        cvb = e(nc.sbuf_tensor(un + "cvb", [128, 128], BF16))
        sqs = e(nc.sbuf_tensor(un + "sqs", [128, S], BF16))
        nrm = e(nc.sbuf_tensor(un + "nrm", [1, 8], F32))
        c11b = e(nc.sbuf_tensor(un + "c11b", [1, 1], BF16))
        negC = e(nc.sbuf_tensor(un + "negC", [128, 1], F32))

        with nc.Block() as blk:
            @blk.sync
            def _(g):
                svh.put(g, vh[:, :, :], v_scr[h].rearrange("m p f -> p m f"))
                svh.wait(g)

        with ExitStack() as sta:
            ptk = sta.enter_context(nc.psum_tensor(un + "ptk", [128, NT, 128],
                                                   BF16))
            with nc.Block() as blk:
                @blk.tensor
                def _(t):
                    for j in range(NT):
                        tp = t.transpose(ptk[:, j, :],
                                    KTr[:, hi, j * 128:(j + 1) * 128], idsb[:, :])
                    tp.then_inc(bb.sem(sm), 1)
                    bb.inc(sm)

                @blk.vector
                def _(v):
                    v.wait_ge(bb.sem(sm), bb.cnt[sm])
                    for j in range(NT):
                        cp = v.tensor_copy(ktok[:, j, :], ptk[:, j, :])
                    cp.then_inc(bb.sem(sv), 1)
                    bb.inc(sv)

        with ExitStack() as st1:
            e1 = st1.enter_context
            pck = e1(nc.psum_tensor(un + "pck", [128, 128], F32))
            pcv = e1(nc.psum_tensor(un + "pcv", [128, 128], F32))
            pnr = e1(nc.psum_tensor(un + "pnr", [1, S], F32))
            pb1 = e1(nc.psum_tensor(un + "pb1", [128, 1], F32))

            with nc.Block() as blk:
                @blk.tensor
                def _(t):
                    for j in range(NT):
                        t.matmul(pck[:, :], ktok[:, j, :], avgb[:, j, :],
                                 start=(j == 0), stop=(j == NT - 1))
                    for j in range(NT):
                        mm = t.matmul(pcv[:, :], avgb[:, j, :], vh[:, j, :],
                                 start=(j == 0), stop=(j == NT - 1))
                    mm.then_inc(bb.sem(sm), 1)
                    bb.inc(sm)

                @blk.scalar
                def _(s):
                    s.wait_ge(bb.sem(sm), bb.cnt[sm])
                    s.copy(ckb[:, :], pck[:, :])
                    s.copy(cvb[:, :], pcv[:, :]).then_inc(bb.sem(ss), 1)
                    bb.inc(ss)

            for (srcT, slot) in ((QTr, 0), (KTr, 1)):
                with nc.Block() as blk:
                    @blk.scalar
                    def _(s, srcT=srcT):
                        s.activation(sqs[:, :], srcT[:, hi, :], AF.Square
                                     ).then_inc(bb.sem(ss), 1)
                        bb.inc(ss)

                    @blk.tensor
                    def _(t):
                        t.wait_ge(bb.sem(ss), bb.cnt[ss])
                        for nb in range(NB):
                            mm = t.matmul(pnr[0:1, nb * 512:(nb + 1) * 512],
                                     onesc[:, :], sqs[:, nb * 512:(nb + 1) * 512],
                                     start=True, stop=True)
                        mm.then_inc(bb.sem(sm), 1)
                        bb.inc(sm)

                    @blk.vector
                    def _(v, slot=slot):
                        v.wait_ge(bb.sem(sm), bb.cnt[sm])
                        v.reduce_max(nrm[0:1, slot:slot + 1], pnr[0:1, :],
                                     axis=AX.X).then_inc(bb.sem(sv), 1)
                        bb.inc(sv)

            with nc.Block() as blk:
                @blk.scalar
                def _(s):
                    s.activation(sqs[:, 0:128], ckb[:, :], AF.Square
                                 ).then_inc(bb.sem(ss), 1)
                    bb.inc(ss)

                @blk.tensor
                def _(t):
                    t.wait_ge(bb.sem(ss), bb.cnt[ss])
                    t.matmul(pnr[0:1, 0:128], onesc[:, :], sqs[:, 0:128],
                             start=True, stop=True).then_inc(bb.sem(sm), 1)
                    bb.inc(sm)

                @blk.vector
                def _(v):
                    v.wait_ge(bb.sem(sm), bb.cnt[sm])
                    v.reduce_max(nrm[0:1, 2:3], pnr[0:1, 0:128], axis=AX.X)
                    v.drain()
                    v.tensor_tensor(nrm[0:1, 3:4], nrm[0:1, 1:2], nrm[0:1, 2:3],
                                    op=ALU.max)
                    v.drain()
                    v.tensor_tensor(nrm[0:1, 4:5], nrm[0:1, 0:1], nrm[0:1, 3:4],
                                    op=ALU.mult).then_inc(bb.sem(sv), 1)
                    bb.inc(sv)

                @blk.scalar
                def _(s2):
                    s2.wait_ge(bb.sem(sv), bb.cnt[sv])
                    s2.activation(nrm[0:1, 5:6], nrm[0:1, 4:5], AF.Sqrt)
                    s2.drain()
                    s2.activation(nrm[0:1, 6:7], nrm[0:1, 5:6], AF.Copy,
                                  scale=-SCALE).then_inc(bb.sem(ss), 1)
                    bb.inc(ss)

                @blk.vector
                def _(v2):
                    v2.wait_ge(bb.sem(ss), bb.cnt[ss])
                    v2.tensor_copy(c11b[:, :], nrm[0:1, 6:7]
                                   ).then_inc(bb.sem(sv), 1)
                    bb.inc(sv)

            with nc.Block() as blk:
                @blk.tensor
                def _(t):
                    t.matmul(pb1[:, :], onesr[0:1, :], c11b[:, :],
                             start=True, stop=True).then_inc(bb.sem(sm), 1)
                    bb.inc(sm)

                @blk.scalar
                def _(s):
                    s.wait_ge(bb.sem(sm), bb.cnt[sm])
                    s.copy(negC[:, :], pb1[:, :]).then_inc(bb.sem(ss), 1)
                    bb.inc(ss)

        for half in range(2):
            _l0_attn_half(bb, h, hi, half, QTr, KTr, vh, OT, onesc, onesr,
                          ckb, cvb, maskq, negC, pfx)


def _l0_attn_half(bb, h, hi, half, QTr, KTr, vh, OT, onesc, onesr, ckb, cvb,
                  maskq, negC, pfx):
    nc = bb.nc
    sm, ss, sv = "BWm", "BWs", "BWv"
    for _s in (sm, ss, sv):
        bb.sem(_s)
    un = pfx + bb.u()
    with ExitStack() as st:
        e = st.enter_context
        pe_sb = e(nc.sbuf_tensor(un + "pe", [128, 2, 512], BF16))
        pm_sb = e(nc.sbuf_tensor(un + "pm", [128, 2, 512], BF16))
        rrow = e(nc.sbuf_tensor(un + "rrow", [1, 1024], F32))
        rrowb = e(nc.sbuf_tensor(un + "rrowb", [1, 1024], BF16))
        rb_sb = e(nc.sbuf_tensor(un + "rb", [128, 1024], BF16))
        ps = e(nc.psum_tensor(un + "ps", [128, 2, 512], F32))
        pOT = e(nc.psum_tensor(un + "pOT", [128, 1024], F32))
        prow = e(nc.psum_tensor(un + "prow", [1, 1024], F32))
        pbc = e(nc.psum_tensor(un + "pbc", [128, 1024], F32))

        qts = [half * 8 + i for i in range(8)]
        bm, bs, bv = bb.cnt[sm], bb.cnt[ss], bb.cnt[sv]

        def sm_scores(i):
            return bm + (i + 1 if i < 2 else 2 * i - 1)

        def sm_consume(i):
            return bm + (2 * i + 4 if i <= 5 else 9 + i)

        with nc.Block() as blk:
            @blk.tensor
            def _(t):
                for j in range(10):
                    if j < 8:
                        i = j
                        qt = qts[i]
                        if i >= 2:
                            t.wait_ge(bb.sem(ss), bs + i - 1)
                        q_ap = QTr[:, hi, qt * 128:(qt + 1) * 128]
                        t.matmul(ps[:, i % 2, 0:128], ckb[:, :], q_ap,
                                 start=True, stop=False, skip_group_check=True)
                        for d, jt in enumerate((max(qt - 2, 0), max(qt - 1, 0),
                                                qt)):
                            mm = t.matmul(
                                ps[:, i % 2, 128 * (d + 1):128 * (d + 2)],
                                KTr[:, hi, jt * 128:(jt + 1) * 128], q_ap,
                                start=False, stop=(d == 2),
                                skip_group_check=True)
                        mm.then_inc(bb.sem(sm), 1)
                        bb.inc(sm)
                    if j >= 2:
                        i = j - 2
                        qt = qts[i]
                        t.wait_ge(bb.sem(sv), bv + i + 1)
                        col = (i % 4) * 128
                        bank = i // 4
                        first = (i % 4 == 0)
                        reg = slice(bank * 512 + col, bank * 512 + col + 128)
                        t.matmul(prow[0:1, reg], onesc[:, :],
                                 pm_sb[:, i % 2, 0:128],
                                 start=first, stop=False, skip_group_check=True)
                        t.matmul(pOT[:, reg], cvb[:, :], pm_sb[:, i % 2, 0:128],
                                 start=first, stop=False, skip_group_check=True)
                        for d, jt in enumerate((max(qt - 2, 0), max(qt - 1, 0),
                                                qt)):
                            pm_ap = pm_sb[:, i % 2, 128 * (d + 1):128 * (d + 2)]
                            t.matmul(prow[0:1, reg], onesc[:, :], pm_ap,
                                     start=False, stop=False,
                                     skip_group_check=True)
                            mm = t.matmul(pOT[:, reg], vh[:, jt, :], pm_ap,
                                     start=False, stop=False,
                                     skip_group_check=True)
                        mm.then_inc(bb.sem(sm), 1)
                        bb.inc(sm)

            @blk.scalar
            def _(s):
                for i in range(8):
                    s.wait_ge(bb.sem(sm), sm_scores(i))
                    if i >= 2:
                        s.wait_ge(bb.sem(sv), bv + i - 1)
                    s.activation(pe_sb[:, i % 2, :], ps[:, i % 2, :], AF.Exp,
                                 scale=SCALE, bias=negC[:, :]
                                 ).then_inc(bb.sem(ss), 1)
                    bb.inc(ss)

            @blk.vector
            def _(v):
                for i in range(8):
                    v.wait_ge(bb.sem(ss), bs + i + 1)
                    if i >= 2:
                        v.wait_ge(bb.sem(sm), sm_consume(i - 2))
                    v.tensor_tensor(pm_sb[:, i % 2, :], pe_sb[:, i % 2, :],
                                    maskq[:, qts[i], :], op=ALU.mult
                                    ).then_inc(bb.sem(sv), 1)
                    bb.inc(sv)

        bm, bs, bv = bb.cnt[sm], bb.cnt[ss], bb.cnt[sv]
        with nc.Block() as blk:
            @blk.vector
            def _(v):
                v.reciprocal(rrow[:, :], prow[0:1, :])
                v.drain()
                v.tensor_copy(rrowb[:, :], rrow[:, :]).then_inc(bb.sem(sv), 1)
                bb.inc(sv)

            @blk.tensor
            def _(t):
                t.wait_ge(bb.sem(sv), bv + 1)
                t.matmul(pbc[:, 0:512], onesr[0:1, :], rrowb[0:1, 0:512],
                         start=True, stop=True)
                t.matmul(pbc[:, 512:1024], onesr[0:1, :], rrowb[0:1, 512:1024],
                         start=True, stop=True).then_inc(bb.sem(sm), 1)
                bb.inc(sm)

            @blk.scalar
            def _(s):
                s.wait_ge(bb.sem(sm), bm + 1)
                s.copy(rb_sb[:, :], pbc[:, :]).then_inc(bb.sem(ss), 1)
                bb.inc(ss)

            @blk.vector
            def _(v2):
                v2.wait_ge(bb.sem(ss), bs + 1)
                v2.tensor_tensor(OT[0:128, h, half * 1024:(half + 1) * 1024],
                                 pOT[:, :], rb_sb[:, :], op=ALU.mult)


def _prep_l0_attn(inputs, perm):
    wq = np.asarray(inputs["L0_attn_wq"], np.float32)
    wk = np.asarray(inputs["L0_attn_wk"], np.float32)
    wv = np.asarray(inputs["L0_attn_wv"], np.float32)
    wo = np.asarray(inputs["L0_attn_wo"], np.float32)
    return {
        "a0_wqT": np.ascontiguousarray(wq.T[:, perm]),
        "a0_wkT": np.ascontiguousarray(wk.T[:, perm]),
        "a0_wvT": np.ascontiguousarray(wv.T),
        "a0_woT": np.ascontiguousarray(wo.T),
    }


def _declare_common(nc, bb):
    for k, arr in _consts().items():
        dt = BF16 if arr.dtype == _BF else F32
        bb.io["c_" + k] = nc.dram_tensor("c_" + k, list(arr.shape), dt,
                                         kind="ExternalInput").ap()


def build_l0_attn():
    nc = bass.Bass("TRN2", target_bir_lowering=False)
    with ExitStack() as ctx:
        bb = BB(nc, ctx)
        x_in = nc.dram_tensor("x", [S, D], F32, kind="ExternalInput").ap()
        x_out = nc.dram_tensor("y", [S, D], F32, kind="ExternalOutput").ap()
        for nm in ["a0_wqT", "a0_wkT", "a0_wvT", "a0_woT"]:
            bb.io[nm] = nc.dram_tensor(nm, [D, D], F32, kind="ExternalInput").ap()
        _declare_common(nc, bb)
        bb.io["a0_xt"] = nc.dram_tensor("a0_xt", [ND, NT2, 128, 256], BF16,
                                        kind="Internal").ap()
        bb.io["a0_vscr"] = nc.dram_tensor("a0_vscr", [NH, NT, 128, 128], BF16,
                                          kind="Internal").ap()
        emit_l0_attn(bb, x_in, x_out, "a0")
    return nc


def kernel(**inputs):
    raise NotImplementedError("full module not assembled yet")


def emit_scr_out_phase(bb, x_in, x_out, acc_scr, pfx):
    nc = bb.nc
    un = pfx + bb.u()
    with ExitStack() as st:
        e = st.enter_context
        otile = [e(nc.sbuf_tensor(f"{un}ot{i}", [128, ND, 128], F32))
                 for i in range(2)]
        idf = e(nc.sbuf_tensor(un + "idf", [128, 128], F32))
        sot = [bb.slot("SCdot%d" % i) for i in range(2)]
        sid = bb.slot("SCdid")

        with nc.Block() as blk:
            @blk.sync
            def _(g):
                sid.put(g, idf[:, :], bb.io["c_identf"][:, :])
                sid.wait(g)

        sm = "OTm"
        bb.sem(sm)
        bm0 = bb.cnt[sm]

        def pre_g(g, m):
            if m >= 2:
                g.wait_ge(bb.sem(sm), bm0 + m - 1)
            sot[m % 2].put(g, otile[m % 2][:, :, :],
                           acc_scr[:, :, m * 128:(m + 1) * 128
                                   ].rearrange("d p f -> p d f"))

        def mm_waits(t, m):
            sot[m % 2].wait_rel(t, 16 * (m // 2 + 1))

        def emit_mm(t, m, psum):
            for dt in range(ND):
                tp = t.transpose(psum[:, dt * 128:(dt + 1) * 128],
                                 otile[m % 2][:, dt, :], idf[:, :])
            return tp

        emit_out_ln_residual(bb, x_in, x_out, pfx, emit_mm, mm_waits,
                             pre_g=pre_g)


def emit_l1_mix(bb, x_in, x_out, pfx="m1"):
    nc = bb.nc
    io = bb.io
    xt = io[pfx + "_xt"]
    emit_preln_xt(bb, x_in, xt, pfx + "A")

    with ExitStack() as st0:
        e0 = st0.enter_context
        vsb = e0(nc.sbuf_tensor(pfx + "vsb", [128, S], BF16))
        onesr = e0(nc.sbuf_tensor(pfx + "onesr", [1, 128], BF16))
        onesS = e0(nc.sbuf_tensor(pfx + "onesS", [1, S], BF16))
        bvb = e0(nc.sbuf_tensor(pfx + "bvb", [1, S], BF16))
        bob = e0(nc.sbuf_tensor(pfx + "bob", [1, S], BF16))

        with ExitStack() as st:
            e = st.enter_context
            wvb = e(nc.sbuf_tensor(pfx + "wvb", [128, ND, D], BF16))
            wvs32 = e(nc.sbuf_tensor(pfx + "wvs32", [128, ND, 128], F32))
            wvsum = e(nc.sbuf_tensor(pfx + "wvsum", [128, ND, 128], BF16))
            bvs32 = e(nc.sbuf_tensor(pfx + "bvs32", [1, 128], F32))
            bvsum = e(nc.sbuf_tensor(pfx + "bvsum", [1, 128], BF16))
            xrow = [e(nc.sbuf_tensor(f"{pfx}xr{i}", [128, S], BF16))
                    for i in range(2)]
            pvs = e(nc.psum_tensor(pfx + "pvs", [128, S], F32))
            swv = bb.slot("M1dwv")
            sxr = [bb.slot("M1dxr%d" % i) for i in range(2)]
            sm, sv = "BVm", "BVv"
            bb.sem(sm), bb.sem(sv)
            bm, bv_ = bb.cnt[sm], bb.cnt[sv]

            with nc.Block() as blk:
                @blk.gpsimd
                def _(g):
                    swv.put(g, onesr[:, :], io["c_ones_row"][:, :])
                    swv.put(g, onesS[:, :], io["c_ones_rowS"][:, :])
                    swv.put(g, bvb[:, :], io[pfx + "_bv"][:, :])
                    swv.put(g, bob[:, :], io[pfx + "_bo"][:, :])
                    for k in range(ND):
                        swv.put(g, wvb[:, k, :],
                                io[pfx + "_wvT"][k * 128:(k + 1) * 128, :])

            with nc.Block() as blk:
                @blk.vector
                def _(v):
                    swv.wait(v)
                    for k in range(ND):
                        v.tensor_tensor(wvs32[:, k, :], wvb[:, k, 0:128],
                                        wvb[:, k, 128:256], op=ALU.add)
                    v.drain()
                    for kh in range(2, NH):
                        for k in range(ND):
                            v.tensor_tensor(wvs32[:, k, :], wvs32[:, k, :],
                                            wvb[:, k, kh * 128:(kh + 1) * 128],
                                            op=ALU.add)
                        v.drain()
                    for k in range(ND):
                        v.tensor_copy(wvsum[:, k, :], wvs32[:, k, :])
                    v.tensor_tensor(bvs32[:, :], bvb[0:1, 0:128],
                                    bvb[0:1, 128:256], op=ALU.add)
                    v.drain()
                    for kh in range(2, NH):
                        v.tensor_tensor(bvs32[:, :], bvs32[:, :],
                                        bvb[0:1, kh * 128:(kh + 1) * 128],
                                        op=ALU.add)
                        v.drain()
                    v.tensor_copy(bvsum[:, :], bvs32[:, :])

            with nc.Block() as blk:
                @blk.sync
                def _(g):
                    for k in range(ND):
                        if k >= 2:
                            g.wait_ge(bb.sem(sm), bm + k - 1)
                        sxr[k % 2].put(g, xrow[k % 2][:, :],
                                       xt[k].rearrange("t p f -> p t f"))

                @blk.tensor
                def _(t):
                    for k in range(ND):
                        sxr[k % 2].wait_rel(t, 16 * (k // 2 + 1))
                        for nb in range(NB):
                            mm = t.matmul(pvs[:, nb * 512:(nb + 1) * 512],
                                          wvsum[:, k, :],
                                          xrow[k % 2][:, nb * 512:(nb + 1) * 512],
                                          start=(k == 0), stop=False,
                                          skip_group_check=True)
                        mm.then_inc(bb.sem(sm), 1)
                        bb.inc(sm)
                    for nb in range(NB):
                        mm = t.matmul(pvs[:, nb * 512:(nb + 1) * 512],
                                      bvsum[0:1, :],
                                      onesS[0:1, nb * 512:(nb + 1) * 512],
                                      start=False, stop=(nb == NB - 1),
                                      skip_group_check=True)
                    mm.then_inc(bb.sem(sm), 1)
                    bb.inc(sm)

                @blk.scalar
                def _(s):
                    s.wait_ge(bb.sem(sm), bm + ND + 1)
                    s.copy(vsb[:, :], pvs[:, :]).then_inc(bb.sem(sv), 1)
                    bb.inc(sv)

        with ExitStack() as st:
            e = st.enter_context
            wob = e(nc.sbuf_tensor(pfx + "wob", [128, ND, D], BF16))
            wos32 = e(nc.sbuf_tensor(pfx + "wos32", [128, D], F32))
            wosum = e(nc.sbuf_tensor(pfx + "wosum", [128, D], BF16))
            swo = bb.slot("M1dwo")
            with nc.Block() as blk:
                @blk.gpsimd
                def _(g):
                    for k in range(ND):
                        swo.put(g, wob[:, k, :],
                                io[pfx + "_woT"][k * 128:(k + 1) * 128, :])
                    swo.wait(g)

            with nc.Block() as blk:
                @blk.vector
                def _(v):
                    v.tensor_tensor(wos32[:, :], wob[:, 0, :], wob[:, 1, :],
                                    op=ALU.add)
                    v.drain()
                    for qh in range(2, NH):
                        v.tensor_tensor(wos32[:, :], wos32[:, :], wob[:, qh, :],
                                        op=ALU.add)
                        v.drain()
                    v.tensor_copy(wosum[:, :], wos32[:, :])

            def emit_mm(t, m, psum):
                for nb in range(NB):
                    t.matmul(psum[:, nb * 512:(nb + 1) * 512],
                             vsb[:, m * 128:(m + 1) * 128],
                             wosum[:, nb * 512:(nb + 1) * 512],
                             start=True, stop=False, skip_group_check=True)
                for nb in range(NB):
                    mm = t.matmul(psum[:, nb * 512:(nb + 1) * 512],
                                  onesr[0:1, 0:128],
                                  bob[0:1, nb * 512:(nb + 1) * 512],
                                  start=False, stop=(nb == NB - 1),
                                  skip_group_check=True)
                return mm

            emit_out_ln_residual(bb, x_in, x_out, pfx + "E", emit_mm)


def emit_l1_ffn(bb, x_in, x_out, pfx="f1"):
    nc = bb.nc
    io = bb.io
    xt = io[pfx + "_xt"]
    acc = io[pfx + "_acc"]
    NHM = HID1 // 128
    TC = 1024
    NCHUNK = S // TC

    emit_preln_xt(bb, x_in, xt, pfx + "A")
    _l1_w2_fold(bb, pfx, NHM)

    with ExitStack() as st0:
        e0 = st0.enter_context
        h128b = e0(nc.sbuf_tensor(pfx + "h128", [128, 128], BF16))
        hA = e0(nc.sbuf_tensor(pfx + "hA", [128, NHM, TC], BF16))
        sic = bb.slot("F1dic")
        with nc.Block() as blk:
            @blk.sync
            def _(g):
                sic.put(g, h128b[:, :], io["c_h128"][:, :])
                sic.wait(g)

        for chunk in range(NCHUNK):
            _l1_ffn_chunk(bb, chunk, xt, acc, h128b, hA, TC, NHM, pfx)

    emit_scr_out_phase(bb, x_in, x_out, acc, pfx + "E")


def _l1_w2_fold(bb, pfx, NHM):
    nc = bb.nc
    io = bb.io
    un = pfx + bb.u()
    w2f = io[pfx + "_w2f"]
    SC = 256
    NS = D // SC

    def halves(s_):
        Dd = 1 << s_
        pairs = [a for a in range(NHM) if not (a & Dd)]
        return Dd, pairs[:len(pairs) // 2], pairs[len(pairs) // 2:]

    with ExitStack() as st:
        e = st.enter_context
        wA = [e(nc.sbuf_tensor(f"{un}wA{i}", [128, NHM, SC], BF16))
              for i in range(2)]
        wB = e(nc.sbuf_tensor(un + "wB", [128, NHM, SC], BF16))
        sld = [bb.slot("F1dfl%d" % i) for i in range(2)]
        sst = [bb.slot("F1dfs%d" % i) for i in range(2)]
        smf, svf = "BWFm", "BWFv"
        bb.sem(smf), bb.sem(svf)
        bmf, bvf = bb.cnt[smf], bb.cnt[svf]

        with nc.Block() as blk:
            @blk.gpsimd
            def _(g):
                for i in range(2):
                    sld[i].put(g, wA[i][:, :, :],
                               io[pfx + "_w2T"][:, i * SC:(i + 1) * SC
                                                ].rearrange("(kt p) n -> p kt n",
                                                            p=128))
                for s in range(NS):
                    sld[s % 2].wait_rel(g, 16 * (s // 2 + 1))
                    for k6 in range(6):
                        G = 6 * s + k6
                        if G > 0:
                            g.wait_ge(bb.sem(svf), bvf + G)
                        Dd, _, mine = halves(k6)
                        c, n = ((wA[s % 2], wB) if k6 % 2 == 0
                                else (wB, wA[s % 2]))
                        for a in mine:
                            b = a + Dd
                            g.tensor_tensor(n[:, a, :], c[:, a, :], c[:, b, :],
                                            op=ALU.add)
                            g.tensor_tensor(n[:, b, :], c[:, a, :], c[:, b, :],
                                            op=ALU.subtract)
                        g.drain()
                        g.nop().then_inc(bb.sem(smf), 1)
                        bb.inc(smf)
                    if s + 2 < NS:
                        sst[s % 2].wait_rel(g, 16 * (s // 2 + 1))
                        sld[s % 2].put(g, wA[s % 2][:, :, :],
                                       io[pfx + "_w2T"][:, (s + 2) * SC:
                                                        (s + 3) * SC
                                                        ].rearrange(
                                                            "(kt p) n -> p kt n",
                                                            p=128))

            @blk.vector
            def _(v):
                for s in range(NS):
                    sld[s % 2].wait_rel(v, 16 * (s // 2 + 1))
                    for k6 in range(6):
                        G = 6 * s + k6
                        if G > 0:
                            v.wait_ge(bb.sem(smf), bmf + G)
                        Dd, mine, _ = halves(k6)
                        c, n = ((wA[s % 2], wB) if k6 % 2 == 0
                                else (wB, wA[s % 2]))
                        for a in mine:
                            b = a + Dd
                            v.tensor_tensor(n[:, a, :], c[:, a, :], c[:, b, :],
                                            op=ALU.add)
                            v.tensor_tensor(n[:, b, :], c[:, a, :], c[:, b, :],
                                            op=ALU.subtract)
                        v.drain()
                        v.nop().then_inc(bb.sem(svf), 1)
                        bb.inc(svf)

            @blk.sync
            def _(g):
                for s in range(NS):
                    g.wait_ge(bb.sem(smf), bmf + 6 * (s + 1))
                    g.wait_ge(bb.sem(svf), bvf + 6 * (s + 1))
                    sst[s % 2].put(g, w2f[:, s * SC:(s + 1) * SC].rearrange(
                        "(kt p) n -> p kt n", p=128), wA[s % 2][:, :, :])
                sst[0].wait(g)
                sst[1].wait(g)


def _l1_ffn_chunk(bb, chunk, xt, acc, h128b, hA, TC, NHM, pfx):
    nc = bb.nc
    io = bb.io
    un = pfx + bb.u()
    NH5 = TC // 512

    with ExitStack() as st:
        e = st.enter_context
        xtc = e(nc.sbuf_tensor(un + "xtc", [128, ND, TC], BF16))
        w1c = [e(nc.sbuf_tensor(f"{un}w1c{i}", [128, ND, 128], BF16))
               for i in range(2)]
        w3c = [e(nc.sbuf_tensor(f"{un}w3c{i}", [128, ND, 128], BF16))
               for i in range(2)]
        s1b = [e(nc.sbuf_tensor(f"{un}s1b{i}", [128, TC], BF16))
               for i in range(2)]
        pu = e(nc.psum_tensor(un + "pu", [128, 2, 2, TC], F32))
        sxc = bb.slot("F1dxc")
        sw1 = [bb.slot("F1dw1%d" % i) for i in range(2)]
        sw3 = [bb.slot("F1dw3%d" % i) for i in range(2)]
        sm, ss, sv = "BUm", "BUs", "BUv"
        for _x in (sm, ss, sv):
            bb.sem(_x)
        bm, bs, bv = bb.cnt[sm], bb.cnt[ss], bb.cnt[sv]

        with nc.Block() as blk:
            @blk.sync
            def _(g):
                for t2 in range(TC // 256):
                    sxc.put(g, xtc[:, :, t2 * 256:(t2 + 1) * 256],
                            xt[:, chunk * (TC // 256) + t2].rearrange(
                                "k p f -> p k f"))

            @blk.gpsimd
            def _(g):
                for hm in range(NHM):
                    if hm >= 2:
                        g.wait_ge(bb.sem(sm), bm + 2 * (hm - 1))
                    i = hm % 2
                    sw1[i].put(g, w1c[i][:, :, :],
                               io[pfx + "_w1T"][:, hm * 128:(hm + 1) * 128
                                                ].rearrange("(kt p) n -> p kt n",
                                                            p=128))
                    sw3[i].put(g, w3c[i][:, :, :],
                               io[pfx + "_w3T"][:, hm * 128:(hm + 1) * 128
                                                ].rearrange("(kt p) n -> p kt n",
                                                            p=128))

            @blk.tensor
            def _(t):
                sxc.wait(t)
                for hm in range(NHM):
                    i = hm % 2
                    sw1[i].wait_rel(t, 16 * (hm // 2 + 1))
                    sw3[i].wait_rel(t, 16 * (hm // 2 + 1))
                    if hm >= 2:
                        t.wait_ge(bb.sem(sv), bv + hm - 1)
                    for h5 in range(NH5):
                        for k in range(ND):
                            mm = t.matmul(
                                pu[:, i, 0, h5 * 512:(h5 + 1) * 512],
                                w1c[i][:, k, :],
                                xtc[:, k, h5 * 512:(h5 + 1) * 512],
                                start=(k == 0), stop=(k == ND - 1))
                    mm.then_inc(bb.sem(sm), 1)
                    bb.inc(sm)
                    for h5 in range(NH5):
                        for k in range(ND):
                            mm = t.matmul(
                                pu[:, i, 1, h5 * 512:(h5 + 1) * 512],
                                w3c[i][:, k, :],
                                xtc[:, k, h5 * 512:(h5 + 1) * 512],
                                start=(k == 0), stop=(k == ND - 1))
                    mm.then_inc(bb.sem(sm), 1)
                    bb.inc(sm)

            @blk.scalar
            def _(s):
                for hm in range(NHM):
                    i = hm % 2
                    s.wait_ge(bb.sem(sm), bm + 2 * hm + 1)
                    s.activation(s1b[i][:, :], pu[:, i, 0, :], AF.Silu
                                 ).then_inc(bb.sem(ss), 1)
                    bb.inc(ss)

            @blk.vector
            def _(v):
                for hm in range(NHM):
                    i = hm % 2
                    v.wait_ge(bb.sem(ss), bs + hm + 1)
                    v.wait_ge(bb.sem(sm), bm + 2 * hm + 2)
                    v.tensor_tensor(hA[:, hm, :], s1b[i][:, :], pu[:, i, 1, :],
                                    op=ALU.mult).then_inc(bb.sem(sv), 1)
                    bb.inc(sv)

    with ExitStack() as st:
        pf = st.enter_context(nc.psum_tensor(un + "pf", [128, 2, TC], F32))
        sm, ss = "BFm", "BFs"
        bb.sem(sm), bb.sem(ss)
        bm, bs = bb.cnt[sm], bb.cnt[ss]
        with nc.Block() as blk:
            @blk.tensor
            def _(t):
                for pm in range(NHM):
                    if pm >= 2:
                        t.wait_ge(bb.sem(ss), bs + pm - 1)
                    for h5 in range(NH5):
                        mm = t.matmul(pf[:, pm % 2, h5 * 512:(h5 + 1) * 512],
                                      h128b[:, :],
                                      hA[:, pm, h5 * 512:(h5 + 1) * 512],
                                      start=True, stop=True)
                    mm.then_inc(bb.sem(sm), 1)
                    bb.inc(sm)

            @blk.scalar
            def _(s):
                for pm in range(NHM):
                    s.wait_ge(bb.sem(sm), bm + pm + 1)
                    s.copy(hA[:, pm, :], pf[:, pm % 2, :]
                           ).then_inc(bb.sem(ss), 1)
                    bb.inc(ss)

    with ExitStack() as st:
        e = st.enter_context
        w2c = [e(nc.sbuf_tensor(f"{un}w2c{i}", [128, NHM, 128], BF16))
               for i in range(2)]
        orow = [e(nc.sbuf_tensor(f"{un}or{i}", [128, TC], F32))
                for i in range(2)]
        pd = e(nc.psum_tensor(un + "pd", [128, 2, TC], F32))
        sw2 = [bb.slot("F1dw2%d" % i) for i in range(2)]
        sor = [bb.slot("F1dor%d" % i) for i in range(2)]
        sm, ss = "BDm", "BDs"
        bb.sem(sm), bb.sem(ss)
        bm, bs = bb.cnt[sm], bb.cnt[ss]
        w2f = io[pfx + "_w2f"]

        with nc.Block() as blk:
            @blk.sync
            def _(g):
                for i in range(2):
                    sw2[i].put(g, w2c[i][:, :, :],
                               w2f[:, i * 128:(i + 1) * 128].rearrange(
                                   "(kt p) n -> p kt n", p=128))
                for md in range(ND):
                    i = md % 2
                    g.wait_ge(bb.sem(ss), bs + md + 1)
                    sor[i].put(g, acc[md, :, chunk * TC:(chunk + 1) * TC],
                               orow[i][:, :])
                    if md + 2 < ND:
                        g.wait_ge(bb.sem(sm), bm + md + 1)
                        sw2[i].put(g, w2c[i][:, :, :],
                                   w2f[:, (md + 2) * 128:(md + 3) * 128
                                       ].rearrange("(kt p) n -> p kt n",
                                                   p=128))
                sor[0].wait(g)
                sor[1].wait(g)

            @blk.tensor
            def _(t):
                for md in range(ND):
                    i = md % 2
                    sw2[i].wait_rel(t, 16 * (md // 2 + 1))
                    if md >= 2:
                        t.wait_ge(bb.sem(ss), bs + md - 1)
                    for h5 in range(NH5):
                        for k in range(NHM):
                            mm = t.matmul(pd[:, i, h5 * 512:(h5 + 1) * 512],
                                          w2c[i][:, k, :],
                                          hA[:, k, h5 * 512:(h5 + 1) * 512],
                                          start=(k == 0), stop=(k == NHM - 1))
                    mm.then_inc(bb.sem(sm), 1)
                    bb.inc(sm)

            @blk.scalar
            def _(s):
                for md in range(ND):
                    i = md % 2
                    s.wait_ge(bb.sem(sm), bm + md + 1)
                    if md >= 2:
                        sor[i].wait_rel(s, 16 * (md // 2))
                    s.copy(orow[i][:, :], pd[:, i, :]).then_inc(bb.sem(ss), 1)
                    bb.inc(ss)


def emit_l0_moe(bb, x_in, x_out, pfx="f0"):
    nc = bb.nc
    io = bb.io
    xt = io[pfx + "_xt"]
    acc = io[pfx + "_acc"]
    NHT = HID0 // 128

    emit_preln_xt(bb, x_in, xt, pfx + "A")

    with ExitStack() as st0:
        e0 = st0.enter_context
        gT = e0(nc.sbuf_tensor(pfx + "gT", [8, S], F32))
        onesf = e0(nc.sbuf_tensor(pfx + "onesf", [1, 128], F32))
        idf = e0(nc.sbuf_tensor(pfx + "idf", [128, 128], F32))
        id8 = e0(nc.sbuf_tensor(pfx + "id8", [8, 8], F32))
        sor = [bb.slot("F0dacc%d" % i) for i in range(2)]
        scn = bb.slot("F0dcn")
        with nc.Block() as blk:
            @blk.sync
            def _(g):
                scn.put(g, onesf[:, :], io["c_ones_row_f"][:, :])
                scn.put(g, idf[:, :], io["c_identf"][:, :])
                scn.put(g, id8[:, :], io["c_ident8f"][:, :])
                scn.wait(g)

        _moe_gate(bb, xt, gT, idf, pfx)

        for ex in range(NE + 1):
            _moe_expert(bb, ex, xt, gT, onesf, id8, acc, sor, pfx, NHT)

    emit_scr_out_phase(bb, x_in, x_out, acc, pfx + "E")


def _moe_gate(bb, xt, gT, idf, pfx):
    nc = bb.nc
    io = bb.io
    un = pfx + bb.u()
    with ExitStack() as st:
        e = st.enter_context
        gwb = e(nc.sbuf_tensor(un + "gwb", [128, ND, 8], BF16))
        ebias = e(nc.sbuf_tensor(un + "ebias", [128, 8], F32))
        xg = [e(nc.sbuf_tensor(f"{un}xg{i}", [128, ND, 128], BF16))
              for i in range(2)]
        ge = e(nc.sbuf_tensor(un + "ge", [128, 2, 8], F32))
        gst = e(nc.sbuf_tensor(un + "gst", [128, 2, 8], F32))
        gsc = e(nc.sbuf_tensor(un + "gsc", [128, 2, 8], F32))
        gpb = e(nc.sbuf_tensor(un + "gpb", [128, 2, 8], F32))
        gw2 = e(nc.sbuf_tensor(un + "gw2", [128, 2, 8], F32))
        gall = e(nc.sbuf_tensor(un + "gall", [128, NT, 8], F32))
        pg = e(nc.psum_tensor(un + "pg", [128, 2, 8], F32))
        pgt = e(nc.psum_tensor(un + "pgt", [8, 128], F32))
        sgw = bb.slot("F0dgw")
        sxg = [bb.slot("F0dxg%d" % i) for i in range(2)]
        sm, ss, sv = "BGm", "BGs", pfx + "Gv"
        for _x in (sm, ss, sv):
            bb.sem(_x)
        bm, bs, bv = bb.cnt[sm], bb.cnt[ss], bb.cnt[sv]

        with nc.Block() as blk:
            @blk.gpsimd
            def _(g):
                sgw.put(g, ebias[:, :], io[pfx + "_ebias"][:, :])
                sgw.put(g, gwb[:, :, :],
                        io[pfx + "_gate"].rearrange("(kt p) n -> p kt n", p=128))

            @blk.sync
            def _(g):
                for m in range(NT):
                    if m >= 2:
                        g.wait_ge(bb.sem(sm), bm + m - 1)
                    sxg[m % 2].put(
                        g, xg[m % 2][:, :, :],
                        xt[:, m // 2, :, (m % 2) * 128:(m % 2) * 128 + 128
                           ].rearrange("k p f -> p k f"))

            @blk.tensor
            def _(t):
                sgw.wait(t)
                for m in range(NT):
                    sxg[m % 2].wait_rel(t, 16 * (m // 2 + 1))
                    if m >= 2:
                        t.wait_ge(bb.sem(ss), bs + m - 1)
                    for k in range(ND):
                        mm = t.matmul(pg[:, m % 2, :], xg[m % 2][:, k, :],
                                      gwb[:, k, :],
                                      start=(k == 0), stop=(k == ND - 1))
                    mm.then_inc(bb.sem(sm), 1)
                    bb.inc(sm)

            @blk.scalar
            def _(s):
                for m in range(NT):
                    i = m % 2
                    s.wait_ge(bb.sem(sm), bm + m + 1)
                    if m >= 2:
                        s.wait_ge(bb.sem(sv), bv + m - 1)
                    s.activation(ge[:, i, :], pg[:, i, :], AF.Exp,
                                 accum_out=gst[:, i, 0:1]).then_inc(bb.sem(ss), 1)
                    bb.inc(ss)

            @blk.vector
            def _(v):
                for m in range(NT):
                    i = m % 2
                    v.wait_ge(bb.sem(ss), bs + m + 1)
                    v.reciprocal(gst[:, i, 1:2], gst[:, i, 0:1])
                    v.drain()
                    v.tensor_scalar(gsc[:, i, :], ge[:, i, :], gst[:, i, 1:2],
                                    None, op0=ALU.mult)
                    v.drain()
                    v.tensor_tensor(gpb[:, i, :], gsc[:, i, :], ebias[:, :],
                                    op=ALU.add)
                    v.drain()
                    v.reduce_max(gst[:, i, 2:3], gpb[:, i, :], axis=AX.X)
                    v.drain()
                    v.tensor_scalar(gw2[:, i, :], gpb[:, i, :], gst[:, i, 2:3],
                                    None, op0=ALU.is_equal)
                    v.drain()
                    v.scalar_tensor_tensor(gw2[:, i, :], gw2[:, i, :], -1e30,
                                           gpb[:, i, :], op0=ALU.mult,
                                           op1=ALU.add)
                    v.drain()
                    v.reduce_max(gst[:, i, 3:4], gw2[:, i, :], axis=AX.X)
                    v.drain()
                    v.tensor_scalar(gw2[:, i, :], gpb[:, i, :], gst[:, i, 3:4],
                                    None, op0=ALU.is_ge)
                    v.drain()
                    v.tensor_tensor(gall[:, m, :], gsc[:, i, :], gw2[:, i, :],
                                    op=ALU.mult).then_inc(bb.sem(sv), 1)
                    bb.inc(sv)

        bm, bv = bb.cnt[sm], bb.cnt[sv]
        with nc.Block() as blk:
            @blk.tensor
            def _(t):
                for m in range(NT):
                    if m >= 1:
                        t.wait_ge(bb.sem(sv), bv + m)
                    t.transpose(pgt[:, :], gall[:, m, :], idf[:, :]
                                ).then_inc(bb.sem(sm), 1)
                    bb.inc(sm)

            @blk.vector
            def _(v):
                for m in range(NT):
                    v.wait_ge(bb.sem(sm), bm + m + 1)
                    v.tensor_copy(gT[0:8, m * 128:(m + 1) * 128], pgt[0:8, :]
                                  ).then_inc(bb.sem(sv), 1)
                    bb.inc(sv)


def _moe_expert(bb, ex, xt, gT, onesf, id8, acc, sor, pfx, NHT):
    nc = bb.nc
    io = bb.io
    un = pfx + bb.u()
    shared = (ex == NE)

    def w1_ap(hm):
        if shared:
            return io[pfx + "_sw1"][:, hm * 128:(hm + 1) * 128]
        return io[pfx + "_w1"][ex, :, hm * 128:(hm + 1) * 128]

    def w3_ap(hm):
        if shared:
            return io[pfx + "_sw3"][:, hm * 128:(hm + 1) * 128]
        return io[pfx + "_w3"][ex, :, hm * 128:(hm + 1) * 128]

    def w2_ap(md):
        if shared:
            return io[pfx + "_sw2"][:, md * 128:(md + 1) * 128]
        return io[pfx + "_w2"][ex, :, md * 128:(md + 1) * 128]

    with ExitStack() as st:
        e = st.enter_context
        xeT = e(nc.sbuf_tensor(un + "xeT", [128, ND, S], BF16))
        heT = e(nc.sbuf_tensor(un + "heT", [128, NHT, S], BF16))

        with ExitStack() as stg:
            eg = stg.enter_context
            gb = eg(nc.sbuf_tensor(un + "gb", [128, S], F32))
            grow = eg(nc.sbuf_tensor(un + "grow", [1, S], F32))
            xrow = [eg(nc.sbuf_tensor(f"{un}xr{i}", [128, S], BF16))
                    for i in range(2)]
            pgb = eg(nc.psum_tensor(un + "pgb", [128, S], F32))
            sxr = [bb.slot("F0dxr%d" % i) for i in range(2)]
            sm, ss, sv = "BXm", "BXs", "BXv"
            for _x in (sm, ss, sv):
                bb.sem(_x)
            bm, bs, bv = bb.cnt[sm], bb.cnt[ss], bb.cnt[sv]

            if not shared:
                with nc.Block() as blk:
                    @blk.tensor
                    def _(t):
                        for nb in range(NB):
                            mm = t.matmul(pgb[0:1, nb * 512:(nb + 1) * 512],
                                          id8[:, ex:ex + 1],
                                          gT[0:8, nb * 512:(nb + 1) * 512],
                                          start=True, stop=True,
                                          skip_group_check=True)
                        mm.then_inc(bb.sem(sm), 1)
                        bb.inc(sm)

                    @blk.scalar
                    def _(s):
                        s.wait_ge(bb.sem(sm), bb.cnt[sm])
                        s.copy(grow[:, :], pgb[0:1, :]).then_inc(bb.sem(ss), 1)
                        bb.inc(ss)

                with nc.Block() as blk:
                    @blk.tensor
                    def _(t):
                        for nb in range(NB):
                            mm = t.matmul(pgb[:, nb * 512:(nb + 1) * 512],
                                          onesf[0:1, :],
                                          grow[0:1, nb * 512:(nb + 1) * 512],
                                          start=True, stop=True,
                                          skip_group_check=True)
                        mm.then_inc(bb.sem(sm), 1)
                        bb.inc(sm)

                    @blk.scalar
                    def _(s):
                        s.wait_ge(bb.sem(sm), bb.cnt[sm])
                        s.copy(gb[:, :], pgb[:, :]).then_inc(bb.sem(ss), 1)
                        bb.inc(ss)

                with nc.Block() as blk:
                    @blk.sync
                    def _(g):
                        for k in range(ND):
                            if k >= 2:
                                g.wait_ge(bb.sem(sv), bv + k - 1)
                            sxr[k % 2].put(
                                g, xrow[k % 2][:, :],
                                xt[k].rearrange("t p f -> p t f"))

                    @blk.vector
                    def _(v):
                        for k in range(ND):
                            sxr[k % 2].wait_rel(v, 16 * (k // 2 + 1))
                            v.tensor_tensor(xeT[:, k, :], xrow[k % 2][:, :],
                                            gb[:, :], op=ALU.mult
                                            ).then_inc(bb.sem(sv), 1)
                            bb.inc(sv)
            else:
                with nc.Block() as blk:
                    @blk.sync
                    def _(g):
                        for k in range(ND):
                            sxr[0].put(g, xeT[:, k, :],
                                       xt[k].rearrange("t p f -> p t f"))
                        sxr[0].wait(g)

        with ExitStack() as stu:
            eu = stu.enter_context
            w1c = [eu(nc.sbuf_tensor(f"{un}w1c{i}", [128, ND, 128], BF16))
                   for i in range(2)]
            w3c = [eu(nc.sbuf_tensor(f"{un}w3c{i}", [128, ND, 128], BF16))
                   for i in range(2)]
            s1b = [eu(nc.sbuf_tensor(f"{un}s1b{i}", [128, S], BF16))
                   for i in range(2)]
            pu = eu(nc.psum_tensor(un + "pu", [128, 2, S], F32))
            sw1 = [bb.slot("F0dw1%d" % i) for i in range(2)]
            sw3 = [bb.slot("F0dw3%d" % i) for i in range(2)]
            sm, ss, sv = "BUm", "BUs", "BUv"
            for _x in (sm, ss, sv):
                bb.sem(_x)
            bm, bs, bv = bb.cnt[sm], bb.cnt[ss], bb.cnt[sv]

            with nc.Block() as blk:
                @blk.gpsimd
                def _(g):
                    for hm in range(NHT):
                        if hm >= 2:
                            g.wait_ge(bb.sem(sm), bm + 2 * (hm - 1))
                        i = hm % 2
                        w1a, w3a = w1_ap(hm), w3_ap(hm)
                        sw1[i].put(g, w1c[i][:, :, :],
                                   w1a.rearrange("(kt p) n -> p kt n", p=128))
                        sw3[i].put(g, w3c[i][:, :, :],
                                   w3a.rearrange("(kt p) n -> p kt n", p=128))

                @blk.tensor
                def _(t):
                    for hm in range(NHT):
                        i = hm % 2
                        sw1[i].wait_rel(t, 16 * (hm // 2 + 1))
                        sw3[i].wait_rel(t, 16 * (hm // 2 + 1))
                        if hm >= 1:
                            t.wait_ge(bb.sem(sv), bv + hm)
                        for k in range(ND):
                            for nb in range(NB):
                                mm = t.matmul(
                                    pu[:, 0, nb * 512:(nb + 1) * 512],
                                    w1c[i][:, k, :],
                                    xeT[:, k, nb * 512:(nb + 1) * 512],
                                    start=(k == 0), stop=(k == ND - 1),
                                    skip_group_check=True)
                        mm.then_inc(bb.sem(sm), 1)
                        bb.inc(sm)
                        for k in range(ND):
                            for nb in range(NB):
                                mm = t.matmul(
                                    pu[:, 1, nb * 512:(nb + 1) * 512],
                                    w3c[i][:, k, :],
                                    xeT[:, k, nb * 512:(nb + 1) * 512],
                                    start=(k == 0), stop=(k == ND - 1),
                                    skip_group_check=True)
                        mm.then_inc(bb.sem(sm), 1)
                        bb.inc(sm)

                @blk.scalar
                def _(s):
                    for hm in range(NHT):
                        i = hm % 2
                        s.wait_ge(bb.sem(sm), bm + 2 * hm + 1)
                        if hm >= 2:
                            s.wait_ge(bb.sem(sv), bv + hm - 1)
                        s.activation(s1b[i][:, :], pu[:, 0, :], AF.Silu
                                     ).then_inc(bb.sem(ss), 1)
                        bb.inc(ss)

                @blk.vector
                def _(v):
                    for hm in range(NHT):
                        i = hm % 2
                        v.wait_ge(bb.sem(ss), bs + hm + 1)
                        v.wait_ge(bb.sem(sm), bm + 2 * hm + 2)
                        v.tensor_tensor(heT[:, hm, :], s1b[i][:, :],
                                        pu[:, 1, :], op=ALU.mult
                                        ).then_inc(bb.sem(sv), 1)
                        bb.inc(sv)

        with ExitStack() as std:
            ed = std.enter_context
            w2c = [ed(nc.sbuf_tensor(f"{un}w2c{i}", [128, NHT, 128], BF16))
                   for i in range(2)]
            orow = [ed(nc.sbuf_tensor(f"{un}or{i}", [128, S], F32))
                    for i in range(2)]
            pd = ed(nc.psum_tensor(un + "pd", [128, 2, S], F32))
            sw2 = [bb.slot("F0dw2%d" % i) for i in range(2)]
            sm, ss = "BDm", "BDs"
            bb.sem(sm), bb.sem(ss)
            bm, bs = bb.cnt[sm], bb.cnt[ss]
            acc_base = [sor[0].n, sor[1].n]

            with nc.Block() as blk:
                @blk.gpsimd
                def _(g):
                    sor[0].wait(g, acc_base[0])
                    sor[1].wait(g, acc_base[1])
                    for md in range(ND + 2):
                        if md < ND:
                            if md >= 2:
                                g.wait_ge(bb.sem(sm), bm + md - 1)
                            i = md % 2
                            w2a = w2_ap(md)
                            sw2[i].put(g, w2c[i][:, :, :],
                                       w2a.rearrange("(kt p) n -> p kt n",
                                                     p=128))
                        if md >= 2:
                            md_ = md - 2
                            g.wait_ge(bb.sem(ss), bs + md_ + 1)
                            i = md_ % 2
                            if ex == 0:
                                sor[i].put(g, acc[md_], orow[i][:, :])
                            else:
                                sor[i].put(g, acc[md_], orow[i][:, :],
                                           accum_op=ALU.add)
                    sor[0].wait(g)
                    sor[1].wait(g)

                @blk.tensor
                def _(t):
                    for md in range(ND):
                        i = md % 2
                        sw2[i].wait_rel(t, 16 * (md // 2 + 1))
                        if md >= 2:
                            t.wait_ge(bb.sem(ss), bs + md - 1)
                        for k in range(NHT):
                            for nb in range(NB):
                                mm = t.matmul(
                                    pd[:, i, nb * 512:(nb + 1) * 512],
                                    w2c[i][:, k, :],
                                    heT[:, k, nb * 512:(nb + 1) * 512],
                                    start=(k == 0), stop=(k == NHT - 1),
                                    skip_group_check=True)
                        mm.then_inc(bb.sem(sm), 1)
                        bb.inc(sm)

                @blk.scalar
                def _(s):
                    for md in range(ND):
                        i = md % 2
                        s.wait_ge(bb.sem(sm), bm + md + 1)
                        if md >= 2:
                            sor[i].wait(s, acc_base[i] + 16 * (md // 2))
                        s.copy(orow[i][:, :], pd[:, i, :]).then_inc(bb.sem(ss), 1)
                        bb.inc(ss)


def _prep_all(inputs):
    perm = _rope_perm()
    w = _prep_l0_attn(inputs, perm)
    f32 = lambda k: np.asarray(inputs[k], np.float32)
    w["f0_gate"] = np.ascontiguousarray(f32("L0_ffn_gate_w"))
    w["f0_w1"] = np.ascontiguousarray(f32("L0_ffn_w1"))
    w["f0_w2"] = np.ascontiguousarray(f32("L0_ffn_w2"))
    w["f0_w3"] = np.ascontiguousarray(f32("L0_ffn_w3"))
    w["f0_sw1"] = np.ascontiguousarray(f32("L0_ffn_sw1"))
    w["f0_sw2"] = np.ascontiguousarray(f32("L0_ffn_sw2"))
    w["f0_sw3"] = np.ascontiguousarray(f32("L0_ffn_sw3"))
    w["f0_ebias"] = np.ascontiguousarray(
        np.broadcast_to(f32("L0_ffn_expert_bias")[None, :], (128, NE)))
    w["m1_wvT"] = np.ascontiguousarray(f32("L1_attn_Wv").T)
    w["m1_woT"] = np.ascontiguousarray(f32("L1_attn_Wo").T)
    w["m1_bv"] = np.ascontiguousarray(f32("L1_attn_bv")[None, :])
    w["m1_bo"] = np.ascontiguousarray(f32("L1_attn_bo")[None, :])
    w["f1_w1T"] = np.ascontiguousarray(f32("L1_ffn_W1").T)
    w["f1_w3T"] = np.ascontiguousarray(f32("L1_ffn_W3").T)
    w["f1_w2T"] = np.ascontiguousarray(f32("L1_ffn_W2").T)
    return w


_SHAPES = {
    "a0_wqT": [D, D], "a0_wkT": [D, D], "a0_wvT": [D, D], "a0_woT": [D, D],
    "f0_gate": [D, NE], "f0_w1": [NE, D, HID0], "f0_w2": [NE, HID0, D],
    "f0_w3": [NE, D, HID0], "f0_sw1": [D, HID0], "f0_sw2": [HID0, D],
    "f0_sw3": [D, HID0], "f0_ebias": [128, NE],
    "m1_wvT": [D, D], "m1_woT": [D, D], "m1_bv": [1, D], "m1_bo": [1, D],
    "f1_w1T": [D, HID1], "f1_w3T": [D, HID1], "f1_w2T": [HID1, D],
}

_BLOCK_WEIGHTS = {
    "a0": ["a0_wqT", "a0_wkT", "a0_wvT", "a0_woT"],
    "f0": ["f0_gate", "f0_w1", "f0_w2", "f0_w3", "f0_sw1", "f0_sw2", "f0_sw3",
           "f0_ebias"],
    "m1": ["m1_wvT", "m1_woT", "m1_bv", "m1_bo"],
    "f1": ["f1_w1T", "f1_w3T", "f1_w2T"],
}

_EMITTERS = {"a0": emit_l0_attn, "f0": emit_l0_moe, "m1": emit_l1_mix,
             "f1": emit_l1_ffn}


def _declare_block_io(nc, bb, pfx):
    for nm in _BLOCK_WEIGHTS[pfx]:
        bb.io[nm] = nc.dram_tensor(nm, _SHAPES[nm], F32,
                                   kind="ExternalInput").ap()
    bb.io[pfx + "_xt"] = nc.dram_tensor(pfx + "_xt", [ND, NT2, 128, 256], BF16,
                                        kind="Internal").ap()
    if pfx == "a0":
        bb.io["a0_vscr"] = nc.dram_tensor("a0_vscr", [NH, NT, 128, 128], BF16,
                                          kind="Internal").ap()
    if pfx in ("f0", "f1"):
        bb.io[pfx + "_acc"] = nc.dram_tensor(pfx + "_acc", [ND, 128, S], F32,
                                             kind="Internal").ap()
    if pfx == "f1":
        bb.io["f1_w2f"] = nc.dram_tensor("f1_w2f", [HID1, D], BF16,
                                         kind="Internal").ap()


def build_block(pfx):
    nc = bass.Bass("TRN2", target_bir_lowering=False)
    with ExitStack() as ctx:
        bb = BB(nc, ctx)
        x_in = nc.dram_tensor("x", [S, D], F32, kind="ExternalInput").ap()
        x_out = nc.dram_tensor("y", [S, D], F32, kind="ExternalOutput").ap()
        _declare_common(nc, bb)
        _declare_block_io(nc, bb, pfx)
        _EMITTERS[pfx](bb, x_in, x_out, pfx)
    return nc


_NC_CACHE = {}


def kernel_unfused(**inputs):
    x = np.asarray(inputs["L0_attn_x"], np.float32)
    ncores = x.shape[0]
    wmap = _prep_all(inputs)
    cmap = {"c_" + k: v for k, v in _consts().items()}

    cur = [np.ascontiguousarray(x[c]) for c in range(ncores)]
    for pfx in ["a0", "f0", "m1", "f1"]:
        if pfx not in _NC_CACHE:
            _NC_CACHE[pfx] = build_block(pfx)
        nc = _NC_CACHE[pfx]
        in_maps = []
        for c in range(ncores):
            m = {"x": cur[c]}
            for nm in _BLOCK_WEIGHTS[pfx]:
                m[nm] = wmap[nm]
            m.update(cmap)
            in_maps.append(m)
        res = run_bass_kernel_spmd(nc, in_maps, core_ids=list(range(ncores)))
        cur = [np.asarray(res.results[c]["y"]) for c in range(ncores)]
    return np.stack(cur).astype(np.float32)


def build_fused():
    nc = bass.Bass("TRN2", target_bir_lowering=False)
    with ExitStack() as ctx:
        bb = BB(nc, ctx)
        x_in = nc.dram_tensor("x", [S, D], F32, kind="ExternalInput").ap()
        y = nc.dram_tensor("y", [S, D], F32, kind="ExternalOutput").ap()
        x1 = nc.dram_tensor("x1", [S, D], F32, kind="Internal").ap()
        x2 = nc.dram_tensor("x2", [S, D], F32, kind="Internal").ap()
        x3 = nc.dram_tensor("x3", [S, D], F32, kind="Internal").ap()
        _declare_common(nc, bb)
        for pfx in ("a0", "f0", "m1", "f1"):
            _declare_block_io(nc, bb, pfx)
        emit_l0_attn(bb, x_in, x1, "a0")
        emit_l0_moe(bb, x1, x2, "f0")
        emit_l1_mix(bb, x2, x3, "m1")
        emit_l1_ffn(bb, x3, y, "f1")
    return nc


def kernel(**inputs):
    x = np.asarray(inputs["L0_attn_x"], np.float32)
    ncores = x.shape[0]
    wmap = _prep_all(inputs)
    cmap = {"c_" + k: v for k, v in _consts().items()}

    if "fused" not in _NC_CACHE:
        _NC_CACHE["fused"] = build_fused()
    nc = _NC_CACHE["fused"]
    in_maps = []
    for c in range(ncores):
        m = {"x": np.ascontiguousarray(x[c])}
        m.update(wmap)
        m.update(cmap)
        in_maps.append(m)
    res = run_bass_kernel_spmd(nc, in_maps, core_ids=list(range(ncores)))
    return np.stack([np.asarray(res.results[c]["y"]) for c in range(ncores)]
                    ).astype(np.float32)


# version 06
# speedup vs baseline: 1.7700x; 1.0142x over previous
import numpy as np
import ml_dtypes
from contextlib import ExitStack

import concourse.bass as bass
import concourse.mybir as mybir
from concourse.bass_utils import run_bass_kernel_spmd

F32 = mybir.dt.float32
BF16 = mybir.dt.bfloat16
AF = mybir.ActivationFunctionType
ALU = mybir.AluOpType
AX = mybir.AxisListType

S = 2048
D = 2048
NH = 16
DH = 128
RATIO = 16
NCMP = S // RATIO
WIN = 256
SCALE = DH ** -0.5
HID0 = 1792
NE = 8
HID1 = 8192
EPS = 1e-5
NT = S // 128
ND = D // 128
NT2 = S // 256
NB = D // 512

_BF = ml_dtypes.bfloat16
_CONST_CACHE = {}


def _consts():
    if _CONST_CACHE:
        return _CONST_CACHE
    c = {}
    c["ident"] = np.eye(128, dtype=_BF)
    c["identf"] = np.eye(128, dtype=np.float32)
    c["epsf"] = np.full((128, 1), EPS, dtype=np.float32)
    c["ones_rowS"] = np.ones((1, S), dtype=_BF)
    c["ones_row_f"] = np.ones((1, 128), dtype=np.float32)
    c["ident8f"] = np.eye(8, dtype=np.float32)
    c["ones_col"] = np.ones((128, 1), dtype=_BF)
    c["ones_row"] = np.ones((1, 128), dtype=_BF)
    A = np.zeros((S, NCMP), dtype=np.float32)
    A[np.arange(S), np.arange(S) // RATIO] = 1.0 / RATIO
    c["avg"] = A.astype(_BF)
    inv = 1.0 / (10000.0 ** (np.arange(0, DH, 2, dtype=np.float64) / DH))
    ang = np.outer(inv, np.arange(S, dtype=np.float64))
    cos = np.cos(ang).astype(np.float32)
    sin = np.sin(ang).astype(np.float32)
    c["cs1"] = np.concatenate([cos, sin], axis=0).astype(_BF)
    c["cs2"] = np.concatenate([sin, cos], axis=0).astype(_BF)
    i64 = np.eye(64, dtype=_BF)
    c["rl_e"] = np.concatenate([i64, -i64], axis=0)
    c["rl_o"] = np.concatenate([i64, i64], axis=0)
    p = np.arange(128)[:, None]
    f = np.arange(128)[None, :]
    c["m_w0"] = (f >= p).astype(_BF)
    c["m_w2"] = (f < p).astype(_BF)
    c["m_ones"] = np.ones((128, 128), dtype=_BF)
    c["m_zero"] = np.zeros((128, 128), dtype=_BF)
    v = np.arange(248)[:, None] - 120
    fq = np.arange(128)[None, :]
    c["m_cmp"] = (v < (fq + 1) // 16).astype(_BF)
    i = np.arange(128)
    pc = np.zeros((128, 128), dtype=np.int64)
    for b in range(7):
        pc += ((i[:, None] >> b) & 1) * ((i[None, :] >> b) & 1)
    c["h128"] = (np.where(pc % 2 == 0, 1.0, -1.0) / np.sqrt(8192.0)).astype(_BF)
    i6 = np.arange(64)
    pc6 = np.zeros((64, 64), dtype=np.int64)
    for b in range(6):
        pc6 += ((i6[:, None] >> b) & 1) * ((i6[None, :] >> b) & 1)
    h64 = np.where(pc6 % 2 == 0, 1.0, -1.0)
    h64d = np.zeros((128, 128), dtype=np.float64)
    h64d[:64, :64] = h64
    h64d[64:, 64:] = h64
    c["h64d"] = h64d.astype(_BF)
    _CONST_CACHE.update(c)
    return c


def _rope_perm():
    perm = np.zeros(D, dtype=np.int64)
    for h in range(NH):
        b = h * DH
        perm[b:b + 64] = b + 2 * np.arange(64)
        perm[b + 64:b + 128] = b + 2 * np.arange(64) + 1
    return perm


class Slot:

    def __init__(self, bb, name):
        self.sem = bb.ctx.enter_context(bb.nc.semaphore(name))
        self.n = 0
        self.base = 0

    def rebase(self):
        self.base = self.n
        return self

    def put(self, eng, out, in_, **kw):
        eng.dma_start(out, in_, **kw).then_inc(self.sem, 16)
        self.n += 16
        return self.n

    def wait(self, eng, val=None):
        eng.wait_ge(self.sem, self.n if val is None else val)

    def wait_rel(self, eng, rel):
        eng.wait_ge(self.sem, self.base + rel)


class BB:
    def __init__(self, nc, ctx):
        self.nc = nc
        self.ctx = ctx
        self.sems = {}
        self.cnt = {}
        self.io = {}
        self._n = 0

    def u(self):
        self._n += 1
        return "u%d" % self._n

    def sem(self, name):
        if name not in self.sems:
            self.sems[name] = self.ctx.enter_context(self.nc.semaphore(name))
            self.cnt[name] = 0
        return self.sems[name]

    def inc(self, name, by=1):
        self.cnt[name] += by
        return self.cnt[name]

    def slot(self, name):
        if name not in self.sems:
            self.sems[name] = Slot(self, name)
        return self.sems[name].rebase()


def emit_preln_xt(bb, x_in, xt_scr, pfx):
    nc = bb.nc
    ss, sm, sv = "PNs", "PNm", "PNv"
    for _s in (ss, sm, sv):
        bb.sem(_s)
    bs, bm, bv = bb.cnt[ss], bb.cnt[sm], bb.cnt[sv]
    with ExitStack() as st:
        e = st.enter_context
        xs = [e(nc.sbuf_tensor(f"{pfx}xs{i}", [128, D], F32)) for i in range(2)]
        sq = [e(nc.sbuf_tensor(f"{pfx}sq{i}", [128, 2, D], BF16)) for i in range(2)]
        xln = [e(nc.sbuf_tensor(f"{pfx}xln{i}", [128, D], BF16)) for i in range(2)]
        xtrow = e(nc.sbuf_tensor(f"{pfx}xtrow", [128, ND, 256], BF16))
        idsb = e(nc.sbuf_tensor(f"{pfx}id", [128, 128], BF16))
        epsb = e(nc.sbuf_tensor(f"{pfx}eps", [128, 1], F32))
        stt = e(nc.sbuf_tensor(f"{pfx}stt", [128, 2, 8], F32))
        ptp = e(nc.psum_tensor(f"{pfx}ptp", [128, 2, 32, 128], BF16))
        sx = [bb.slot("PNdx%d" % i) for i in range(2)]
        sxt = bb.slot("PNdt")
        scn = bb.slot("PNdc")

        with nc.Block() as blk:
            @blk.sync
            def _(g):
                scn.put(g, epsb[:, :], bb.io["c_epsf"][:, :])
                scn.put(g, idsb[:, :], bb.io["c_ident"][:, :])
                for m2 in range(NT2 + 1):
                    if m2 < NT2:
                        for sub in range(2):
                            m = 2 * m2 + sub
                            if m >= 2:
                                g.wait_ge(bb.sem(ss), bs + m - 1)
                            sx[m % 2].put(g, xs[m % 2][:, :],
                                          x_in[m * 128:(m + 1) * 128, :])
                    if m2 >= 1:
                        g.wait_ge(bb.sem(sv), bv + m2)
                        sxt.put(g, xt_scr[:, m2 - 1].rearrange("d p f -> p d f"),
                                xtrow[:, :, :])
                sxt.wait(g)

            @blk.scalar
            def _(s):
                scn.wait_rel(s, 32)
                for m in range(NT):
                    i = m % 2
                    s1, s2 = stt[:, i, 0:1], stt[:, i, 1:2]
                    mean, s2d = stt[:, i, 2:3], stt[:, i, 3:4]
                    tvd, var = stt[:, i, 4:5], stt[:, i, 5:6]
                    lnv, rstd = stt[:, i, 6:7], stt[:, i, 7:8]
                    sx[i].wait_rel(s, 16 * (m // 2 + 1))
                    s.activation(sq[i][:, 0, :], xs[i][:, :], AF.Copy,
                                 accum_out=s1)
                    s.activation(sq[i][:, 1, :], xs[i][:, :], AF.Square,
                                 accum_out=s2)
                    s.drain()
                    s.activation(mean, s1, AF.Copy, scale=1.0 / D)
                    s.activation(s2d, s2, AF.Copy, scale=1.0 / D)
                    s.drain()
                    s.activation(tvd, mean, AF.Square)
                    s.drain()
                    s.activation(var, tvd, AF.Identity, scale=-1.0, bias=s2d)
                    s.drain()
                    s.activation(lnv, var, AF.Ln, bias=epsb[:, :])
                    s.drain()
                    s.activation(rstd, lnv, AF.Exp, scale=-0.5)
                    s.drain()
                    s.activation(tvd, rstd, AF.Copy, scale=-1.0)
                    s.drain()
                    s.activation(s2d, mean, AF.Copy, scale=tvd)
                    s.drain()
                    if m >= 2:
                        s.wait_ge(bb.sem(sm), bm + m // 2)
                    s.activation(xln[i][:, :], xs[i][:, :], AF.Identity,
                                 scale=rstd, bias=s2d).then_inc(bb.sem(ss), 1)
                    bb.inc(ss)

            @blk.tensor
            def _(t):
                scn.wait_rel(t, 32)
                for m2 in range(NT2):
                    if m2 >= 2:
                        t.wait_ge(bb.sem(sv), bv + m2 - 1)
                    for sub in range(2):
                        m = 2 * m2 + sub
                        t.wait_ge(bb.sem(ss), bs + m + 1)
                        for dt in range(ND):
                            tp = t.transpose(ptp[:, m2 % 2, sub * ND + dt, :],
                                        xln[m % 2][:, dt * 128:(dt + 1) * 128],
                                        idsb[:, :])
                    tp.then_inc(bb.sem(sm), 1)
                    bb.inc(sm)

            @blk.vector
            def _(v):
                for m2 in range(NT2):
                    v.wait_ge(bb.sem(sm), bm + m2 + 1)
                    if m2 >= 1:
                        sxt.wait_rel(v, 16 * m2)
                    for sub in range(2):
                        for dt in range(ND):
                            cp = v.tensor_copy(
                                xtrow[:, dt, sub * 128:(sub + 1) * 128],
                                ptp[:, m2 % 2, sub * ND + dt, :])
                    cp.then_inc(bb.sem(sv), 1)
                    bb.inc(sv)


def emit_out_ln_residual(bb, x_in, x_out, pfx, emit_mm, mm_waits=None,
                         pre_g=None):
    nc = bb.nc
    ss, sm, sv = "OTs", "OTm", "OTv"
    for _s in (ss, sm, sv):
        bb.sem(_s)
    bs, bm, bv = bb.cnt[ss], bb.cnt[sm], bb.cnt[sv]
    with ExitStack() as st:
        e = st.enter_context
        xs = [e(nc.sbuf_tensor(f"{pfx}xs{i}", [128, D], F32)) for i in range(2)]
        sq = [e(nc.sbuf_tensor(f"{pfx}sq{i}", [128, 2, D], BF16)) for i in range(2)]
        tmp = [e(nc.sbuf_tensor(f"{pfx}tmp{i}", [128, D], F32)) for i in range(2)]
        xo = [e(nc.sbuf_tensor(f"{pfx}xo{i}", [128, D], F32)) for i in range(2)]
        epsb = e(nc.sbuf_tensor(f"{pfx}eps", [128, 1], F32))
        stt = e(nc.sbuf_tensor(f"{pfx}stt", [128, 2, 8], F32))
        pE = e(nc.psum_tensor(f"{pfx}pE", [128, 2, D], F32))
        sxi = [bb.slot("OTdi%d" % i) for i in range(2)]
        sxo = [bb.slot("OTdo%d" % i) for i in range(2)]
        sce = bb.slot("OTdce")

        with nc.Block() as blk:
            @blk.sync
            def _(g):
                sce.put(g, epsb[:, :], bb.io["c_epsf"][:, :])
                for m in range(NT + 2):
                    if m < NT:
                        if pre_g is not None:
                            pre_g(g, m)
                        if m >= 2:
                            g.wait_ge(bb.sem(sv), bv + 2 * (m - 1))
                        sxi[m % 2].put(g, xs[m % 2][:, :],
                                       x_in[m * 128:(m + 1) * 128, :])
                    if m >= 2:
                        mm_ = m - 2
                        g.wait_ge(bb.sem(sv), bv + 2 * mm_ + 2)
                        sxo[mm_ % 2].put(g, x_out[mm_ * 128:(mm_ + 1) * 128, :],
                                         xo[mm_ % 2][:, :])
                sxo[0].wait(g)
                sxo[1].wait(g)

            @blk.tensor
            def _(t):
                for m in range(NT):
                    if m >= 2:
                        t.wait_ge(bb.sem(sv), bv + 2 * (m - 1) + 2)
                        t.wait_ge(bb.sem(ss), bs + m - 1)
                    if mm_waits is not None:
                        mm_waits(t, m)
                    last = emit_mm(t, m, pE[:, m % 2, :])
                    last.then_inc(bb.sem(sm), 1)
                    bb.inc(sm)

            @blk.scalar
            def _(s):
                sce.wait_rel(s, 16)
                for m in range(NT):
                    i = m % 2
                    s1, s2 = stt[:, i, 0:1], stt[:, i, 1:2]
                    mean, s2d = stt[:, i, 2:3], stt[:, i, 3:4]
                    tvd, var = stt[:, i, 4:5], stt[:, i, 5:6]
                    lnv, rstd = stt[:, i, 6:7], stt[:, i, 7:8]
                    s.wait_ge(bb.sem(sm), bm + m + 1)
                    s.activation(sq[i][:, 0, :], pE[:, i, :], AF.Copy,
                                 accum_out=s1)
                    s.activation(sq[i][:, 1, :], pE[:, i, :], AF.Square,
                                 accum_out=s2)
                    s.drain()
                    s.activation(mean, s1, AF.Copy, scale=1.0 / D)
                    s.activation(s2d, s2, AF.Copy, scale=1.0 / D)
                    s.drain()
                    s.activation(tvd, mean, AF.Square)
                    s.drain()
                    s.activation(var, tvd, AF.Identity, scale=-1.0, bias=s2d)
                    s.drain()
                    s.activation(lnv, var, AF.Ln, bias=epsb[:, :])
                    s.drain()
                    s.activation(rstd, lnv, AF.Exp, scale=-0.5
                                 ).then_inc(bb.sem(ss), 1)
                    bb.inc(ss)

            @blk.vector
            def _(v):
                for m in range(NT):
                    i = m % 2
                    mean, rstd = stt[:, i, 2:3], stt[:, i, 7:8]
                    v.wait_ge(bb.sem(ss), bs + m + 1)
                    v.tensor_scalar(tmp[i][:, :], pE[:, i, :], mean, rstd,
                                    op0=ALU.subtract, op1=ALU.mult
                                    ).then_inc(bb.sem(sv), 1)
                    bb.inc(sv)
                    v.drain()
                    sxi[i].wait_rel(v, 16 * (m // 2 + 1))
                    if m >= 2:
                        sxo[i].wait_rel(v, 16 * (m // 2))
                    v.tensor_tensor(xo[i][:, :], tmp[i][:, :], xs[i][:, :],
                                    op=ALU.add).then_inc(bb.sem(sv), 1)
                    bb.inc(sv)


def emit_l0_attn(bb, x_in, x_out, pfx="a0"):
    nc = bb.nc
    io = bb.io
    xt = io[pfx + "_xt"]
    v_scr = io[pfx + "_vscr"]

    emit_preln_xt(bb, x_in, xt, pfx + "A")

    with ExitStack() as st0:
        e0 = st0.enter_context
        OT = e0(nc.sbuf_tensor(pfx + "OT", [128, ND, S], BF16))
        onesc = e0(nc.sbuf_tensor(pfx + "onesc", [128, 1], BF16))
        onesr = e0(nc.sbuf_tensor(pfx + "onesr", [1, 128], BF16))
        idsb = e0(nc.sbuf_tensor(pfx + "id", [128, 128], BF16))
        scst = bb.slot("A0dc")

        with nc.Block() as blk:
            @blk.sync
            def _(g):
                scst.put(g, onesc[:, :], io["c_ones_col"][:, :])
                scst.put(g, onesr[:, :], io["c_ones_row"][:, :])
                scst.put(g, idsb[:, :], io["c_ident"][:, :])
                scst.wait(g)

        _l0_v_phase(bb, xt, v_scr, pfx)

        with ExitStack() as stC:
            eC = stC.enter_context
            avgb = eC(nc.sbuf_tensor(pfx + "avg", [128, NT, 128], BF16))
            maskq = eC(nc.sbuf_tensor(pfx + "mq", [128, NT, 512], BF16))
            cs1b = eC(nc.sbuf_tensor(pfx + "cs1", [128, S], BF16))
            cs2b = eC(nc.sbuf_tensor(pfx + "cs2", [128, S], BF16))
            rleb = eC(nc.sbuf_tensor(pfx + "rle", [128, 64], BF16))
            rlob = eC(nc.sbuf_tensor(pfx + "rlo", [128, 64], BF16))
            sck = bb.slot("A0dk")
            svh = bb.slot("A0dv")
            with nc.Block() as blk:
                @blk.sync
                def _(g):
                    sck.put(g, cs1b[:, :], io["c_cs1"][:, :])
                    sck.put(g, cs2b[:, :], io["c_cs2"][:, :])
                    sck.put(g, rleb[:, :], io["c_rl_e"][:, :])
                    sck.put(g, rlob[:, :], io["c_rl_o"][:, :])
                    for j in range(NT):
                        sck.put(g, avgb[:, j, :],
                                io["c_avg"][j * 128:(j + 1) * 128, :])
                    for qt in range(NT):
                        sck.put(g, maskq[:, qt, 0:128],
                                io["c_m_cmp"][120 - 8 * qt:248 - 8 * qt, :])
                        src2 = io["c_m_w2"] if qt >= 2 else io["c_m_zero"]
                        src1 = io["c_m_ones"] if qt >= 1 else io["c_m_zero"]
                        sck.put(g, maskq[:, qt, 128:256], src2[:, :])
                        sck.put(g, maskq[:, qt, 256:384], src1[:, :])
                        sck.put(g, maskq[:, qt, 384:512], io["c_m_w0"][:, :])
                    sck.wait(g)

            for grp in range(NH // 2):
                _l0_attn_group(bb, grp, xt, v_scr, OT, onesc, onesr, idsb,
                               avgb, maskq, cs1b, cs2b, rleb, rlob, svh, pfx)

        with ExitStack() as st:
            wob = st.enter_context(nc.sbuf_tensor(pfx + "wob", [128, ND, D], BF16))
            swo = bb.slot("A0dw")
            with nc.Block() as blk:
                @blk.gpsimd
                def _(g):
                    for k in range(ND):
                        swo.put(g, wob[:, k, :],
                                io[pfx + "_woT"][k * 128:(k + 1) * 128, :])
                    swo.wait(g)

            def emit_mm(t, m, psum):
                for k in range(ND):
                    for nb in range(NB):
                        mm = t.matmul(psum[:, nb * 512:(nb + 1) * 512],
                                 OT[:, k, m * 128:(m + 1) * 128],
                                 wob[:, k, nb * 512:(nb + 1) * 512],
                                 start=(k == 0), stop=(k == ND - 1))
                return mm

            emit_out_ln_residual(bb, x_in, x_out, pfx + "E", emit_mm)


def _l0_v_phase(bb, xt, v_scr, pfx):
    nc = bb.nc
    io = bb.io
    with ExitStack() as st:
        e = st.enter_context
        wvb = e(nc.sbuf_tensor(pfx + "wvb", [128, ND, D], BF16))
        xtb = e(nc.sbuf_tensor(pfx + "xtb", [128, 2, ND, 256], BF16))
        vout = [e(nc.sbuf_tensor(f"{pfx}vo{i}", [128, D], BF16)) for i in range(2)]
        pV = e(nc.psum_tensor(pfx + "pV", [128, 2, D], F32))
        swv = bb.slot("Vdwv")
        sxtb = [bb.slot("Vdxb%d" % i) for i in range(2)]
        svo = [bb.slot("Vdvo%d" % i) for i in range(2)]
        sm, sv = "BVm", "BVv"
        bb.sem(sm), bb.sem(sv)
        bm, bv = bb.cnt[sm], bb.cnt[sv]

        with nc.Block() as blk:
            @blk.gpsimd
            def _(g):
                for k in range(ND):
                    swv.put(g, wvb[:, k, :],
                            io[pfx + "_wvT"][k * 128:(k + 1) * 128, :])

            @blk.sync
            def _(g):
                for m2 in range(NT2 + 1):
                    if m2 < NT2:
                        if m2 >= 2:
                            g.wait_ge(bb.sem(sm), bm + 2 * (m2 - 1))
                        sxtb[m2 % 2].put(g, xtb[:, m2 % 2, :, :],
                                         xt[:, m2].rearrange("k p f -> p k f"))
                    if m2 >= 1:
                        for sub in range(2):
                            m = 2 * (m2 - 1) + sub
                            g.wait_ge(bb.sem(sv), bv + m + 1)
                            svo[m % 2].put(
                                g, v_scr[:, m].rearrange("h p f -> p h f"),
                                vout[m % 2][:, :])
                svo[0].wait(g)
                svo[1].wait(g)

            @blk.tensor
            def _(t):
                swv.wait(t)
                for m2 in range(NT2):
                    sxtb[m2 % 2].wait_rel(t, 16 * (m2 // 2 + 1))
                    for sub in range(2):
                        m = 2 * m2 + sub
                        if m >= 2:
                            t.wait_ge(bb.sem(sv), bv + m - 1)
                        for k in range(ND):
                            for nb in range(NB):
                                mm = t.matmul(pV[:, m % 2, nb * 512:(nb + 1) * 512],
                                         xtb[:, m2 % 2, k, sub * 128:(sub + 1) * 128],
                                         wvb[:, k, nb * 512:(nb + 1) * 512],
                                         start=(k == 0), stop=(k == ND - 1))
                        mm.then_inc(bb.sem(sm), 1)
                        bb.inc(sm)

            @blk.scalar
            def _(s):
                for m in range(NT):
                    s.wait_ge(bb.sem(sm), bm + m + 1)
                    if m >= 2:
                        svo[m % 2].wait_rel(s, 16 * (m // 2))
                    s.copy(vout[m % 2][:, :], pV[:, m % 2, :]
                           ).then_inc(bb.sem(sv), 1)
                    bb.inc(sv)


def _l0_attn_group(bb, grp, xt, v_scr, OT, onesc, onesr, idsb, avgb, maskq,
                   cs1b, cs2b, rleb, rlob, svh, pfx):
    nc = bb.nc
    io = bb.io
    g0 = grp * 2
    un = pfx + bb.u()
    with ExitStack() as st:
        e = st.enter_context
        wqg = e(nc.sbuf_tensor(un + "wqg", [128, ND, 256], BF16))
        wkg = e(nc.sbuf_tensor(un + "wkg", [128, ND, 256], BF16))
        xtq = e(nc.sbuf_tensor(un + "xtq", [128, 2, ND, 512], BF16))
        QT = e(nc.sbuf_tensor(un + "QT", [128, 2, S], BF16))
        KT = e(nc.sbuf_tensor(un + "KT", [128, 2, S], BF16))
        QTr = e(nc.sbuf_tensor(un + "QTr", [128, 2, S], BF16))
        KTr = e(nc.sbuf_tensor(un + "KTr", [128, 2, S], BF16))
        rP = e(nc.sbuf_tensor(un + "rP", [128, S], BF16))
        rP2 = e(nc.sbuf_tensor(un + "rP2", [128, S], BF16))
        swg = bb.slot("Gdw")
        sxq = [bb.slot("Gdq%d" % i) for i in range(2)]

        sm, ss = "BGm", "BGs"
        bb.sem(sm), bb.sem(ss)
        bm, bs = bb.cnt[sm], bb.cnt[ss]

        with ExitStack() as st1:
            pqk = st1.enter_context(
                nc.psum_tensor(un + "pqk", [128, 2, 4, 512], F32))
            with nc.Block() as blk:
                @blk.gpsimd
                def _(g):
                    swg.put(g, wqg[:, :, :],
                            io[pfx + "_wqT"][:, g0 * 128:(g0 + 2) * 128
                                             ].rearrange("(kt p) n -> p kt n",
                                                         p=128))
                    swg.put(g, wkg[:, :, :],
                            io[pfx + "_wkT"][:, g0 * 128:(g0 + 2) * 128
                                             ].rearrange("(kt p) n -> p kt n",
                                                         p=128))

                @blk.sync
                def _(g):
                    for tq in range(4):
                        if tq >= 2:
                            g.wait_ge(bb.sem(sm), bm + tq - 1)
                        sxq[tq % 2].put(g, xtq[:, tq % 2, :, 0:256],
                                        xt[:, 2 * tq].rearrange("k p f -> p k f"))
                        sxq[tq % 2].put(g, xtq[:, tq % 2, :, 256:512],
                                        xt[:, 2 * tq + 1].rearrange(
                                            "k p f -> p k f"))

                @blk.tensor
                def _(t):
                    swg.wait(t)
                    for tq in range(4):
                        sxq[tq % 2].wait_rel(t, 32 * (tq // 2 + 1))
                        if tq >= 2:
                            t.wait_ge(bb.sem(ss), bs + 4 * (tq - 1))
                        for k in range(ND):
                            for hi in range(2):
                                t.matmul(pqk[:, tq % 2, hi, :],
                                         wqg[:, k, hi * 128:(hi + 1) * 128],
                                         xtq[:, tq % 2, k, :],
                                         start=(k == 0), stop=(k == ND - 1))
                                mm = t.matmul(pqk[:, tq % 2, 2 + hi, :],
                                         wkg[:, k, hi * 128:(hi + 1) * 128],
                                         xtq[:, tq % 2, k, :],
                                         start=(k == 0), stop=(k == ND - 1))
                        mm.then_inc(bb.sem(sm), 1)
                        bb.inc(sm)

                @blk.scalar
                def _(s):
                    for tq in range(4):
                        s.wait_ge(bb.sem(sm), bm + tq + 1)
                        s.copy(QT[:, 0, tq * 512:(tq + 1) * 512],
                               pqk[:, tq % 2, 0, :])
                        s.copy(QT[:, 1, tq * 512:(tq + 1) * 512],
                               pqk[:, tq % 2, 1, :])
                        s.copy(KT[:, 0, tq * 512:(tq + 1) * 512],
                               pqk[:, tq % 2, 2, :])
                        s.copy(KT[:, 1, tq * 512:(tq + 1) * 512],
                               pqk[:, tq % 2, 3, :]).then_inc(bb.sem(ss), 4)
                        bb.inc(ss, 4)

        with ExitStack() as str_:
            prr = str_.enter_context(nc.psum_tensor(un + "prr", [128, S], F32))
            smr, svr, ssr = "BRm", "BRv", "BRs"
            bb.sem(smr), bb.sem(svr), bb.sem(ssr)
            bmr, bvr, bsr = bb.cnt[smr], bb.cnt[svr], bb.cnt[ssr]
            cases = [(hi, src, dst) for hi in range(2)
                     for (src, dst) in ((QT, QTr), (KT, KTr))]
            with nc.Block() as blk:
                @blk.vector
                def _(v):
                    for i, (hi, src, dst) in enumerate(cases):
                        if i >= 1:
                            v.wait_ge(bb.sem(smr), bmr + i)
                        v.tensor_tensor(rP[0:64, :], src[0:64, hi, :],
                                        cs1b[0:64, :], op=ALU.mult)
                        v.tensor_tensor(rP[64:128, :], src[64:128, hi, :],
                                        cs1b[64:128, :], op=ALU.mult)
                        v.tensor_tensor(rP2[0:64, :], src[0:64, hi, :],
                                        cs2b[0:64, :], op=ALU.mult)
                        v.tensor_tensor(rP2[64:128, :], src[64:128, hi, :],
                                        cs2b[64:128, :], op=ALU.mult
                                        ).then_inc(bb.sem(svr), 1)
                        bb.inc(svr)

                @blk.tensor
                def _(t):
                    for i, (hi, src, dst) in enumerate(cases):
                        t.wait_ge(bb.sem(svr), bvr + i + 1)
                        if i >= 1:
                            t.wait_ge(bb.sem(ssr), bsr + i)
                        for nb in range(NB):
                            t.matmul(prr[0:64, nb * 512:(nb + 1) * 512],
                                     rleb[:, :], rP[:, nb * 512:(nb + 1) * 512],
                                     start=True, stop=True, skip_group_check=True)
                            mm = t.matmul(prr[64:128, nb * 512:(nb + 1) * 512],
                                     rlob[:, :], rP2[:, nb * 512:(nb + 1) * 512],
                                     start=True, stop=True, skip_group_check=True)
                        mm.then_inc(bb.sem(smr), 1)
                        bb.inc(smr)

                @blk.scalar
                def _(s):
                    for i, (hi, src, dst) in enumerate(cases):
                        s.wait_ge(bb.sem(smr), bmr + i + 1)
                        s.copy(dst[:, hi, :], prr[:, :]).then_inc(bb.sem(ssr), 1)
                        bb.inc(ssr)

        for hi in range(2):
            _l0_attn_head(bb, g0 + hi, hi, QTr, KTr, v_scr, OT, onesc, onesr,
                          idsb, avgb, maskq, svh, pfx)


def _l0_attn_head(bb, h, hi, QTr, KTr, v_scr, OT, onesc, onesr, idsb, avgb,
                  maskq, svh, pfx):
    nc = bb.nc
    sm, ss, sv = "BHm", "BHs", "BHv"
    for _s in (sm, ss, sv):
        bb.sem(_s)
    un = pfx + bb.u()
    with ExitStack() as st:
        e = st.enter_context
        vh = e(nc.sbuf_tensor(un + "vh", [128, NT, 128], BF16))
        ktok = e(nc.sbuf_tensor(un + "ktok", [128, NT, 128], BF16))
        ckb = e(nc.sbuf_tensor(un + "ckb", [128, 128], BF16))
        cvb = e(nc.sbuf_tensor(un + "cvb", [128, 128], BF16))
        sqs = e(nc.sbuf_tensor(un + "sqs", [128, S], BF16))
        nrm = e(nc.sbuf_tensor(un + "nrm", [1, 8], F32))
        c11b = e(nc.sbuf_tensor(un + "c11b", [1, 1], BF16))
        negC = e(nc.sbuf_tensor(un + "negC", [128, 1], F32))

        with nc.Block() as blk:
            @blk.sync
            def _(g):
                svh.put(g, vh[:, :, :], v_scr[h].rearrange("m p f -> p m f"))
                svh.wait(g)

        with ExitStack() as sta:
            ptk = sta.enter_context(nc.psum_tensor(un + "ptk", [128, NT, 128],
                                                   BF16))
            with nc.Block() as blk:
                @blk.tensor
                def _(t):
                    for j in range(NT):
                        tp = t.transpose(ptk[:, j, :],
                                    KTr[:, hi, j * 128:(j + 1) * 128], idsb[:, :])
                    tp.then_inc(bb.sem(sm), 1)
                    bb.inc(sm)

                @blk.vector
                def _(v):
                    v.wait_ge(bb.sem(sm), bb.cnt[sm])
                    for j in range(NT):
                        cp = v.tensor_copy(ktok[:, j, :], ptk[:, j, :])
                    cp.then_inc(bb.sem(sv), 1)
                    bb.inc(sv)

        with ExitStack() as st1:
            e1 = st1.enter_context
            pck = e1(nc.psum_tensor(un + "pck", [128, 128], F32))
            pcv = e1(nc.psum_tensor(un + "pcv", [128, 128], F32))
            pnr = e1(nc.psum_tensor(un + "pnr", [1, S], F32))
            pb1 = e1(nc.psum_tensor(un + "pb1", [128, 1], F32))

            with nc.Block() as blk:
                @blk.tensor
                def _(t):
                    for j in range(NT):
                        t.matmul(pck[:, :], ktok[:, j, :], avgb[:, j, :],
                                 start=(j == 0), stop=(j == NT - 1))
                    for j in range(NT):
                        mm = t.matmul(pcv[:, :], avgb[:, j, :], vh[:, j, :],
                                 start=(j == 0), stop=(j == NT - 1))
                    mm.then_inc(bb.sem(sm), 1)
                    bb.inc(sm)

                @blk.scalar
                def _(s):
                    s.wait_ge(bb.sem(sm), bb.cnt[sm])
                    s.copy(ckb[:, :], pck[:, :])
                    s.copy(cvb[:, :], pcv[:, :]).then_inc(bb.sem(ss), 1)
                    bb.inc(ss)

            for (srcT, slot) in ((QTr, 0), (KTr, 1)):
                with nc.Block() as blk:
                    @blk.scalar
                    def _(s, srcT=srcT):
                        s.activation(sqs[:, :], srcT[:, hi, :], AF.Square
                                     ).then_inc(bb.sem(ss), 1)
                        bb.inc(ss)

                    @blk.tensor
                    def _(t):
                        t.wait_ge(bb.sem(ss), bb.cnt[ss])
                        for nb in range(NB):
                            mm = t.matmul(pnr[0:1, nb * 512:(nb + 1) * 512],
                                     onesc[:, :], sqs[:, nb * 512:(nb + 1) * 512],
                                     start=True, stop=True)
                        mm.then_inc(bb.sem(sm), 1)
                        bb.inc(sm)

                    @blk.vector
                    def _(v, slot=slot):
                        v.wait_ge(bb.sem(sm), bb.cnt[sm])
                        v.reduce_max(nrm[0:1, slot:slot + 1], pnr[0:1, :],
                                     axis=AX.X).then_inc(bb.sem(sv), 1)
                        bb.inc(sv)

            with nc.Block() as blk:
                @blk.scalar
                def _(s):
                    s.activation(sqs[:, 0:128], ckb[:, :], AF.Square
                                 ).then_inc(bb.sem(ss), 1)
                    bb.inc(ss)

                @blk.tensor
                def _(t):
                    t.wait_ge(bb.sem(ss), bb.cnt[ss])
                    t.matmul(pnr[0:1, 0:128], onesc[:, :], sqs[:, 0:128],
                             start=True, stop=True).then_inc(bb.sem(sm), 1)
                    bb.inc(sm)

                @blk.vector
                def _(v):
                    v.wait_ge(bb.sem(sm), bb.cnt[sm])
                    v.reduce_max(nrm[0:1, 2:3], pnr[0:1, 0:128], axis=AX.X)
                    v.drain()
                    v.tensor_tensor(nrm[0:1, 3:4], nrm[0:1, 1:2], nrm[0:1, 2:3],
                                    op=ALU.max)
                    v.drain()
                    v.tensor_tensor(nrm[0:1, 4:5], nrm[0:1, 0:1], nrm[0:1, 3:4],
                                    op=ALU.mult).then_inc(bb.sem(sv), 1)
                    bb.inc(sv)

                @blk.scalar
                def _(s2):
                    s2.wait_ge(bb.sem(sv), bb.cnt[sv])
                    s2.activation(nrm[0:1, 5:6], nrm[0:1, 4:5], AF.Sqrt)
                    s2.drain()
                    s2.activation(nrm[0:1, 6:7], nrm[0:1, 5:6], AF.Copy,
                                  scale=-SCALE).then_inc(bb.sem(ss), 1)
                    bb.inc(ss)

                @blk.vector
                def _(v2):
                    v2.wait_ge(bb.sem(ss), bb.cnt[ss])
                    v2.tensor_copy(c11b[:, :], nrm[0:1, 6:7]
                                   ).then_inc(bb.sem(sv), 1)
                    bb.inc(sv)

            with nc.Block() as blk:
                @blk.tensor
                def _(t):
                    t.matmul(pb1[:, :], onesr[0:1, :], c11b[:, :],
                             start=True, stop=True).then_inc(bb.sem(sm), 1)
                    bb.inc(sm)

                @blk.scalar
                def _(s):
                    s.wait_ge(bb.sem(sm), bb.cnt[sm])
                    s.copy(negC[:, :], pb1[:, :]).then_inc(bb.sem(ss), 1)
                    bb.inc(ss)

        for half in range(2):
            _l0_attn_half(bb, h, hi, half, QTr, KTr, vh, OT, onesc, onesr,
                          ckb, cvb, maskq, negC, pfx)


def _l0_attn_half(bb, h, hi, half, QTr, KTr, vh, OT, onesc, onesr, ckb, cvb,
                  maskq, negC, pfx):
    nc = bb.nc
    sm, ss, sv = "BWm", "BWs", "BWv"
    for _s in (sm, ss, sv):
        bb.sem(_s)
    un = pfx + bb.u()
    with ExitStack() as st:
        e = st.enter_context
        pe_sb = e(nc.sbuf_tensor(un + "pe", [128, 2, 512], BF16))
        pm_sb = e(nc.sbuf_tensor(un + "pm", [128, 2, 512], BF16))
        rrow = e(nc.sbuf_tensor(un + "rrow", [1, 1024], F32))
        rrowb = e(nc.sbuf_tensor(un + "rrowb", [1, 1024], BF16))
        rb_sb = e(nc.sbuf_tensor(un + "rb", [128, 1024], BF16))
        ps = e(nc.psum_tensor(un + "ps", [128, 2, 512], F32))
        pOT = e(nc.psum_tensor(un + "pOT", [128, 1024], F32))
        prow = e(nc.psum_tensor(un + "prow", [1, 1024], F32))
        pbc = e(nc.psum_tensor(un + "pbc", [128, 1024], F32))

        qts = [half * 8 + i for i in range(8)]
        bm, bs, bv = bb.cnt[sm], bb.cnt[ss], bb.cnt[sv]

        def sm_scores(i):
            return bm + (i + 1 if i < 2 else 2 * i - 1)

        def sm_consume(i):
            return bm + (2 * i + 4 if i <= 5 else 9 + i)

        with nc.Block() as blk:
            @blk.tensor
            def _(t):
                for j in range(10):
                    if j < 8:
                        i = j
                        qt = qts[i]
                        if i >= 2:
                            t.wait_ge(bb.sem(ss), bs + i - 1)
                        q_ap = QTr[:, hi, qt * 128:(qt + 1) * 128]
                        t.matmul(ps[:, i % 2, 0:128], ckb[:, :], q_ap,
                                 start=True, stop=False, skip_group_check=True)
                        for d, jt in enumerate((max(qt - 2, 0), max(qt - 1, 0),
                                                qt)):
                            mm = t.matmul(
                                ps[:, i % 2, 128 * (d + 1):128 * (d + 2)],
                                KTr[:, hi, jt * 128:(jt + 1) * 128], q_ap,
                                start=False, stop=(d == 2),
                                skip_group_check=True)
                        mm.then_inc(bb.sem(sm), 1)
                        bb.inc(sm)
                    if j >= 2:
                        i = j - 2
                        qt = qts[i]
                        t.wait_ge(bb.sem(sv), bv + i + 1)
                        col = (i % 4) * 128
                        bank = i // 4
                        first = (i % 4 == 0)
                        reg = slice(bank * 512 + col, bank * 512 + col + 128)
                        t.matmul(prow[0:1, reg], onesc[:, :],
                                 pm_sb[:, i % 2, 0:128],
                                 start=first, stop=False, skip_group_check=True)
                        t.matmul(pOT[:, reg], cvb[:, :], pm_sb[:, i % 2, 0:128],
                                 start=first, stop=False, skip_group_check=True)
                        for d, jt in enumerate((max(qt - 2, 0), max(qt - 1, 0),
                                                qt)):
                            pm_ap = pm_sb[:, i % 2, 128 * (d + 1):128 * (d + 2)]
                            t.matmul(prow[0:1, reg], onesc[:, :], pm_ap,
                                     start=False, stop=False,
                                     skip_group_check=True)
                            mm = t.matmul(pOT[:, reg], vh[:, jt, :], pm_ap,
                                     start=False, stop=False,
                                     skip_group_check=True)
                        mm.then_inc(bb.sem(sm), 1)
                        bb.inc(sm)

            @blk.scalar
            def _(s):
                for i in range(8):
                    s.wait_ge(bb.sem(sm), sm_scores(i))
                    if i >= 2:
                        s.wait_ge(bb.sem(sv), bv + i - 1)
                    s.activation(pe_sb[:, i % 2, :], ps[:, i % 2, :], AF.Exp,
                                 scale=SCALE, bias=negC[:, :]
                                 ).then_inc(bb.sem(ss), 1)
                    bb.inc(ss)

            @blk.vector
            def _(v):
                for i in range(8):
                    v.wait_ge(bb.sem(ss), bs + i + 1)
                    if i >= 2:
                        v.wait_ge(bb.sem(sm), sm_consume(i - 2))
                    v.tensor_tensor(pm_sb[:, i % 2, :], pe_sb[:, i % 2, :],
                                    maskq[:, qts[i], :], op=ALU.mult
                                    ).then_inc(bb.sem(sv), 1)
                    bb.inc(sv)

        bm, bs, bv = bb.cnt[sm], bb.cnt[ss], bb.cnt[sv]
        with nc.Block() as blk:
            @blk.vector
            def _(v):
                v.reciprocal(rrow[:, :], prow[0:1, :])
                v.drain()
                v.tensor_copy(rrowb[:, :], rrow[:, :]).then_inc(bb.sem(sv), 1)
                bb.inc(sv)

            @blk.tensor
            def _(t):
                t.wait_ge(bb.sem(sv), bv + 1)
                t.matmul(pbc[:, 0:512], onesr[0:1, :], rrowb[0:1, 0:512],
                         start=True, stop=True)
                t.matmul(pbc[:, 512:1024], onesr[0:1, :], rrowb[0:1, 512:1024],
                         start=True, stop=True).then_inc(bb.sem(sm), 1)
                bb.inc(sm)

            @blk.scalar
            def _(s):
                s.wait_ge(bb.sem(sm), bm + 1)
                s.copy(rb_sb[:, :], pbc[:, :]).then_inc(bb.sem(ss), 1)
                bb.inc(ss)

            @blk.vector
            def _(v2):
                v2.wait_ge(bb.sem(ss), bs + 1)
                v2.tensor_tensor(OT[0:128, h, half * 1024:(half + 1) * 1024],
                                 pOT[:, :], rb_sb[:, :], op=ALU.mult)


def _prep_l0_attn(inputs, perm):
    wq = np.asarray(inputs["L0_attn_wq"], np.float32)
    wk = np.asarray(inputs["L0_attn_wk"], np.float32)
    wv = np.asarray(inputs["L0_attn_wv"], np.float32)
    wo = np.asarray(inputs["L0_attn_wo"], np.float32)
    return {
        "a0_wqT": np.ascontiguousarray(wq.T[:, perm]),
        "a0_wkT": np.ascontiguousarray(wk.T[:, perm]),
        "a0_wvT": np.ascontiguousarray(wv.T),
        "a0_woT": np.ascontiguousarray(wo.T),
    }


def _declare_common(nc, bb):
    for k, arr in _consts().items():
        dt = BF16 if arr.dtype == _BF else F32
        bb.io["c_" + k] = nc.dram_tensor("c_" + k, list(arr.shape), dt,
                                         kind="ExternalInput").ap()


def build_l0_attn():
    nc = bass.Bass("TRN2", target_bir_lowering=False)
    with ExitStack() as ctx:
        bb = BB(nc, ctx)
        x_in = nc.dram_tensor("x", [S, D], F32, kind="ExternalInput").ap()
        x_out = nc.dram_tensor("y", [S, D], F32, kind="ExternalOutput").ap()
        for nm in ["a0_wqT", "a0_wkT", "a0_wvT", "a0_woT"]:
            bb.io[nm] = nc.dram_tensor(nm, [D, D], F32, kind="ExternalInput").ap()
        _declare_common(nc, bb)
        bb.io["a0_xt"] = nc.dram_tensor("a0_xt", [ND, NT2, 128, 256], BF16,
                                        kind="Internal").ap()
        bb.io["a0_vscr"] = nc.dram_tensor("a0_vscr", [NH, NT, 128, 128], BF16,
                                          kind="Internal").ap()
        emit_l0_attn(bb, x_in, x_out, "a0")
    return nc


def kernel(**inputs):
    raise NotImplementedError("full module not assembled yet")


def emit_scr_out_phase(bb, x_in, x_out, acc_scr, pfx):
    nc = bb.nc
    un = pfx + bb.u()
    with ExitStack() as st:
        e = st.enter_context
        otile = [e(nc.sbuf_tensor(f"{un}ot{i}", [128, ND, 128], F32))
                 for i in range(2)]
        idf = e(nc.sbuf_tensor(un + "idf", [128, 128], F32))
        sot = [bb.slot("SCdot%d" % i) for i in range(2)]
        sid = bb.slot("SCdid")

        with nc.Block() as blk:
            @blk.sync
            def _(g):
                sid.put(g, idf[:, :], bb.io["c_identf"][:, :])
                sid.wait(g)

        sm = "OTm"
        bb.sem(sm)
        bm0 = bb.cnt[sm]

        def pre_g(g, m):
            if m >= 2:
                g.wait_ge(bb.sem(sm), bm0 + m - 1)
            sot[m % 2].put(g, otile[m % 2][:, :, :],
                           acc_scr[:, :, m * 128:(m + 1) * 128
                                   ].rearrange("d p f -> p d f"))

        def mm_waits(t, m):
            sot[m % 2].wait_rel(t, 16 * (m // 2 + 1))

        def emit_mm(t, m, psum):
            for dt in range(ND):
                tp = t.transpose(psum[:, dt * 128:(dt + 1) * 128],
                                 otile[m % 2][:, dt, :], idf[:, :])
            return tp

        emit_out_ln_residual(bb, x_in, x_out, pfx, emit_mm, mm_waits,
                             pre_g=pre_g)


def emit_l1_mix(bb, x_in, x_out, pfx="m1"):
    nc = bb.nc
    io = bb.io
    xt = io[pfx + "_xt"]
    emit_preln_xt(bb, x_in, xt, pfx + "A")

    with ExitStack() as st0:
        e0 = st0.enter_context
        vsb = e0(nc.sbuf_tensor(pfx + "vsb", [128, S], BF16))
        onesr = e0(nc.sbuf_tensor(pfx + "onesr", [1, 128], BF16))
        onesS = e0(nc.sbuf_tensor(pfx + "onesS", [1, S], BF16))
        bvb = e0(nc.sbuf_tensor(pfx + "bvb", [1, S], BF16))
        bob = e0(nc.sbuf_tensor(pfx + "bob", [1, S], BF16))

        with ExitStack() as st:
            e = st.enter_context
            wvb = e(nc.sbuf_tensor(pfx + "wvb", [128, ND, D], BF16))
            wvs32 = e(nc.sbuf_tensor(pfx + "wvs32", [128, ND, 128], F32))
            wvsum = e(nc.sbuf_tensor(pfx + "wvsum", [128, ND, 128], BF16))
            bvs32 = e(nc.sbuf_tensor(pfx + "bvs32", [1, 128], F32))
            bvsum = e(nc.sbuf_tensor(pfx + "bvsum", [1, 128], BF16))
            xrow = [e(nc.sbuf_tensor(f"{pfx}xr{i}", [128, S], BF16))
                    for i in range(2)]
            pvs = e(nc.psum_tensor(pfx + "pvs", [128, S], F32))
            swv = bb.slot("M1dwv")
            sxr = [bb.slot("M1dxr%d" % i) for i in range(2)]
            sm, sv = "BVm", "BVv"
            bb.sem(sm), bb.sem(sv)
            bm, bv_ = bb.cnt[sm], bb.cnt[sv]

            with nc.Block() as blk:
                @blk.gpsimd
                def _(g):
                    swv.put(g, onesr[:, :], io["c_ones_row"][:, :])
                    swv.put(g, onesS[:, :], io["c_ones_rowS"][:, :])
                    swv.put(g, bvb[:, :], io[pfx + "_bv"][:, :])
                    swv.put(g, bob[:, :], io[pfx + "_bo"][:, :])
                    for k in range(ND):
                        swv.put(g, wvb[:, k, :],
                                io[pfx + "_wvT"][k * 128:(k + 1) * 128, :])

            with nc.Block() as blk:
                @blk.vector
                def _(v):
                    swv.wait(v)
                    for k in range(ND):
                        v.tensor_tensor(wvs32[:, k, :], wvb[:, k, 0:128],
                                        wvb[:, k, 128:256], op=ALU.add)
                    v.drain()
                    for kh in range(2, NH):
                        for k in range(ND):
                            v.tensor_tensor(wvs32[:, k, :], wvs32[:, k, :],
                                            wvb[:, k, kh * 128:(kh + 1) * 128],
                                            op=ALU.add)
                        v.drain()
                    for k in range(ND):
                        v.tensor_copy(wvsum[:, k, :], wvs32[:, k, :])
                    v.tensor_tensor(bvs32[:, :], bvb[0:1, 0:128],
                                    bvb[0:1, 128:256], op=ALU.add)
                    v.drain()
                    for kh in range(2, NH):
                        v.tensor_tensor(bvs32[:, :], bvs32[:, :],
                                        bvb[0:1, kh * 128:(kh + 1) * 128],
                                        op=ALU.add)
                        v.drain()
                    v.tensor_copy(bvsum[:, :], bvs32[:, :])

            with nc.Block() as blk:
                @blk.sync
                def _(g):
                    for k in range(ND):
                        if k >= 2:
                            g.wait_ge(bb.sem(sm), bm + k - 1)
                        sxr[k % 2].put(g, xrow[k % 2][:, :],
                                       xt[k].rearrange("t p f -> p t f"))

                @blk.tensor
                def _(t):
                    for k in range(ND):
                        sxr[k % 2].wait_rel(t, 16 * (k // 2 + 1))
                        for nb in range(NB):
                            mm = t.matmul(pvs[:, nb * 512:(nb + 1) * 512],
                                          wvsum[:, k, :],
                                          xrow[k % 2][:, nb * 512:(nb + 1) * 512],
                                          start=(k == 0), stop=False,
                                          skip_group_check=True)
                        mm.then_inc(bb.sem(sm), 1)
                        bb.inc(sm)
                    for nb in range(NB):
                        mm = t.matmul(pvs[:, nb * 512:(nb + 1) * 512],
                                      bvsum[0:1, :],
                                      onesS[0:1, nb * 512:(nb + 1) * 512],
                                      start=False, stop=(nb == NB - 1),
                                      skip_group_check=True)
                    mm.then_inc(bb.sem(sm), 1)
                    bb.inc(sm)

                @blk.scalar
                def _(s):
                    s.wait_ge(bb.sem(sm), bm + ND + 1)
                    s.copy(vsb[:, :], pvs[:, :]).then_inc(bb.sem(sv), 1)
                    bb.inc(sv)

        with ExitStack() as st:
            e = st.enter_context
            wob = e(nc.sbuf_tensor(pfx + "wob", [128, ND, D], BF16))
            wos32 = e(nc.sbuf_tensor(pfx + "wos32", [128, D], F32))
            wosum = e(nc.sbuf_tensor(pfx + "wosum", [128, D], BF16))
            swo = bb.slot("M1dwo")
            with nc.Block() as blk:
                @blk.gpsimd
                def _(g):
                    for k in range(ND):
                        swo.put(g, wob[:, k, :],
                                io[pfx + "_woT"][k * 128:(k + 1) * 128, :])
                    swo.wait(g)

            with nc.Block() as blk:
                @blk.vector
                def _(v):
                    v.tensor_tensor(wos32[:, :], wob[:, 0, :], wob[:, 1, :],
                                    op=ALU.add)
                    v.drain()
                    for qh in range(2, NH):
                        v.tensor_tensor(wos32[:, :], wos32[:, :], wob[:, qh, :],
                                        op=ALU.add)
                        v.drain()
                    v.tensor_copy(wosum[:, :], wos32[:, :])

            def emit_mm(t, m, psum):
                for nb in range(NB):
                    t.matmul(psum[:, nb * 512:(nb + 1) * 512],
                             vsb[:, m * 128:(m + 1) * 128],
                             wosum[:, nb * 512:(nb + 1) * 512],
                             start=True, stop=False, skip_group_check=True)
                for nb in range(NB):
                    mm = t.matmul(psum[:, nb * 512:(nb + 1) * 512],
                                  onesr[0:1, 0:128],
                                  bob[0:1, nb * 512:(nb + 1) * 512],
                                  start=False, stop=(nb == NB - 1),
                                  skip_group_check=True)
                return mm

            emit_out_ln_residual(bb, x_in, x_out, pfx + "E", emit_mm)


def emit_l1_ffn(bb, x_in, x_out, pfx="f1"):
    nc = bb.nc
    io = bb.io
    xt = io[pfx + "_xt"]
    acc = io[pfx + "_acc"]
    NHM = HID1 // 128
    TC = 1024
    NCHUNK = S // TC

    emit_preln_xt(bb, x_in, xt, pfx + "A")
    _l1_w2_fold(bb, pfx, NHM)

    with ExitStack() as st0:
        e0 = st0.enter_context
        h128b = e0(nc.sbuf_tensor(pfx + "h128", [128, 128], BF16))
        hA = e0(nc.sbuf_tensor(pfx + "hA", [128, NHM, TC], BF16))
        sic = bb.slot("F1dic")
        with nc.Block() as blk:
            @blk.sync
            def _(g):
                sic.put(g, h128b[:, :], io["c_h128"][:, :])
                sic.wait(g)

        for chunk in range(NCHUNK):
            _l1_ffn_chunk(bb, chunk, xt, acc, h128b, hA, TC, NHM, pfx)

    emit_scr_out_phase(bb, x_in, x_out, acc, pfx + "E")


def _l1_w2_fold(bb, pfx, NHM):
    nc = bb.nc
    io = bb.io
    un = pfx + bb.u()
    vw = io[pfx + "_w2T"].rearrange("(b x) d -> b x d", b=64)
    wf = io[pfx + "_w2f"].rearrange("(a x) d -> a x d", a=64)
    XH = 64
    NSUB = 2
    NFB = XH // NSUB
    CS = D

    with ExitStack() as st:
        e = st.enter_context
        h64b = e(nc.sbuf_tensor(un + "h64", [128, 128], BF16))
        mv = e(nc.sbuf_tensor(un + "mv", [128, 2, NSUB, CS], BF16))
        res = e(nc.sbuf_tensor(un + "res", [128, 2, NSUB, CS], BF16))
        pfo = e(nc.psum_tensor(un + "pfo", [128, 2, CS], F32))
        sldA = bb.slot("F1dfl0")
        sldB = bb.slot("F1dfl1")
        sstA = bb.slot("F1dfs0")
        sstB = bb.slot("F1dfs1")
        shc = bb.slot("F1dfh")
        smf, sss, ssv = "BWFm", "BWFs", "BWFv"
        for _x in (smf, sss, ssv):
            bb.sem(_x)
        bmf, bss, bsv = bb.cnt[smf], bb.cnt[sss], bb.cnt[ssv]

        with nc.Block() as blk:
            @blk.gpsimd
            def _(g):
                for s in range(NFB):
                    if s >= 2:
                        g.wait_ge(bb.sem(smf), bmf + NSUB * (s - 1))
                    sldA.put(g, mv[0:64, s % 2, :, :],
                             vw[:, NSUB * s:NSUB * (s + 1), :])
                    sldB.put(g, mv[64:128, s % 2, :, :],
                             vw[:, XH + NSUB * s:XH + NSUB * (s + 1), :])

            @blk.scalar
            def _(s_):
                for s in range(NFB):
                    s_.wait_ge(bb.sem(smf), bmf + NSUB * s + 1)
                    if s >= 2:
                        sstA.wait_rel(s_, 16 * (s - 1))
                        sstB.wait_rel(s_, 16 * (s - 1))
                    s_.copy(res[:, s % 2, 0, :], pfo[:, 0, :]
                            ).then_inc(bb.sem(sss), 1)
                    bb.inc(sss)

            @blk.vector
            def _(v):
                for s in range(NFB):
                    v.wait_ge(bb.sem(smf), bmf + NSUB * s + 2)
                    if s >= 2:
                        sstA.wait_rel(v, 16 * (s - 1))
                        sstB.wait_rel(v, 16 * (s - 1))
                    v.tensor_copy(res[:, s % 2, 1, :], pfo[:, 1, :]
                                  ).then_inc(bb.sem(ssv), 1)
                    bb.inc(ssv)

            @blk.tensor
            def _(t):
                shc.wait(t)
                for j in range(NFB * NSUB):
                    s, sub = j // NSUB, j % NSUB
                    sldA.wait_rel(t, 16 * (s + 1))
                    sldB.wait_rel(t, 16 * (s + 1))
                    if j >= 2:
                        if j % 2 == 0:
                            t.wait_ge(bb.sem(sss), bss + j // 2)
                        else:
                            t.wait_ge(bb.sem(ssv), bsv + (j - 1) // 2)
                    for q in range(CS // 512):
                        mm = t.matmul(
                            pfo[:, j % 2, q * 512:(q + 1) * 512],
                            h64b[:, :],
                            mv[:, s % 2, sub, q * 512:(q + 1) * 512],
                            start=True, stop=True)
                    mm.then_inc(bb.sem(smf), 1)
                    bb.inc(smf)

            @blk.sync
            def _(g):
                shc.put(g, h64b[:, :], io["c_h64d"][:, :])
                for s in range(NFB):
                    g.wait_ge(bb.sem(sss), bss + s + 1)
                    g.wait_ge(bb.sem(ssv), bsv + s + 1)
                    sstA.put(g, wf[:, NSUB * s:NSUB * (s + 1), :],
                             res[0:64, s % 2, :, :])
                    sstB.put(g, wf[:, XH + NSUB * s:XH + NSUB * (s + 1), :],
                             res[64:128, s % 2, :, :])
                sstA.wait(g)
                sstB.wait(g)


def _l1_ffn_chunk(bb, chunk, xt, acc, h128b, hA, TC, NHM, pfx):
    nc = bb.nc
    io = bb.io
    un = pfx + bb.u()
    NH5 = TC // 512

    with ExitStack() as st:
        e = st.enter_context
        xtc = e(nc.sbuf_tensor(un + "xtc", [128, ND, TC], BF16))
        w1c = [e(nc.sbuf_tensor(f"{un}w1c{i}", [128, ND, 128], BF16))
               for i in range(2)]
        w3c = [e(nc.sbuf_tensor(f"{un}w3c{i}", [128, ND, 128], BF16))
               for i in range(2)]
        s1b = [e(nc.sbuf_tensor(f"{un}s1b{i}", [128, TC], BF16))
               for i in range(2)]
        pu = e(nc.psum_tensor(un + "pu", [128, 2, 2, TC], F32))
        sxc = bb.slot("F1dxc")
        sw1 = [bb.slot("F1dw1%d" % i) for i in range(2)]
        sw3 = [bb.slot("F1dw3%d" % i) for i in range(2)]
        sm, ss, sv = "BUm", "BUs", "BUv"
        for _x in (sm, ss, sv):
            bb.sem(_x)
        bm, bs, bv = bb.cnt[sm], bb.cnt[ss], bb.cnt[sv]

        with nc.Block() as blk:
            @blk.sync
            def _(g):
                for t2 in range(TC // 256):
                    sxc.put(g, xtc[:, :, t2 * 256:(t2 + 1) * 256],
                            xt[:, chunk * (TC // 256) + t2].rearrange(
                                "k p f -> p k f"))

            @blk.gpsimd
            def _(g):
                for hm in range(NHM):
                    if hm >= 2:
                        g.wait_ge(bb.sem(sm), bm + 2 * (hm - 1))
                    i = hm % 2
                    sw1[i].put(g, w1c[i][:, :, :],
                               io[pfx + "_w1T"][:, hm * 128:(hm + 1) * 128
                                                ].rearrange("(kt p) n -> p kt n",
                                                            p=128))
                    sw3[i].put(g, w3c[i][:, :, :],
                               io[pfx + "_w3T"][:, hm * 128:(hm + 1) * 128
                                                ].rearrange("(kt p) n -> p kt n",
                                                            p=128))

            @blk.tensor
            def _(t):
                sxc.wait(t)
                for hm in range(NHM):
                    i = hm % 2
                    sw1[i].wait_rel(t, 16 * (hm // 2 + 1))
                    sw3[i].wait_rel(t, 16 * (hm // 2 + 1))
                    if hm >= 2:
                        t.wait_ge(bb.sem(sv), bv + hm - 1)
                    for h5 in range(NH5):
                        for k in range(ND):
                            mm = t.matmul(
                                pu[:, i, 0, h5 * 512:(h5 + 1) * 512],
                                w1c[i][:, k, :],
                                xtc[:, k, h5 * 512:(h5 + 1) * 512],
                                start=(k == 0), stop=(k == ND - 1))
                    mm.then_inc(bb.sem(sm), 1)
                    bb.inc(sm)
                    for h5 in range(NH5):
                        for k in range(ND):
                            mm = t.matmul(
                                pu[:, i, 1, h5 * 512:(h5 + 1) * 512],
                                w3c[i][:, k, :],
                                xtc[:, k, h5 * 512:(h5 + 1) * 512],
                                start=(k == 0), stop=(k == ND - 1))
                    mm.then_inc(bb.sem(sm), 1)
                    bb.inc(sm)

            @blk.scalar
            def _(s):
                for hm in range(NHM):
                    i = hm % 2
                    s.wait_ge(bb.sem(sm), bm + 2 * hm + 1)
                    s.activation(s1b[i][:, :], pu[:, i, 0, :], AF.Silu
                                 ).then_inc(bb.sem(ss), 1)
                    bb.inc(ss)

            @blk.vector
            def _(v):
                for hm in range(NHM):
                    i = hm % 2
                    v.wait_ge(bb.sem(ss), bs + hm + 1)
                    v.wait_ge(bb.sem(sm), bm + 2 * hm + 2)
                    v.tensor_tensor(hA[:, hm, :], s1b[i][:, :], pu[:, i, 1, :],
                                    op=ALU.mult).then_inc(bb.sem(sv), 1)
                    bb.inc(sv)

    with ExitStack() as st:
        pf = st.enter_context(nc.psum_tensor(un + "pf", [128, 2, TC], F32))
        sm, ss = "BFm", "BFs"
        bb.sem(sm), bb.sem(ss)
        bm, bs = bb.cnt[sm], bb.cnt[ss]
        with nc.Block() as blk:
            @blk.tensor
            def _(t):
                for pm in range(NHM):
                    if pm >= 2:
                        t.wait_ge(bb.sem(ss), bs + pm - 1)
                    for h5 in range(NH5):
                        mm = t.matmul(pf[:, pm % 2, h5 * 512:(h5 + 1) * 512],
                                      h128b[:, :],
                                      hA[:, pm, h5 * 512:(h5 + 1) * 512],
                                      start=True, stop=True)
                    mm.then_inc(bb.sem(sm), 1)
                    bb.inc(sm)

            @blk.scalar
            def _(s):
                for pm in range(NHM):
                    s.wait_ge(bb.sem(sm), bm + pm + 1)
                    s.copy(hA[:, pm, :], pf[:, pm % 2, :]
                           ).then_inc(bb.sem(ss), 1)
                    bb.inc(ss)

    with ExitStack() as st:
        e = st.enter_context
        w2c = [e(nc.sbuf_tensor(f"{un}w2c{i}", [128, NHM, 128], BF16))
               for i in range(2)]
        orow = [e(nc.sbuf_tensor(f"{un}or{i}", [128, TC], F32))
                for i in range(2)]
        pd = e(nc.psum_tensor(un + "pd", [128, 2, TC], F32))
        sw2 = [bb.slot("F1dw2%d" % i) for i in range(2)]
        sor = [bb.slot("F1dor%d" % i) for i in range(2)]
        sm, ss = "BDm", "BDs"
        bb.sem(sm), bb.sem(ss)
        bm, bs = bb.cnt[sm], bb.cnt[ss]
        w2f = io[pfx + "_w2f"]

        with nc.Block() as blk:
            @blk.sync
            def _(g):
                for i in range(2):
                    sw2[i].put(g, w2c[i][:, :, :],
                               w2f[:, i * 128:(i + 1) * 128].rearrange(
                                   "(kt p) n -> p kt n", p=128))
                for md in range(ND):
                    i = md % 2
                    g.wait_ge(bb.sem(ss), bs + md + 1)
                    sor[i].put(g, acc[md, :, chunk * TC:(chunk + 1) * TC],
                               orow[i][:, :])
                    if md + 2 < ND:
                        g.wait_ge(bb.sem(sm), bm + md + 1)
                        sw2[i].put(g, w2c[i][:, :, :],
                                   w2f[:, (md + 2) * 128:(md + 3) * 128
                                       ].rearrange("(kt p) n -> p kt n",
                                                   p=128))
                sor[0].wait(g)
                sor[1].wait(g)

            @blk.tensor
            def _(t):
                for md in range(ND):
                    i = md % 2
                    sw2[i].wait_rel(t, 16 * (md // 2 + 1))
                    if md >= 2:
                        t.wait_ge(bb.sem(ss), bs + md - 1)
                    for h5 in range(NH5):
                        for k in range(NHM):
                            mm = t.matmul(pd[:, i, h5 * 512:(h5 + 1) * 512],
                                          w2c[i][:, k, :],
                                          hA[:, k, h5 * 512:(h5 + 1) * 512],
                                          start=(k == 0), stop=(k == NHM - 1))
                    mm.then_inc(bb.sem(sm), 1)
                    bb.inc(sm)

            @blk.scalar
            def _(s):
                for md in range(ND):
                    i = md % 2
                    s.wait_ge(bb.sem(sm), bm + md + 1)
                    if md >= 2:
                        sor[i].wait_rel(s, 16 * (md // 2))
                    s.copy(orow[i][:, :], pd[:, i, :]).then_inc(bb.sem(ss), 1)
                    bb.inc(ss)


def emit_l0_moe(bb, x_in, x_out, pfx="f0"):
    nc = bb.nc
    io = bb.io
    xt = io[pfx + "_xt"]
    acc = io[pfx + "_acc"]
    NHT = HID0 // 128

    emit_preln_xt(bb, x_in, xt, pfx + "A")

    with ExitStack() as st0:
        e0 = st0.enter_context
        gT = e0(nc.sbuf_tensor(pfx + "gT", [8, S], F32))
        onesf = e0(nc.sbuf_tensor(pfx + "onesf", [1, 128], F32))
        idf = e0(nc.sbuf_tensor(pfx + "idf", [128, 128], F32))
        id8 = e0(nc.sbuf_tensor(pfx + "id8", [8, 8], F32))
        sor = [bb.slot("F0dacc%d" % i) for i in range(2)]
        scn = bb.slot("F0dcn")
        with nc.Block() as blk:
            @blk.sync
            def _(g):
                scn.put(g, onesf[:, :], io["c_ones_row_f"][:, :])
                scn.put(g, idf[:, :], io["c_identf"][:, :])
                scn.put(g, id8[:, :], io["c_ident8f"][:, :])
                scn.wait(g)

        _moe_gate(bb, xt, gT, idf, pfx)

        for ex in range(NE + 1):
            _moe_expert(bb, ex, xt, gT, onesf, id8, acc, sor, pfx, NHT)

    emit_scr_out_phase(bb, x_in, x_out, acc, pfx + "E")


def _moe_gate(bb, xt, gT, idf, pfx):
    nc = bb.nc
    io = bb.io
    un = pfx + bb.u()
    with ExitStack() as st:
        e = st.enter_context
        gwb = e(nc.sbuf_tensor(un + "gwb", [128, ND, 8], BF16))
        ebias = e(nc.sbuf_tensor(un + "ebias", [128, 8], F32))
        xg = [e(nc.sbuf_tensor(f"{un}xg{i}", [128, ND, 128], BF16))
              for i in range(2)]
        ge = e(nc.sbuf_tensor(un + "ge", [128, 2, 8], F32))
        gst = e(nc.sbuf_tensor(un + "gst", [128, 2, 8], F32))
        gsc = e(nc.sbuf_tensor(un + "gsc", [128, 2, 8], F32))
        gpb = e(nc.sbuf_tensor(un + "gpb", [128, 2, 8], F32))
        gw2 = e(nc.sbuf_tensor(un + "gw2", [128, 2, 8], F32))
        gall = e(nc.sbuf_tensor(un + "gall", [128, NT, 8], F32))
        pg = e(nc.psum_tensor(un + "pg", [128, 2, 8], F32))
        pgt = e(nc.psum_tensor(un + "pgt", [8, 128], F32))
        sgw = bb.slot("F0dgw")
        sxg = [bb.slot("F0dxg%d" % i) for i in range(2)]
        sm, ss, sv = "BGm", "BGs", pfx + "Gv"
        for _x in (sm, ss, sv):
            bb.sem(_x)
        bm, bs, bv = bb.cnt[sm], bb.cnt[ss], bb.cnt[sv]

        with nc.Block() as blk:
            @blk.gpsimd
            def _(g):
                sgw.put(g, ebias[:, :], io[pfx + "_ebias"][:, :])
                sgw.put(g, gwb[:, :, :],
                        io[pfx + "_gate"].rearrange("(kt p) n -> p kt n", p=128))

            @blk.sync
            def _(g):
                for m in range(NT):
                    if m >= 2:
                        g.wait_ge(bb.sem(sm), bm + m - 1)
                    sxg[m % 2].put(
                        g, xg[m % 2][:, :, :],
                        xt[:, m // 2, :, (m % 2) * 128:(m % 2) * 128 + 128
                           ].rearrange("k p f -> p k f"))

            @blk.tensor
            def _(t):
                sgw.wait(t)
                for m in range(NT):
                    sxg[m % 2].wait_rel(t, 16 * (m // 2 + 1))
                    if m >= 2:
                        t.wait_ge(bb.sem(ss), bs + m - 1)
                    for k in range(ND):
                        mm = t.matmul(pg[:, m % 2, :], xg[m % 2][:, k, :],
                                      gwb[:, k, :],
                                      start=(k == 0), stop=(k == ND - 1))
                    mm.then_inc(bb.sem(sm), 1)
                    bb.inc(sm)

            @blk.scalar
            def _(s):
                for m in range(NT):
                    i = m % 2
                    s.wait_ge(bb.sem(sm), bm + m + 1)
                    if m >= 2:
                        s.wait_ge(bb.sem(sv), bv + m - 1)
                    s.activation(ge[:, i, :], pg[:, i, :], AF.Exp,
                                 accum_out=gst[:, i, 0:1]).then_inc(bb.sem(ss), 1)
                    bb.inc(ss)

            @blk.vector
            def _(v):
                for m in range(NT):
                    i = m % 2
                    v.wait_ge(bb.sem(ss), bs + m + 1)
                    v.reciprocal(gst[:, i, 1:2], gst[:, i, 0:1])
                    v.drain()
                    v.tensor_scalar(gsc[:, i, :], ge[:, i, :], gst[:, i, 1:2],
                                    None, op0=ALU.mult)
                    v.drain()
                    v.tensor_tensor(gpb[:, i, :], gsc[:, i, :], ebias[:, :],
                                    op=ALU.add)
                    v.drain()
                    v.reduce_max(gst[:, i, 2:3], gpb[:, i, :], axis=AX.X)
                    v.drain()
                    v.tensor_scalar(gw2[:, i, :], gpb[:, i, :], gst[:, i, 2:3],
                                    None, op0=ALU.is_equal)
                    v.drain()
                    v.scalar_tensor_tensor(gw2[:, i, :], gw2[:, i, :], -1e30,
                                           gpb[:, i, :], op0=ALU.mult,
                                           op1=ALU.add)
                    v.drain()
                    v.reduce_max(gst[:, i, 3:4], gw2[:, i, :], axis=AX.X)
                    v.drain()
                    v.tensor_scalar(gw2[:, i, :], gpb[:, i, :], gst[:, i, 3:4],
                                    None, op0=ALU.is_ge)
                    v.drain()
                    v.tensor_tensor(gall[:, m, :], gsc[:, i, :], gw2[:, i, :],
                                    op=ALU.mult).then_inc(bb.sem(sv), 1)
                    bb.inc(sv)

        bm, bv = bb.cnt[sm], bb.cnt[sv]
        with nc.Block() as blk:
            @blk.tensor
            def _(t):
                for m in range(NT):
                    if m >= 1:
                        t.wait_ge(bb.sem(sv), bv + m)
                    t.transpose(pgt[:, :], gall[:, m, :], idf[:, :]
                                ).then_inc(bb.sem(sm), 1)
                    bb.inc(sm)

            @blk.vector
            def _(v):
                for m in range(NT):
                    v.wait_ge(bb.sem(sm), bm + m + 1)
                    v.tensor_copy(gT[0:8, m * 128:(m + 1) * 128], pgt[0:8, :]
                                  ).then_inc(bb.sem(sv), 1)
                    bb.inc(sv)


def _moe_expert(bb, ex, xt, gT, onesf, id8, acc, sor, pfx, NHT):
    nc = bb.nc
    io = bb.io
    un = pfx + bb.u()
    shared = (ex == NE)

    def w1_ap(hm):
        if shared:
            return io[pfx + "_sw1"][:, hm * 128:(hm + 1) * 128]
        return io[pfx + "_w1"][ex, :, hm * 128:(hm + 1) * 128]

    def w3_ap(hm):
        if shared:
            return io[pfx + "_sw3"][:, hm * 128:(hm + 1) * 128]
        return io[pfx + "_w3"][ex, :, hm * 128:(hm + 1) * 128]

    def w2_ap(md):
        if shared:
            return io[pfx + "_sw2"][:, md * 128:(md + 1) * 128]
        return io[pfx + "_w2"][ex, :, md * 128:(md + 1) * 128]

    with ExitStack() as st:
        e = st.enter_context
        xeT = e(nc.sbuf_tensor(un + "xeT", [128, ND, S], BF16))
        heT = e(nc.sbuf_tensor(un + "heT", [128, NHT, S], BF16))

        with ExitStack() as stg:
            eg = stg.enter_context
            gb = eg(nc.sbuf_tensor(un + "gb", [128, S], F32))
            grow = eg(nc.sbuf_tensor(un + "grow", [1, S], F32))
            xrow = [eg(nc.sbuf_tensor(f"{un}xr{i}", [128, S], BF16))
                    for i in range(2)]
            pgb = eg(nc.psum_tensor(un + "pgb", [128, S], F32))
            sxr = [bb.slot("F0dxr%d" % i) for i in range(2)]
            sm, ss, sv = "BXm", "BXs", "BXv"
            for _x in (sm, ss, sv):
                bb.sem(_x)
            bm, bs, bv = bb.cnt[sm], bb.cnt[ss], bb.cnt[sv]

            if not shared:
                with nc.Block() as blk:
                    @blk.tensor
                    def _(t):
                        for nb in range(NB):
                            mm = t.matmul(pgb[0:1, nb * 512:(nb + 1) * 512],
                                          id8[:, ex:ex + 1],
                                          gT[0:8, nb * 512:(nb + 1) * 512],
                                          start=True, stop=True,
                                          skip_group_check=True)
                        mm.then_inc(bb.sem(sm), 1)
                        bb.inc(sm)

                    @blk.scalar
                    def _(s):
                        s.wait_ge(bb.sem(sm), bb.cnt[sm])
                        s.copy(grow[:, :], pgb[0:1, :]).then_inc(bb.sem(ss), 1)
                        bb.inc(ss)

                with nc.Block() as blk:
                    @blk.tensor
                    def _(t):
                        for nb in range(NB):
                            mm = t.matmul(pgb[:, nb * 512:(nb + 1) * 512],
                                          onesf[0:1, :],
                                          grow[0:1, nb * 512:(nb + 1) * 512],
                                          start=True, stop=True,
                                          skip_group_check=True)
                        mm.then_inc(bb.sem(sm), 1)
                        bb.inc(sm)

                    @blk.scalar
                    def _(s):
                        s.wait_ge(bb.sem(sm), bb.cnt[sm])
                        s.copy(gb[:, :], pgb[:, :]).then_inc(bb.sem(ss), 1)
                        bb.inc(ss)

                with nc.Block() as blk:
                    @blk.sync
                    def _(g):
                        for k in range(ND):
                            if k >= 2:
                                g.wait_ge(bb.sem(sv), bv + k - 1)
                            sxr[k % 2].put(
                                g, xrow[k % 2][:, :],
                                xt[k].rearrange("t p f -> p t f"))

                    @blk.vector
                    def _(v):
                        for k in range(ND):
                            sxr[k % 2].wait_rel(v, 16 * (k // 2 + 1))
                            v.tensor_tensor(xeT[:, k, :], xrow[k % 2][:, :],
                                            gb[:, :], op=ALU.mult
                                            ).then_inc(bb.sem(sv), 1)
                            bb.inc(sv)
            else:
                with nc.Block() as blk:
                    @blk.sync
                    def _(g):
                        for k in range(ND):
                            sxr[0].put(g, xeT[:, k, :],
                                       xt[k].rearrange("t p f -> p t f"))
                        sxr[0].wait(g)

        with ExitStack() as stu:
            eu = stu.enter_context
            w1c = [eu(nc.sbuf_tensor(f"{un}w1c{i}", [128, ND, 128], BF16))
                   for i in range(2)]
            w3c = [eu(nc.sbuf_tensor(f"{un}w3c{i}", [128, ND, 128], BF16))
                   for i in range(2)]
            s1b = [eu(nc.sbuf_tensor(f"{un}s1b{i}", [128, S], BF16))
                   for i in range(2)]
            pu = eu(nc.psum_tensor(un + "pu", [128, 2, S], F32))
            sw1 = [bb.slot("F0dw1%d" % i) for i in range(2)]
            sw3 = [bb.slot("F0dw3%d" % i) for i in range(2)]
            sm, ss, sv = "BUm", "BUs", "BUv"
            for _x in (sm, ss, sv):
                bb.sem(_x)
            bm, bs, bv = bb.cnt[sm], bb.cnt[ss], bb.cnt[sv]

            with nc.Block() as blk:
                @blk.gpsimd
                def _(g):
                    for hm in range(NHT):
                        if hm >= 2:
                            g.wait_ge(bb.sem(sm), bm + 2 * (hm - 1))
                        i = hm % 2
                        w1a, w3a = w1_ap(hm), w3_ap(hm)
                        sw1[i].put(g, w1c[i][:, :, :],
                                   w1a.rearrange("(kt p) n -> p kt n", p=128))
                        sw3[i].put(g, w3c[i][:, :, :],
                                   w3a.rearrange("(kt p) n -> p kt n", p=128))

                @blk.tensor
                def _(t):
                    for hm in range(NHT):
                        i = hm % 2
                        sw1[i].wait_rel(t, 16 * (hm // 2 + 1))
                        sw3[i].wait_rel(t, 16 * (hm // 2 + 1))
                        if hm >= 1:
                            t.wait_ge(bb.sem(sv), bv + hm)
                        for k in range(ND):
                            for nb in range(NB):
                                mm = t.matmul(
                                    pu[:, 0, nb * 512:(nb + 1) * 512],
                                    w1c[i][:, k, :],
                                    xeT[:, k, nb * 512:(nb + 1) * 512],
                                    start=(k == 0), stop=(k == ND - 1),
                                    skip_group_check=True)
                        mm.then_inc(bb.sem(sm), 1)
                        bb.inc(sm)
                        for k in range(ND):
                            for nb in range(NB):
                                mm = t.matmul(
                                    pu[:, 1, nb * 512:(nb + 1) * 512],
                                    w3c[i][:, k, :],
                                    xeT[:, k, nb * 512:(nb + 1) * 512],
                                    start=(k == 0), stop=(k == ND - 1),
                                    skip_group_check=True)
                        mm.then_inc(bb.sem(sm), 1)
                        bb.inc(sm)

                @blk.scalar
                def _(s):
                    for hm in range(NHT):
                        i = hm % 2
                        s.wait_ge(bb.sem(sm), bm + 2 * hm + 1)
                        if hm >= 2:
                            s.wait_ge(bb.sem(sv), bv + hm - 1)
                        s.activation(s1b[i][:, :], pu[:, 0, :], AF.Silu
                                     ).then_inc(bb.sem(ss), 1)
                        bb.inc(ss)

                @blk.vector
                def _(v):
                    for hm in range(NHT):
                        i = hm % 2
                        v.wait_ge(bb.sem(ss), bs + hm + 1)
                        v.wait_ge(bb.sem(sm), bm + 2 * hm + 2)
                        v.tensor_tensor(heT[:, hm, :], s1b[i][:, :],
                                        pu[:, 1, :], op=ALU.mult
                                        ).then_inc(bb.sem(sv), 1)
                        bb.inc(sv)

        with ExitStack() as std:
            ed = std.enter_context
            w2c = [ed(nc.sbuf_tensor(f"{un}w2c{i}", [128, NHT, 128], BF16))
                   for i in range(2)]
            orow = [ed(nc.sbuf_tensor(f"{un}or{i}", [128, S], F32))
                    for i in range(2)]
            pd = ed(nc.psum_tensor(un + "pd", [128, 2, S], F32))
            sw2 = [bb.slot("F0dw2%d" % i) for i in range(2)]
            sm, ss = "BDm", "BDs"
            bb.sem(sm), bb.sem(ss)
            bm, bs = bb.cnt[sm], bb.cnt[ss]
            acc_base = [sor[0].n, sor[1].n]

            with nc.Block() as blk:
                @blk.gpsimd
                def _(g):
                    sor[0].wait(g, acc_base[0])
                    sor[1].wait(g, acc_base[1])
                    for md in range(ND + 2):
                        if md < ND:
                            if md >= 2:
                                g.wait_ge(bb.sem(sm), bm + md - 1)
                            i = md % 2
                            w2a = w2_ap(md)
                            sw2[i].put(g, w2c[i][:, :, :],
                                       w2a.rearrange("(kt p) n -> p kt n",
                                                     p=128))
                        if md >= 2:
                            md_ = md - 2
                            g.wait_ge(bb.sem(ss), bs + md_ + 1)
                            i = md_ % 2
                            if ex == 0:
                                sor[i].put(g, acc[md_], orow[i][:, :])
                            else:
                                sor[i].put(g, acc[md_], orow[i][:, :],
                                           accum_op=ALU.add)
                    sor[0].wait(g)
                    sor[1].wait(g)

                @blk.tensor
                def _(t):
                    for md in range(ND):
                        i = md % 2
                        sw2[i].wait_rel(t, 16 * (md // 2 + 1))
                        if md >= 2:
                            t.wait_ge(bb.sem(ss), bs + md - 1)
                        for k in range(NHT):
                            for nb in range(NB):
                                mm = t.matmul(
                                    pd[:, i, nb * 512:(nb + 1) * 512],
                                    w2c[i][:, k, :],
                                    heT[:, k, nb * 512:(nb + 1) * 512],
                                    start=(k == 0), stop=(k == NHT - 1),
                                    skip_group_check=True)
                        mm.then_inc(bb.sem(sm), 1)
                        bb.inc(sm)

                @blk.scalar
                def _(s):
                    for md in range(ND):
                        i = md % 2
                        s.wait_ge(bb.sem(sm), bm + md + 1)
                        if md >= 2:
                            sor[i].wait(s, acc_base[i] + 16 * (md // 2))
                        s.copy(orow[i][:, :], pd[:, i, :]).then_inc(bb.sem(ss), 1)
                        bb.inc(ss)


def _prep_all(inputs):
    perm = _rope_perm()
    w = _prep_l0_attn(inputs, perm)
    f32 = lambda k: np.asarray(inputs[k], np.float32)
    w["f0_gate"] = np.ascontiguousarray(f32("L0_ffn_gate_w"))
    w["f0_w1"] = np.ascontiguousarray(f32("L0_ffn_w1"))
    w["f0_w2"] = np.ascontiguousarray(f32("L0_ffn_w2"))
    w["f0_w3"] = np.ascontiguousarray(f32("L0_ffn_w3"))
    w["f0_sw1"] = np.ascontiguousarray(f32("L0_ffn_sw1"))
    w["f0_sw2"] = np.ascontiguousarray(f32("L0_ffn_sw2"))
    w["f0_sw3"] = np.ascontiguousarray(f32("L0_ffn_sw3"))
    w["f0_ebias"] = np.ascontiguousarray(
        np.broadcast_to(f32("L0_ffn_expert_bias")[None, :], (128, NE)))
    w["m1_wvT"] = np.ascontiguousarray(f32("L1_attn_Wv").T)
    w["m1_woT"] = np.ascontiguousarray(f32("L1_attn_Wo").T)
    w["m1_bv"] = np.ascontiguousarray(f32("L1_attn_bv")[None, :])
    w["m1_bo"] = np.ascontiguousarray(f32("L1_attn_bo")[None, :])
    w["f1_w1T"] = np.ascontiguousarray(f32("L1_ffn_W1").T)
    w["f1_w3T"] = np.ascontiguousarray(f32("L1_ffn_W3").T)
    w["f1_w2T"] = np.ascontiguousarray(f32("L1_ffn_W2").T)
    return w


_SHAPES = {
    "a0_wqT": [D, D], "a0_wkT": [D, D], "a0_wvT": [D, D], "a0_woT": [D, D],
    "f0_gate": [D, NE], "f0_w1": [NE, D, HID0], "f0_w2": [NE, HID0, D],
    "f0_w3": [NE, D, HID0], "f0_sw1": [D, HID0], "f0_sw2": [HID0, D],
    "f0_sw3": [D, HID0], "f0_ebias": [128, NE],
    "m1_wvT": [D, D], "m1_woT": [D, D], "m1_bv": [1, D], "m1_bo": [1, D],
    "f1_w1T": [D, HID1], "f1_w3T": [D, HID1], "f1_w2T": [HID1, D],
}

_BLOCK_WEIGHTS = {
    "a0": ["a0_wqT", "a0_wkT", "a0_wvT", "a0_woT"],
    "f0": ["f0_gate", "f0_w1", "f0_w2", "f0_w3", "f0_sw1", "f0_sw2", "f0_sw3",
           "f0_ebias"],
    "m1": ["m1_wvT", "m1_woT", "m1_bv", "m1_bo"],
    "f1": ["f1_w1T", "f1_w3T", "f1_w2T"],
}

_EMITTERS = {"a0": emit_l0_attn, "f0": emit_l0_moe, "m1": emit_l1_mix,
             "f1": emit_l1_ffn}


def _declare_block_io(nc, bb, pfx):
    for nm in _BLOCK_WEIGHTS[pfx]:
        bb.io[nm] = nc.dram_tensor(nm, _SHAPES[nm], F32,
                                   kind="ExternalInput").ap()
    bb.io[pfx + "_xt"] = nc.dram_tensor(pfx + "_xt", [ND, NT2, 128, 256], BF16,
                                        kind="Internal").ap()
    if pfx == "a0":
        bb.io["a0_vscr"] = nc.dram_tensor("a0_vscr", [NH, NT, 128, 128], BF16,
                                          kind="Internal").ap()
    if pfx in ("f0", "f1"):
        bb.io[pfx + "_acc"] = nc.dram_tensor(pfx + "_acc", [ND, 128, S], F32,
                                             kind="Internal").ap()
    if pfx == "f1":
        bb.io["f1_w2f"] = nc.dram_tensor("f1_w2f", [HID1, D], BF16,
                                         kind="Internal").ap()


def build_block(pfx):
    nc = bass.Bass("TRN2", target_bir_lowering=False)
    with ExitStack() as ctx:
        bb = BB(nc, ctx)
        x_in = nc.dram_tensor("x", [S, D], F32, kind="ExternalInput").ap()
        x_out = nc.dram_tensor("y", [S, D], F32, kind="ExternalOutput").ap()
        _declare_common(nc, bb)
        _declare_block_io(nc, bb, pfx)
        _EMITTERS[pfx](bb, x_in, x_out, pfx)
    return nc


_NC_CACHE = {}


def kernel_unfused(**inputs):
    x = np.asarray(inputs["L0_attn_x"], np.float32)
    ncores = x.shape[0]
    wmap = _prep_all(inputs)
    cmap = {"c_" + k: v for k, v in _consts().items()}

    cur = [np.ascontiguousarray(x[c]) for c in range(ncores)]
    for pfx in ["a0", "f0", "m1", "f1"]:
        if pfx not in _NC_CACHE:
            _NC_CACHE[pfx] = build_block(pfx)
        nc = _NC_CACHE[pfx]
        in_maps = []
        for c in range(ncores):
            m = {"x": cur[c]}
            for nm in _BLOCK_WEIGHTS[pfx]:
                m[nm] = wmap[nm]
            m.update(cmap)
            in_maps.append(m)
        res = run_bass_kernel_spmd(nc, in_maps, core_ids=list(range(ncores)))
        cur = [np.asarray(res.results[c]["y"]) for c in range(ncores)]
    return np.stack(cur).astype(np.float32)


def build_fused():
    nc = bass.Bass("TRN2", target_bir_lowering=False)
    with ExitStack() as ctx:
        bb = BB(nc, ctx)
        x_in = nc.dram_tensor("x", [S, D], F32, kind="ExternalInput").ap()
        y = nc.dram_tensor("y", [S, D], F32, kind="ExternalOutput").ap()
        x1 = nc.dram_tensor("x1", [S, D], F32, kind="Internal").ap()
        x2 = nc.dram_tensor("x2", [S, D], F32, kind="Internal").ap()
        x3 = nc.dram_tensor("x3", [S, D], F32, kind="Internal").ap()
        _declare_common(nc, bb)
        for pfx in ("a0", "f0", "m1", "f1"):
            _declare_block_io(nc, bb, pfx)
        emit_l0_attn(bb, x_in, x1, "a0")
        emit_l0_moe(bb, x1, x2, "f0")
        emit_l1_mix(bb, x2, x3, "m1")
        emit_l1_ffn(bb, x3, y, "f1")
    return nc


def kernel(**inputs):
    x = np.asarray(inputs["L0_attn_x"], np.float32)
    ncores = x.shape[0]
    wmap = _prep_all(inputs)
    cmap = {"c_" + k: v for k, v in _consts().items()}

    if "fused" not in _NC_CACHE:
        _NC_CACHE["fused"] = build_fused()
    nc = _NC_CACHE["fused"]
    in_maps = []
    for c in range(ncores):
        m = {"x": np.ascontiguousarray(x[c])}
        m.update(wmap)
        m.update(cmap)
        in_maps.append(m)
    res = run_bass_kernel_spmd(nc, in_maps, core_ids=list(range(ncores)))
    return np.stack([np.asarray(res.results[c]["y"]) for c in range(ncores)]
                    ).astype(np.float32)


# version 07 (final)
# speedup vs baseline: 1.8921x; 1.0690x over previous
import numpy as np
import ml_dtypes
from contextlib import ExitStack

import concourse.bass as bass
import concourse.mybir as mybir
from concourse.bass_utils import run_bass_kernel_spmd

F32 = mybir.dt.float32
BF16 = mybir.dt.bfloat16
AF = mybir.ActivationFunctionType
ALU = mybir.AluOpType
AX = mybir.AxisListType

S = 2048
D = 2048
NH = 16
DH = 128
RATIO = 16
NCMP = S // RATIO
WIN = 256
SCALE = DH ** -0.5
HID0 = 1792
NE = 8
HID1 = 8192
EPS = 1e-5
NT = S // 128
ND = D // 128
NT2 = S // 256
NB = D // 512

_BF = ml_dtypes.bfloat16
_CONST_CACHE = {}


def _consts():
    if _CONST_CACHE:
        return _CONST_CACHE
    c = {}
    c["ident"] = np.eye(128, dtype=_BF)
    c["identf"] = np.eye(128, dtype=np.float32)
    c["epsf"] = np.full((128, 1), EPS, dtype=np.float32)
    c["ones_rowS"] = np.ones((1, S), dtype=_BF)
    c["ones_row_f"] = np.ones((1, 128), dtype=np.float32)
    c["ident8f"] = np.eye(8, dtype=np.float32)
    c["ones_col"] = np.ones((128, 1), dtype=_BF)
    c["ones_row"] = np.ones((1, 128), dtype=_BF)
    A = np.zeros((S, NCMP), dtype=np.float32)
    A[np.arange(S), np.arange(S) // RATIO] = 1.0 / RATIO
    c["avg"] = A.astype(_BF)
    inv = 1.0 / (10000.0 ** (np.arange(0, DH, 2, dtype=np.float64) / DH))
    ang = np.outer(inv, np.arange(S, dtype=np.float64))
    cos = np.cos(ang).astype(np.float32)
    sin = np.sin(ang).astype(np.float32)
    c["cs1"] = np.concatenate([cos, sin], axis=0).astype(_BF)
    c["cs2"] = np.concatenate([sin, cos], axis=0).astype(_BF)
    i64 = np.eye(64, dtype=_BF)
    c["rl_e"] = np.concatenate([i64, -i64], axis=0)
    c["rl_o"] = np.concatenate([i64, i64], axis=0)
    p = np.arange(128)[:, None]
    f = np.arange(128)[None, :]
    c["m_w0"] = (f >= p).astype(_BF)
    c["m_w2"] = (f < p).astype(_BF)
    c["m_ones"] = np.ones((128, 128), dtype=_BF)
    c["m_zero"] = np.zeros((128, 128), dtype=_BF)
    v = np.arange(248)[:, None] - 120
    fq = np.arange(128)[None, :]
    c["m_cmp"] = (v < (fq + 1) // 16).astype(_BF)
    i = np.arange(128)
    pc = np.zeros((128, 128), dtype=np.int64)
    for b in range(7):
        pc += ((i[:, None] >> b) & 1) * ((i[None, :] >> b) & 1)
    c["h128"] = (np.where(pc % 2 == 0, 1.0, -1.0) / np.sqrt(8192.0)).astype(_BF)
    i6 = np.arange(64)
    pc6 = np.zeros((64, 64), dtype=np.int64)
    for b in range(6):
        pc6 += ((i6[:, None] >> b) & 1) * ((i6[None, :] >> b) & 1)
    h64 = np.where(pc6 % 2 == 0, 1.0, -1.0)
    h64d = np.zeros((128, 128), dtype=np.float64)
    h64d[:64, :64] = h64
    h64d[64:, 64:] = h64
    c["h64d"] = h64d.astype(_BF)
    _CONST_CACHE.update(c)
    return c


def _rope_perm():
    perm = np.zeros(D, dtype=np.int64)
    for h in range(NH):
        b = h * DH
        perm[b:b + 64] = b + 2 * np.arange(64)
        perm[b + 64:b + 128] = b + 2 * np.arange(64) + 1
    return perm


class Slot:

    def __init__(self, bb, name):
        self.sem = bb.ctx.enter_context(bb.nc.semaphore(name))
        self.n = 0
        self.base = 0

    def rebase(self):
        self.base = self.n
        return self

    def put(self, eng, out, in_, **kw):
        eng.dma_start(out, in_, **kw).then_inc(self.sem, 16)
        self.n += 16
        return self.n

    def wait(self, eng, val=None):
        eng.wait_ge(self.sem, self.n if val is None else val)

    def wait_rel(self, eng, rel):
        eng.wait_ge(self.sem, self.base + rel)


class BB:
    def __init__(self, nc, ctx):
        self.nc = nc
        self.ctx = ctx
        self.sems = {}
        self.cnt = {}
        self.io = {}
        self._n = 0

    def u(self):
        self._n += 1
        return "u%d" % self._n

    def sem(self, name):
        if name not in self.sems:
            self.sems[name] = self.ctx.enter_context(self.nc.semaphore(name))
            self.cnt[name] = 0
        return self.sems[name]

    def inc(self, name, by=1):
        self.cnt[name] += by
        return self.cnt[name]

    def slot(self, name):
        if name not in self.sems:
            self.sems[name] = Slot(self, name)
        return self.sems[name].rebase()


def emit_preln_xt(bb, x_in, xt_scr, pfx, xtr=None):
    nc = bb.nc
    ss, sm, sv = "PNs", "PNm", "PNv"
    for _s in (ss, sm, sv):
        bb.sem(_s)
    bs, bm, bv = bb.cnt[ss], bb.cnt[sm], bb.cnt[sv]
    with ExitStack() as st:
        e = st.enter_context
        xs = [e(nc.sbuf_tensor(f"{pfx}xs{i}", [128, D], F32)) for i in range(2)]
        sq = [e(nc.sbuf_tensor(f"{pfx}sq{i}", [128, 2, D], BF16)) for i in range(2)]
        xln = [e(nc.sbuf_tensor(f"{pfx}xln{i}", [128, D], BF16)) for i in range(2)]
        if xtr is None:
            xtrow = e(nc.sbuf_tensor(f"{pfx}xtrow", [128, ND, 256], BF16))
        idsb = e(nc.sbuf_tensor(f"{pfx}id", [128, 128], BF16))
        epsb = e(nc.sbuf_tensor(f"{pfx}eps", [128, 1], F32))
        stt = e(nc.sbuf_tensor(f"{pfx}stt", [128, 2, 8], F32))
        ptp = e(nc.psum_tensor(f"{pfx}ptp", [128, 2, 32, 128], BF16))
        sx = [bb.slot("PNdx%d" % i) for i in range(2)]
        sxt = bb.slot("PNdt")
        scn = bb.slot("PNdc")

        with nc.Block() as blk:
            @blk.sync
            def _(g):
                scn.put(g, epsb[:, :], bb.io["c_epsf"][:, :])
                scn.put(g, idsb[:, :], bb.io["c_ident"][:, :])
                for m2 in range(NT2 + 1):
                    if m2 < NT2:
                        for sub in range(2):
                            m = 2 * m2 + sub
                            if m >= 2:
                                g.wait_ge(bb.sem(ss), bs + m - 1)
                            sx[m % 2].put(g, xs[m % 2][:, :],
                                          x_in[m * 128:(m + 1) * 128, :])
                    if xtr is None and m2 >= 1:
                        g.wait_ge(bb.sem(sv), bv + m2)
                        sxt.put(g, xt_scr[:, m2 - 1].rearrange("d p f -> p d f"),
                                xtrow[:, :, :])
                if xtr is None:
                    sxt.wait(g)

            @blk.scalar
            def _(s):
                scn.wait_rel(s, 32)
                for m in range(NT):
                    i = m % 2
                    s1, s2 = stt[:, i, 0:1], stt[:, i, 1:2]
                    mean, s2d = stt[:, i, 2:3], stt[:, i, 3:4]
                    tvd, var = stt[:, i, 4:5], stt[:, i, 5:6]
                    lnv, rstd = stt[:, i, 6:7], stt[:, i, 7:8]
                    sx[i].wait_rel(s, 16 * (m // 2 + 1))
                    s.activation(sq[i][:, 0, :], xs[i][:, :], AF.Copy,
                                 accum_out=s1)
                    s.activation(sq[i][:, 1, :], xs[i][:, :], AF.Square,
                                 accum_out=s2)
                    s.drain()
                    s.activation(mean, s1, AF.Copy, scale=1.0 / D)
                    s.activation(s2d, s2, AF.Copy, scale=1.0 / D)
                    s.drain()
                    s.activation(tvd, mean, AF.Square)
                    s.drain()
                    s.activation(var, tvd, AF.Identity, scale=-1.0, bias=s2d)
                    s.drain()
                    s.activation(lnv, var, AF.Ln, bias=epsb[:, :])
                    s.drain()
                    s.activation(rstd, lnv, AF.Exp, scale=-0.5)
                    s.drain()
                    s.activation(tvd, rstd, AF.Copy, scale=-1.0)
                    s.drain()
                    s.activation(s2d, mean, AF.Copy, scale=tvd)
                    s.drain()
                    if m >= 2:
                        s.wait_ge(bb.sem(sm), bm + m // 2)
                    s.activation(xln[i][:, :], xs[i][:, :], AF.Identity,
                                 scale=rstd, bias=s2d).then_inc(bb.sem(ss), 1)
                    bb.inc(ss)

            @blk.tensor
            def _(t):
                scn.wait_rel(t, 32)
                for m2 in range(NT2):
                    if m2 >= 2:
                        t.wait_ge(bb.sem(sv), bv + m2 - 1)
                    for sub in range(2):
                        m = 2 * m2 + sub
                        t.wait_ge(bb.sem(ss), bs + m + 1)
                        for dt in range(ND):
                            tp = t.transpose(ptp[:, m2 % 2, sub * ND + dt, :],
                                        xln[m % 2][:, dt * 128:(dt + 1) * 128],
                                        idsb[:, :])
                    tp.then_inc(bb.sem(sm), 1)
                    bb.inc(sm)

            @blk.vector
            def _(v):
                for m2 in range(NT2):
                    v.wait_ge(bb.sem(sm), bm + m2 + 1)
                    if xtr is None and m2 >= 1:
                        sxt.wait_rel(v, 16 * m2)
                    for sub in range(2):
                        for dt in range(ND):
                            dst = (xtrow[:, dt, sub * 128:(sub + 1) * 128]
                                   if xtr is None else
                                   xtr[:, dt, m2 * 256 + sub * 128:
                                       m2 * 256 + (sub + 1) * 128])
                            cp = v.tensor_copy(
                                dst, ptp[:, m2 % 2, sub * ND + dt, :])
                    cp.then_inc(bb.sem(sv), 1)
                    bb.inc(sv)


def emit_out_ln_residual(bb, x_in, x_out, pfx, emit_mm, mm_waits=None,
                         pre_g=None):
    nc = bb.nc
    ss, sm, sv = "OTs", "OTm", "OTv"
    for _s in (ss, sm, sv):
        bb.sem(_s)
    bs, bm, bv = bb.cnt[ss], bb.cnt[sm], bb.cnt[sv]
    with ExitStack() as st:
        e = st.enter_context
        xs = [e(nc.sbuf_tensor(f"{pfx}xs{i}", [128, D], F32)) for i in range(2)]
        sq = [e(nc.sbuf_tensor(f"{pfx}sq{i}", [128, 2, D], BF16)) for i in range(2)]
        tmp = [e(nc.sbuf_tensor(f"{pfx}tmp{i}", [128, D], F32)) for i in range(2)]
        xo = [e(nc.sbuf_tensor(f"{pfx}xo{i}", [128, D], F32)) for i in range(2)]
        epsb = e(nc.sbuf_tensor(f"{pfx}eps", [128, 1], F32))
        stt = e(nc.sbuf_tensor(f"{pfx}stt", [128, 2, 8], F32))
        pE = e(nc.psum_tensor(f"{pfx}pE", [128, 2, D], F32))
        sxi = [bb.slot("OTdi%d" % i) for i in range(2)]
        sxo = [bb.slot("OTdo%d" % i) for i in range(2)]
        sce = bb.slot("OTdce")

        with nc.Block() as blk:
            @blk.sync
            def _(g):
                sce.put(g, epsb[:, :], bb.io["c_epsf"][:, :])
                for m in range(NT + 2):
                    if m < NT:
                        if pre_g is not None:
                            pre_g(g, m)
                        if m >= 2:
                            g.wait_ge(bb.sem(sv), bv + 2 * (m - 1))
                        sxi[m % 2].put(g, xs[m % 2][:, :],
                                       x_in[m * 128:(m + 1) * 128, :])
                    if m >= 2:
                        mm_ = m - 2
                        g.wait_ge(bb.sem(sv), bv + 2 * mm_ + 2)
                        sxo[mm_ % 2].put(g, x_out[mm_ * 128:(mm_ + 1) * 128, :],
                                         xo[mm_ % 2][:, :])
                sxo[0].wait(g)
                sxo[1].wait(g)

            @blk.tensor
            def _(t):
                for m in range(NT):
                    if m >= 2:
                        t.wait_ge(bb.sem(sv), bv + 2 * (m - 1) + 2)
                        t.wait_ge(bb.sem(ss), bs + m - 1)
                    if mm_waits is not None:
                        mm_waits(t, m)
                    last = emit_mm(t, m, pE[:, m % 2, :])
                    last.then_inc(bb.sem(sm), 1)
                    bb.inc(sm)

            @blk.scalar
            def _(s):
                sce.wait_rel(s, 16)
                for m in range(NT):
                    i = m % 2
                    s1, s2 = stt[:, i, 0:1], stt[:, i, 1:2]
                    mean, s2d = stt[:, i, 2:3], stt[:, i, 3:4]
                    tvd, var = stt[:, i, 4:5], stt[:, i, 5:6]
                    lnv, rstd = stt[:, i, 6:7], stt[:, i, 7:8]
                    s.wait_ge(bb.sem(sm), bm + m + 1)
                    s.activation(sq[i][:, 0, :], pE[:, i, :], AF.Copy,
                                 accum_out=s1)
                    s.activation(sq[i][:, 1, :], pE[:, i, :], AF.Square,
                                 accum_out=s2)
                    s.drain()
                    s.activation(mean, s1, AF.Copy, scale=1.0 / D)
                    s.activation(s2d, s2, AF.Copy, scale=1.0 / D)
                    s.drain()
                    s.activation(tvd, mean, AF.Square)
                    s.drain()
                    s.activation(var, tvd, AF.Identity, scale=-1.0, bias=s2d)
                    s.drain()
                    s.activation(lnv, var, AF.Ln, bias=epsb[:, :])
                    s.drain()
                    s.activation(rstd, lnv, AF.Exp, scale=-0.5
                                 ).then_inc(bb.sem(ss), 1)
                    bb.inc(ss)

            @blk.vector
            def _(v):
                for m in range(NT):
                    i = m % 2
                    mean, rstd = stt[:, i, 2:3], stt[:, i, 7:8]
                    v.wait_ge(bb.sem(ss), bs + m + 1)
                    v.tensor_scalar(tmp[i][:, :], pE[:, i, :], mean, rstd,
                                    op0=ALU.subtract, op1=ALU.mult
                                    ).then_inc(bb.sem(sv), 1)
                    bb.inc(sv)
                    v.drain()
                    sxi[i].wait_rel(v, 16 * (m // 2 + 1))
                    if m >= 2:
                        sxo[i].wait_rel(v, 16 * (m // 2))
                    v.tensor_tensor(xo[i][:, :], tmp[i][:, :], xs[i][:, :],
                                    op=ALU.add).then_inc(bb.sem(sv), 1)
                    bb.inc(sv)


def emit_l0_attn(bb, x_in, x_out, pfx="a0"):
    nc = bb.nc
    io = bb.io
    xt = io[pfx + "_xt"]
    v_scr = io[pfx + "_vscr"]

    emit_preln_xt(bb, x_in, xt, pfx + "A")

    with ExitStack() as st0:
        e0 = st0.enter_context
        OT = e0(nc.sbuf_tensor(pfx + "OT", [128, ND, S], BF16))
        onesc = e0(nc.sbuf_tensor(pfx + "onesc", [128, 1], BF16))
        onesr = e0(nc.sbuf_tensor(pfx + "onesr", [1, 128], BF16))
        idsb = e0(nc.sbuf_tensor(pfx + "id", [128, 128], BF16))
        scst = bb.slot("A0dc")

        with nc.Block() as blk:
            @blk.sync
            def _(g):
                scst.put(g, onesc[:, :], io["c_ones_col"][:, :])
                scst.put(g, onesr[:, :], io["c_ones_row"][:, :])
                scst.put(g, idsb[:, :], io["c_ident"][:, :])
                scst.wait(g)

        _l0_v_phase(bb, xt, v_scr, pfx)

        with ExitStack() as stC:
            eC = stC.enter_context
            avgb = eC(nc.sbuf_tensor(pfx + "avg", [128, NT, 128], BF16))
            maskq = eC(nc.sbuf_tensor(pfx + "mq", [128, NT, 512], BF16))
            cs1b = eC(nc.sbuf_tensor(pfx + "cs1", [128, S], BF16))
            cs2b = eC(nc.sbuf_tensor(pfx + "cs2", [128, S], BF16))
            rleb = eC(nc.sbuf_tensor(pfx + "rle", [128, 64], BF16))
            rlob = eC(nc.sbuf_tensor(pfx + "rlo", [128, 64], BF16))
            sck = bb.slot("A0dk")
            svh = bb.slot("A0dv")
            with nc.Block() as blk:
                @blk.sync
                def _(g):
                    sck.put(g, cs1b[:, :], io["c_cs1"][:, :])
                    sck.put(g, cs2b[:, :], io["c_cs2"][:, :])
                    sck.put(g, rleb[:, :], io["c_rl_e"][:, :])
                    sck.put(g, rlob[:, :], io["c_rl_o"][:, :])
                    for j in range(NT):
                        sck.put(g, avgb[:, j, :],
                                io["c_avg"][j * 128:(j + 1) * 128, :])
                    for qt in range(NT):
                        sck.put(g, maskq[:, qt, 0:128],
                                io["c_m_cmp"][120 - 8 * qt:248 - 8 * qt, :])
                        src2 = io["c_m_w2"] if qt >= 2 else io["c_m_zero"]
                        src1 = io["c_m_ones"] if qt >= 1 else io["c_m_zero"]
                        sck.put(g, maskq[:, qt, 128:256], src2[:, :])
                        sck.put(g, maskq[:, qt, 256:384], src1[:, :])
                        sck.put(g, maskq[:, qt, 384:512], io["c_m_w0"][:, :])
                    sck.wait(g)

            for grp in range(NH // 2):
                _l0_attn_group(bb, grp, xt, v_scr, OT, onesc, onesr, idsb,
                               avgb, maskq, cs1b, cs2b, rleb, rlob, svh, pfx)

        with ExitStack() as st:
            wob = st.enter_context(nc.sbuf_tensor(pfx + "wob", [128, ND, D], BF16))
            swo = bb.slot("A0dw")
            with nc.Block() as blk:
                @blk.gpsimd
                def _(g):
                    for k in range(ND):
                        swo.put(g, wob[:, k, :],
                                io[pfx + "_woT"][k * 128:(k + 1) * 128, :])
                    swo.wait(g)

            def emit_mm(t, m, psum):
                for k in range(ND):
                    for nb in range(NB):
                        mm = t.matmul(psum[:, nb * 512:(nb + 1) * 512],
                                 OT[:, k, m * 128:(m + 1) * 128],
                                 wob[:, k, nb * 512:(nb + 1) * 512],
                                 start=(k == 0), stop=(k == ND - 1))
                return mm

            emit_out_ln_residual(bb, x_in, x_out, pfx + "E", emit_mm)


def _l0_v_phase(bb, xt, v_scr, pfx):
    nc = bb.nc
    io = bb.io
    with ExitStack() as st:
        e = st.enter_context
        wvb = e(nc.sbuf_tensor(pfx + "wvb", [128, ND, D], BF16))
        xtb = e(nc.sbuf_tensor(pfx + "xtb", [128, 2, ND, 256], BF16))
        vout = [e(nc.sbuf_tensor(f"{pfx}vo{i}", [128, D], BF16)) for i in range(2)]
        pV = e(nc.psum_tensor(pfx + "pV", [128, 2, D], F32))
        swv = bb.slot("Vdwv")
        sxtb = [bb.slot("Vdxb%d" % i) for i in range(2)]
        svo = [bb.slot("Vdvo%d" % i) for i in range(2)]
        sm, sv = "BVm", "BVv"
        bb.sem(sm), bb.sem(sv)
        bm, bv = bb.cnt[sm], bb.cnt[sv]

        with nc.Block() as blk:
            @blk.gpsimd
            def _(g):
                for k in range(ND):
                    swv.put(g, wvb[:, k, :],
                            io[pfx + "_wvT"][k * 128:(k + 1) * 128, :])

            @blk.sync
            def _(g):
                for m2 in range(NT2 + 1):
                    if m2 < NT2:
                        if m2 >= 2:
                            g.wait_ge(bb.sem(sm), bm + 2 * (m2 - 1))
                        sxtb[m2 % 2].put(g, xtb[:, m2 % 2, :, :],
                                         xt[:, m2].rearrange("k p f -> p k f"))
                    if m2 >= 1:
                        for sub in range(2):
                            m = 2 * (m2 - 1) + sub
                            g.wait_ge(bb.sem(sv), bv + m + 1)
                            svo[m % 2].put(
                                g, v_scr[:, m].rearrange("h p f -> p h f"),
                                vout[m % 2][:, :])
                svo[0].wait(g)
                svo[1].wait(g)

            @blk.tensor
            def _(t):
                swv.wait(t)
                for m2 in range(NT2):
                    sxtb[m2 % 2].wait_rel(t, 16 * (m2 // 2 + 1))
                    for sub in range(2):
                        m = 2 * m2 + sub
                        if m >= 2:
                            t.wait_ge(bb.sem(sv), bv + m - 1)
                        for k in range(ND):
                            for nb in range(NB):
                                mm = t.matmul(pV[:, m % 2, nb * 512:(nb + 1) * 512],
                                         xtb[:, m2 % 2, k, sub * 128:(sub + 1) * 128],
                                         wvb[:, k, nb * 512:(nb + 1) * 512],
                                         start=(k == 0), stop=(k == ND - 1))
                        mm.then_inc(bb.sem(sm), 1)
                        bb.inc(sm)

            @blk.scalar
            def _(s):
                for m in range(NT):
                    s.wait_ge(bb.sem(sm), bm + m + 1)
                    if m >= 2:
                        svo[m % 2].wait_rel(s, 16 * (m // 2))
                    s.copy(vout[m % 2][:, :], pV[:, m % 2, :]
                           ).then_inc(bb.sem(sv), 1)
                    bb.inc(sv)


def _l0_attn_group(bb, grp, xt, v_scr, OT, onesc, onesr, idsb, avgb, maskq,
                   cs1b, cs2b, rleb, rlob, svh, pfx):
    nc = bb.nc
    io = bb.io
    g0 = grp * 2
    un = pfx + bb.u()
    with ExitStack() as st:
        e = st.enter_context
        wqg = e(nc.sbuf_tensor(un + "wqg", [128, ND, 256], BF16))
        wkg = e(nc.sbuf_tensor(un + "wkg", [128, ND, 256], BF16))
        xtq = e(nc.sbuf_tensor(un + "xtq", [128, 2, ND, 512], BF16))
        QT = e(nc.sbuf_tensor(un + "QT", [128, 2, S], BF16))
        KT = e(nc.sbuf_tensor(un + "KT", [128, 2, S], BF16))
        QTr = e(nc.sbuf_tensor(un + "QTr", [128, 2, S], BF16))
        KTr = e(nc.sbuf_tensor(un + "KTr", [128, 2, S], BF16))
        rP = e(nc.sbuf_tensor(un + "rP", [128, S], BF16))
        rP2 = e(nc.sbuf_tensor(un + "rP2", [128, S], BF16))
        swg = bb.slot("Gdw")
        sxq = [bb.slot("Gdq%d" % i) for i in range(2)]

        sm, ss = "BGm", "BGs"
        bb.sem(sm), bb.sem(ss)
        bm, bs = bb.cnt[sm], bb.cnt[ss]

        with ExitStack() as st1:
            pqk = st1.enter_context(
                nc.psum_tensor(un + "pqk", [128, 2, 4, 512], F32))
            with nc.Block() as blk:
                @blk.gpsimd
                def _(g):
                    swg.put(g, wqg[:, :, :],
                            io[pfx + "_wqT"][:, g0 * 128:(g0 + 2) * 128
                                             ].rearrange("(kt p) n -> p kt n",
                                                         p=128))
                    swg.put(g, wkg[:, :, :],
                            io[pfx + "_wkT"][:, g0 * 128:(g0 + 2) * 128
                                             ].rearrange("(kt p) n -> p kt n",
                                                         p=128))

                @blk.sync
                def _(g):
                    for tq in range(4):
                        if tq >= 2:
                            g.wait_ge(bb.sem(sm), bm + tq - 1)
                        sxq[tq % 2].put(g, xtq[:, tq % 2, :, 0:256],
                                        xt[:, 2 * tq].rearrange("k p f -> p k f"))
                        sxq[tq % 2].put(g, xtq[:, tq % 2, :, 256:512],
                                        xt[:, 2 * tq + 1].rearrange(
                                            "k p f -> p k f"))

                @blk.tensor
                def _(t):
                    swg.wait(t)
                    for tq in range(4):
                        sxq[tq % 2].wait_rel(t, 32 * (tq // 2 + 1))
                        if tq >= 2:
                            t.wait_ge(bb.sem(ss), bs + 4 * (tq - 1))
                        for k in range(ND):
                            for hi in range(2):
                                t.matmul(pqk[:, tq % 2, hi, :],
                                         wqg[:, k, hi * 128:(hi + 1) * 128],
                                         xtq[:, tq % 2, k, :],
                                         start=(k == 0), stop=(k == ND - 1))
                                mm = t.matmul(pqk[:, tq % 2, 2 + hi, :],
                                         wkg[:, k, hi * 128:(hi + 1) * 128],
                                         xtq[:, tq % 2, k, :],
                                         start=(k == 0), stop=(k == ND - 1))
                        mm.then_inc(bb.sem(sm), 1)
                        bb.inc(sm)

                @blk.scalar
                def _(s):
                    for tq in range(4):
                        s.wait_ge(bb.sem(sm), bm + tq + 1)
                        s.copy(QT[:, 0, tq * 512:(tq + 1) * 512],
                               pqk[:, tq % 2, 0, :])
                        s.copy(QT[:, 1, tq * 512:(tq + 1) * 512],
                               pqk[:, tq % 2, 1, :])
                        s.copy(KT[:, 0, tq * 512:(tq + 1) * 512],
                               pqk[:, tq % 2, 2, :])
                        s.copy(KT[:, 1, tq * 512:(tq + 1) * 512],
                               pqk[:, tq % 2, 3, :]).then_inc(bb.sem(ss), 4)
                        bb.inc(ss, 4)

        with ExitStack() as str_:
            prr = str_.enter_context(nc.psum_tensor(un + "prr", [128, S], F32))
            smr, svr, ssr = "BRm", "BRv", "BRs"
            bb.sem(smr), bb.sem(svr), bb.sem(ssr)
            bmr, bvr, bsr = bb.cnt[smr], bb.cnt[svr], bb.cnt[ssr]
            cases = [(hi, src, dst) for hi in range(2)
                     for (src, dst) in ((QT, QTr), (KT, KTr))]
            with nc.Block() as blk:
                @blk.vector
                def _(v):
                    for i, (hi, src, dst) in enumerate(cases):
                        if i >= 1:
                            v.wait_ge(bb.sem(smr), bmr + i)
                        v.tensor_tensor(rP[0:64, :], src[0:64, hi, :],
                                        cs1b[0:64, :], op=ALU.mult)
                        v.tensor_tensor(rP[64:128, :], src[64:128, hi, :],
                                        cs1b[64:128, :], op=ALU.mult)
                        v.tensor_tensor(rP2[0:64, :], src[0:64, hi, :],
                                        cs2b[0:64, :], op=ALU.mult)
                        v.tensor_tensor(rP2[64:128, :], src[64:128, hi, :],
                                        cs2b[64:128, :], op=ALU.mult
                                        ).then_inc(bb.sem(svr), 1)
                        bb.inc(svr)

                @blk.tensor
                def _(t):
                    for i, (hi, src, dst) in enumerate(cases):
                        t.wait_ge(bb.sem(svr), bvr + i + 1)
                        if i >= 1:
                            t.wait_ge(bb.sem(ssr), bsr + i)
                        for nb in range(NB):
                            t.matmul(prr[0:64, nb * 512:(nb + 1) * 512],
                                     rleb[:, :], rP[:, nb * 512:(nb + 1) * 512],
                                     start=True, stop=True, skip_group_check=True)
                            mm = t.matmul(prr[64:128, nb * 512:(nb + 1) * 512],
                                     rlob[:, :], rP2[:, nb * 512:(nb + 1) * 512],
                                     start=True, stop=True, skip_group_check=True)
                        mm.then_inc(bb.sem(smr), 1)
                        bb.inc(smr)

                @blk.scalar
                def _(s):
                    for i, (hi, src, dst) in enumerate(cases):
                        s.wait_ge(bb.sem(smr), bmr + i + 1)
                        s.copy(dst[:, hi, :], prr[:, :]).then_inc(bb.sem(ssr), 1)
                        bb.inc(ssr)

        for hi in range(2):
            _l0_attn_head(bb, g0 + hi, hi, QTr, KTr, v_scr, OT, onesc, onesr,
                          idsb, avgb, maskq, svh, pfx)


def _l0_attn_head(bb, h, hi, QTr, KTr, v_scr, OT, onesc, onesr, idsb, avgb,
                  maskq, svh, pfx):
    nc = bb.nc
    sm, ss, sv = "BHm", "BHs", "BHv"
    for _s in (sm, ss, sv):
        bb.sem(_s)
    un = pfx + bb.u()
    with ExitStack() as st:
        e = st.enter_context
        vh = e(nc.sbuf_tensor(un + "vh", [128, NT, 128], BF16))
        ktok = e(nc.sbuf_tensor(un + "ktok", [128, NT, 128], BF16))
        ckb = e(nc.sbuf_tensor(un + "ckb", [128, 128], BF16))
        cvb = e(nc.sbuf_tensor(un + "cvb", [128, 128], BF16))
        sqs = e(nc.sbuf_tensor(un + "sqs", [128, S], BF16))
        nrm = e(nc.sbuf_tensor(un + "nrm", [1, 8], F32))
        c11b = e(nc.sbuf_tensor(un + "c11b", [1, 1], BF16))
        negC = e(nc.sbuf_tensor(un + "negC", [128, 1], F32))

        with nc.Block() as blk:
            @blk.sync
            def _(g):
                svh.put(g, vh[:, :, :], v_scr[h].rearrange("m p f -> p m f"))
                svh.wait(g)

        with ExitStack() as sta:
            ptk = sta.enter_context(nc.psum_tensor(un + "ptk", [128, NT, 128],
                                                   BF16))
            with nc.Block() as blk:
                @blk.tensor
                def _(t):
                    for j in range(NT):
                        tp = t.transpose(ptk[:, j, :],
                                    KTr[:, hi, j * 128:(j + 1) * 128], idsb[:, :])
                    tp.then_inc(bb.sem(sm), 1)
                    bb.inc(sm)

                @blk.vector
                def _(v):
                    v.wait_ge(bb.sem(sm), bb.cnt[sm])
                    for j in range(NT):
                        cp = v.tensor_copy(ktok[:, j, :], ptk[:, j, :])
                    cp.then_inc(bb.sem(sv), 1)
                    bb.inc(sv)

        with ExitStack() as st1:
            e1 = st1.enter_context
            pck = e1(nc.psum_tensor(un + "pck", [128, 128], F32))
            pcv = e1(nc.psum_tensor(un + "pcv", [128, 128], F32))
            pnr = e1(nc.psum_tensor(un + "pnr", [1, S], F32))
            pb1 = e1(nc.psum_tensor(un + "pb1", [128, 1], F32))

            with nc.Block() as blk:
                @blk.tensor
                def _(t):
                    for j in range(NT):
                        t.matmul(pck[:, :], ktok[:, j, :], avgb[:, j, :],
                                 start=(j == 0), stop=(j == NT - 1))
                    for j in range(NT):
                        mm = t.matmul(pcv[:, :], avgb[:, j, :], vh[:, j, :],
                                 start=(j == 0), stop=(j == NT - 1))
                    mm.then_inc(bb.sem(sm), 1)
                    bb.inc(sm)

                @blk.scalar
                def _(s):
                    s.wait_ge(bb.sem(sm), bb.cnt[sm])
                    s.copy(ckb[:, :], pck[:, :])
                    s.copy(cvb[:, :], pcv[:, :]).then_inc(bb.sem(ss), 1)
                    bb.inc(ss)

            for (srcT, slot) in ((QTr, 0), (KTr, 1)):
                with nc.Block() as blk:
                    @blk.scalar
                    def _(s, srcT=srcT):
                        s.activation(sqs[:, :], srcT[:, hi, :], AF.Square
                                     ).then_inc(bb.sem(ss), 1)
                        bb.inc(ss)

                    @blk.tensor
                    def _(t):
                        t.wait_ge(bb.sem(ss), bb.cnt[ss])
                        for nb in range(NB):
                            mm = t.matmul(pnr[0:1, nb * 512:(nb + 1) * 512],
                                     onesc[:, :], sqs[:, nb * 512:(nb + 1) * 512],
                                     start=True, stop=True)
                        mm.then_inc(bb.sem(sm), 1)
                        bb.inc(sm)

                    @blk.vector
                    def _(v, slot=slot):
                        v.wait_ge(bb.sem(sm), bb.cnt[sm])
                        v.reduce_max(nrm[0:1, slot:slot + 1], pnr[0:1, :],
                                     axis=AX.X).then_inc(bb.sem(sv), 1)
                        bb.inc(sv)

            with nc.Block() as blk:
                @blk.scalar
                def _(s):
                    s.activation(sqs[:, 0:128], ckb[:, :], AF.Square
                                 ).then_inc(bb.sem(ss), 1)
                    bb.inc(ss)

                @blk.tensor
                def _(t):
                    t.wait_ge(bb.sem(ss), bb.cnt[ss])
                    t.matmul(pnr[0:1, 0:128], onesc[:, :], sqs[:, 0:128],
                             start=True, stop=True).then_inc(bb.sem(sm), 1)
                    bb.inc(sm)

                @blk.vector
                def _(v):
                    v.wait_ge(bb.sem(sm), bb.cnt[sm])
                    v.reduce_max(nrm[0:1, 2:3], pnr[0:1, 0:128], axis=AX.X)
                    v.drain()
                    v.tensor_tensor(nrm[0:1, 3:4], nrm[0:1, 1:2], nrm[0:1, 2:3],
                                    op=ALU.max)
                    v.drain()
                    v.tensor_tensor(nrm[0:1, 4:5], nrm[0:1, 0:1], nrm[0:1, 3:4],
                                    op=ALU.mult).then_inc(bb.sem(sv), 1)
                    bb.inc(sv)

                @blk.scalar
                def _(s2):
                    s2.wait_ge(bb.sem(sv), bb.cnt[sv])
                    s2.activation(nrm[0:1, 5:6], nrm[0:1, 4:5], AF.Sqrt)
                    s2.drain()
                    s2.activation(nrm[0:1, 6:7], nrm[0:1, 5:6], AF.Copy,
                                  scale=-SCALE).then_inc(bb.sem(ss), 1)
                    bb.inc(ss)

                @blk.vector
                def _(v2):
                    v2.wait_ge(bb.sem(ss), bb.cnt[ss])
                    v2.tensor_copy(c11b[:, :], nrm[0:1, 6:7]
                                   ).then_inc(bb.sem(sv), 1)
                    bb.inc(sv)

            with nc.Block() as blk:
                @blk.tensor
                def _(t):
                    t.matmul(pb1[:, :], onesr[0:1, :], c11b[:, :],
                             start=True, stop=True).then_inc(bb.sem(sm), 1)
                    bb.inc(sm)

                @blk.scalar
                def _(s):
                    s.wait_ge(bb.sem(sm), bb.cnt[sm])
                    s.copy(negC[:, :], pb1[:, :]).then_inc(bb.sem(ss), 1)
                    bb.inc(ss)

        for half in range(2):
            _l0_attn_half(bb, h, hi, half, QTr, KTr, vh, OT, onesc, onesr,
                          ckb, cvb, maskq, negC, pfx)


def _l0_attn_half(bb, h, hi, half, QTr, KTr, vh, OT, onesc, onesr, ckb, cvb,
                  maskq, negC, pfx):
    nc = bb.nc
    sm, ss, sv = "BWm", "BWs", "BWv"
    for _s in (sm, ss, sv):
        bb.sem(_s)
    un = pfx + bb.u()
    with ExitStack() as st:
        e = st.enter_context
        pe_sb = e(nc.sbuf_tensor(un + "pe", [128, 2, 512], BF16))
        pm_sb = e(nc.sbuf_tensor(un + "pm", [128, 2, 512], BF16))
        rrow = e(nc.sbuf_tensor(un + "rrow", [1, 1024], F32))
        rrowb = e(nc.sbuf_tensor(un + "rrowb", [1, 1024], BF16))
        rb_sb = e(nc.sbuf_tensor(un + "rb", [128, 1024], BF16))
        ps = e(nc.psum_tensor(un + "ps", [128, 2, 512], F32))
        pOT = e(nc.psum_tensor(un + "pOT", [128, 1024], F32))
        prow = e(nc.psum_tensor(un + "prow", [1, 1024], F32))
        pbc = e(nc.psum_tensor(un + "pbc", [128, 1024], F32))

        qts = [half * 8 + i for i in range(8)]
        bm, bs, bv = bb.cnt[sm], bb.cnt[ss], bb.cnt[sv]

        def sm_scores(i):
            return bm + (i + 1 if i < 2 else 2 * i - 1)

        def sm_consume(i):
            return bm + (2 * i + 4 if i <= 5 else 9 + i)

        with nc.Block() as blk:
            @blk.tensor
            def _(t):
                for j in range(10):
                    if j < 8:
                        i = j
                        qt = qts[i]
                        if i >= 2:
                            t.wait_ge(bb.sem(ss), bs + i - 1)
                        q_ap = QTr[:, hi, qt * 128:(qt + 1) * 128]
                        t.matmul(ps[:, i % 2, 0:128], ckb[:, :], q_ap,
                                 start=True, stop=False, skip_group_check=True)
                        for d, jt in enumerate((max(qt - 2, 0), max(qt - 1, 0),
                                                qt)):
                            mm = t.matmul(
                                ps[:, i % 2, 128 * (d + 1):128 * (d + 2)],
                                KTr[:, hi, jt * 128:(jt + 1) * 128], q_ap,
                                start=False, stop=(d == 2),
                                skip_group_check=True)
                        mm.then_inc(bb.sem(sm), 1)
                        bb.inc(sm)
                    if j >= 2:
                        i = j - 2
                        qt = qts[i]
                        t.wait_ge(bb.sem(sv), bv + i + 1)
                        col = (i % 4) * 128
                        bank = i // 4
                        first = (i % 4 == 0)
                        reg = slice(bank * 512 + col, bank * 512 + col + 128)
                        t.matmul(prow[0:1, reg], onesc[:, :],
                                 pm_sb[:, i % 2, 0:128],
                                 start=first, stop=False, skip_group_check=True)
                        t.matmul(pOT[:, reg], cvb[:, :], pm_sb[:, i % 2, 0:128],
                                 start=first, stop=False, skip_group_check=True)
                        for d, jt in enumerate((max(qt - 2, 0), max(qt - 1, 0),
                                                qt)):
                            pm_ap = pm_sb[:, i % 2, 128 * (d + 1):128 * (d + 2)]
                            t.matmul(prow[0:1, reg], onesc[:, :], pm_ap,
                                     start=False, stop=False,
                                     skip_group_check=True)
                            mm = t.matmul(pOT[:, reg], vh[:, jt, :], pm_ap,
                                     start=False, stop=False,
                                     skip_group_check=True)
                        mm.then_inc(bb.sem(sm), 1)
                        bb.inc(sm)

            @blk.scalar
            def _(s):
                for i in range(8):
                    s.wait_ge(bb.sem(sm), sm_scores(i))
                    if i >= 2:
                        s.wait_ge(bb.sem(sv), bv + i - 1)
                    s.activation(pe_sb[:, i % 2, :], ps[:, i % 2, :], AF.Exp,
                                 scale=SCALE, bias=negC[:, :]
                                 ).then_inc(bb.sem(ss), 1)
                    bb.inc(ss)

            @blk.vector
            def _(v):
                for i in range(8):
                    v.wait_ge(bb.sem(ss), bs + i + 1)
                    if i >= 2:
                        v.wait_ge(bb.sem(sm), sm_consume(i - 2))
                    v.tensor_tensor(pm_sb[:, i % 2, :], pe_sb[:, i % 2, :],
                                    maskq[:, qts[i], :], op=ALU.mult
                                    ).then_inc(bb.sem(sv), 1)
                    bb.inc(sv)

        bm, bs, bv = bb.cnt[sm], bb.cnt[ss], bb.cnt[sv]
        with nc.Block() as blk:
            @blk.vector
            def _(v):
                v.reciprocal(rrow[:, :], prow[0:1, :])
                v.drain()
                v.tensor_copy(rrowb[:, :], rrow[:, :]).then_inc(bb.sem(sv), 1)
                bb.inc(sv)

            @blk.tensor
            def _(t):
                t.wait_ge(bb.sem(sv), bv + 1)
                t.matmul(pbc[:, 0:512], onesr[0:1, :], rrowb[0:1, 0:512],
                         start=True, stop=True)
                t.matmul(pbc[:, 512:1024], onesr[0:1, :], rrowb[0:1, 512:1024],
                         start=True, stop=True).then_inc(bb.sem(sm), 1)
                bb.inc(sm)

            @blk.scalar
            def _(s):
                s.wait_ge(bb.sem(sm), bm + 1)
                s.copy(rb_sb[:, :], pbc[:, :]).then_inc(bb.sem(ss), 1)
                bb.inc(ss)

            @blk.vector
            def _(v2):
                v2.wait_ge(bb.sem(ss), bs + 1)
                v2.tensor_tensor(OT[0:128, h, half * 1024:(half + 1) * 1024],
                                 pOT[:, :], rb_sb[:, :], op=ALU.mult)


def _prep_l0_attn(inputs, perm):
    wq = np.asarray(inputs["L0_attn_wq"], np.float32)
    wk = np.asarray(inputs["L0_attn_wk"], np.float32)
    wv = np.asarray(inputs["L0_attn_wv"], np.float32)
    wo = np.asarray(inputs["L0_attn_wo"], np.float32)
    return {
        "a0_wqT": np.ascontiguousarray(wq.T[:, perm]),
        "a0_wkT": np.ascontiguousarray(wk.T[:, perm]),
        "a0_wvT": np.ascontiguousarray(wv.T),
        "a0_woT": np.ascontiguousarray(wo.T),
    }


def _declare_common(nc, bb):
    for k, arr in _consts().items():
        dt = BF16 if arr.dtype == _BF else F32
        bb.io["c_" + k] = nc.dram_tensor("c_" + k, list(arr.shape), dt,
                                         kind="ExternalInput").ap()


def build_l0_attn():
    nc = bass.Bass("TRN2", target_bir_lowering=False)
    with ExitStack() as ctx:
        bb = BB(nc, ctx)
        x_in = nc.dram_tensor("x", [S, D], F32, kind="ExternalInput").ap()
        x_out = nc.dram_tensor("y", [S, D], F32, kind="ExternalOutput").ap()
        for nm in ["a0_wqT", "a0_wkT", "a0_wvT", "a0_woT"]:
            bb.io[nm] = nc.dram_tensor(nm, [D, D], F32, kind="ExternalInput").ap()
        _declare_common(nc, bb)
        bb.io["a0_xt"] = nc.dram_tensor("a0_xt", [ND, NT2, 128, 256], BF16,
                                        kind="Internal").ap()
        bb.io["a0_vscr"] = nc.dram_tensor("a0_vscr", [NH, NT, 128, 128], BF16,
                                          kind="Internal").ap()
        emit_l0_attn(bb, x_in, x_out, "a0")
    return nc


def kernel(**inputs):
    raise NotImplementedError("full module not assembled yet")


def emit_scr_out_phase(bb, x_in, x_out, acc_scr, pfx):
    nc = bb.nc
    un = pfx + bb.u()
    with ExitStack() as st:
        e = st.enter_context
        otile = [e(nc.sbuf_tensor(f"{un}ot{i}", [128, ND, 128], F32))
                 for i in range(2)]
        idf = e(nc.sbuf_tensor(un + "idf", [128, 128], F32))
        sot = [bb.slot("SCdot%d" % i) for i in range(2)]
        sid = bb.slot("SCdid")

        with nc.Block() as blk:
            @blk.sync
            def _(g):
                sid.put(g, idf[:, :], bb.io["c_identf"][:, :])
                sid.wait(g)

        sm = "OTm"
        bb.sem(sm)
        bm0 = bb.cnt[sm]

        def pre_g(g, m):
            if m >= 2:
                g.wait_ge(bb.sem(sm), bm0 + m - 1)
            sot[m % 2].put(g, otile[m % 2][:, :, :],
                           acc_scr[:, :, m * 128:(m + 1) * 128
                                   ].rearrange("d p f -> p d f"))

        def mm_waits(t, m):
            sot[m % 2].wait_rel(t, 16 * (m // 2 + 1))

        def emit_mm(t, m, psum):
            for dt in range(ND):
                tp = t.transpose(psum[:, dt * 128:(dt + 1) * 128],
                                 otile[m % 2][:, dt, :], idf[:, :])
            return tp

        emit_out_ln_residual(bb, x_in, x_out, pfx, emit_mm, mm_waits,
                             pre_g=pre_g)


def emit_l1_mix(bb, x_in, x_out, pfx="m1"):
    nc = bb.nc
    io = bb.io
    xt = io[pfx + "_xt"]
    emit_preln_xt(bb, x_in, xt, pfx + "A")

    with ExitStack() as st0:
        e0 = st0.enter_context
        vsb = e0(nc.sbuf_tensor(pfx + "vsb", [128, S], BF16))
        onesr = e0(nc.sbuf_tensor(pfx + "onesr", [1, 128], BF16))
        onesS = e0(nc.sbuf_tensor(pfx + "onesS", [1, S], BF16))
        bvb = e0(nc.sbuf_tensor(pfx + "bvb", [1, S], BF16))
        bob = e0(nc.sbuf_tensor(pfx + "bob", [1, S], BF16))

        with ExitStack() as st:
            e = st.enter_context
            wvb = e(nc.sbuf_tensor(pfx + "wvb", [128, ND, D], BF16))
            wvs32 = e(nc.sbuf_tensor(pfx + "wvs32", [128, ND, 128], F32))
            wvsum = e(nc.sbuf_tensor(pfx + "wvsum", [128, ND, 128], BF16))
            bvs32 = e(nc.sbuf_tensor(pfx + "bvs32", [1, 128], F32))
            bvsum = e(nc.sbuf_tensor(pfx + "bvsum", [1, 128], BF16))
            xrow = [e(nc.sbuf_tensor(f"{pfx}xr{i}", [128, S], BF16))
                    for i in range(2)]
            pvs = e(nc.psum_tensor(pfx + "pvs", [128, S], F32))
            swv = bb.slot("M1dwv")
            sxr = [bb.slot("M1dxr%d" % i) for i in range(2)]
            sm, sv = "BVm", "BVv"
            bb.sem(sm), bb.sem(sv)
            bm, bv_ = bb.cnt[sm], bb.cnt[sv]

            with nc.Block() as blk:
                @blk.gpsimd
                def _(g):
                    swv.put(g, onesr[:, :], io["c_ones_row"][:, :])
                    swv.put(g, onesS[:, :], io["c_ones_rowS"][:, :])
                    swv.put(g, bvb[:, :], io[pfx + "_bv"][:, :])
                    swv.put(g, bob[:, :], io[pfx + "_bo"][:, :])
                    for k in range(ND):
                        swv.put(g, wvb[:, k, :],
                                io[pfx + "_wvT"][k * 128:(k + 1) * 128, :])

            with nc.Block() as blk:
                @blk.vector
                def _(v):
                    swv.wait(v)
                    for k in range(ND):
                        v.tensor_tensor(wvs32[:, k, :], wvb[:, k, 0:128],
                                        wvb[:, k, 128:256], op=ALU.add)
                    v.drain()
                    for kh in range(2, NH):
                        for k in range(ND):
                            v.tensor_tensor(wvs32[:, k, :], wvs32[:, k, :],
                                            wvb[:, k, kh * 128:(kh + 1) * 128],
                                            op=ALU.add)
                        v.drain()
                    for k in range(ND):
                        v.tensor_copy(wvsum[:, k, :], wvs32[:, k, :])
                    v.tensor_tensor(bvs32[:, :], bvb[0:1, 0:128],
                                    bvb[0:1, 128:256], op=ALU.add)
                    v.drain()
                    for kh in range(2, NH):
                        v.tensor_tensor(bvs32[:, :], bvs32[:, :],
                                        bvb[0:1, kh * 128:(kh + 1) * 128],
                                        op=ALU.add)
                        v.drain()
                    v.tensor_copy(bvsum[:, :], bvs32[:, :])

            with nc.Block() as blk:
                @blk.sync
                def _(g):
                    for k in range(ND):
                        if k >= 2:
                            g.wait_ge(bb.sem(sm), bm + k - 1)
                        sxr[k % 2].put(g, xrow[k % 2][:, :],
                                       xt[k].rearrange("t p f -> p t f"))

                @blk.tensor
                def _(t):
                    for k in range(ND):
                        sxr[k % 2].wait_rel(t, 16 * (k // 2 + 1))
                        for nb in range(NB):
                            mm = t.matmul(pvs[:, nb * 512:(nb + 1) * 512],
                                          wvsum[:, k, :],
                                          xrow[k % 2][:, nb * 512:(nb + 1) * 512],
                                          start=(k == 0), stop=False,
                                          skip_group_check=True)
                        mm.then_inc(bb.sem(sm), 1)
                        bb.inc(sm)
                    for nb in range(NB):
                        mm = t.matmul(pvs[:, nb * 512:(nb + 1) * 512],
                                      bvsum[0:1, :],
                                      onesS[0:1, nb * 512:(nb + 1) * 512],
                                      start=False, stop=(nb == NB - 1),
                                      skip_group_check=True)
                    mm.then_inc(bb.sem(sm), 1)
                    bb.inc(sm)

                @blk.scalar
                def _(s):
                    s.wait_ge(bb.sem(sm), bm + ND + 1)
                    s.copy(vsb[:, :], pvs[:, :]).then_inc(bb.sem(sv), 1)
                    bb.inc(sv)

        with ExitStack() as st:
            e = st.enter_context
            wob = e(nc.sbuf_tensor(pfx + "wob", [128, ND, D], BF16))
            wos32 = e(nc.sbuf_tensor(pfx + "wos32", [128, D], F32))
            wosum = e(nc.sbuf_tensor(pfx + "wosum", [128, D], BF16))
            swo = bb.slot("M1dwo")
            with nc.Block() as blk:
                @blk.gpsimd
                def _(g):
                    for k in range(ND):
                        swo.put(g, wob[:, k, :],
                                io[pfx + "_woT"][k * 128:(k + 1) * 128, :])
                    swo.wait(g)

            with nc.Block() as blk:
                @blk.vector
                def _(v):
                    v.tensor_tensor(wos32[:, :], wob[:, 0, :], wob[:, 1, :],
                                    op=ALU.add)
                    v.drain()
                    for qh in range(2, NH):
                        v.tensor_tensor(wos32[:, :], wos32[:, :], wob[:, qh, :],
                                        op=ALU.add)
                        v.drain()
                    v.tensor_copy(wosum[:, :], wos32[:, :])

            def emit_mm(t, m, psum):
                for nb in range(NB):
                    t.matmul(psum[:, nb * 512:(nb + 1) * 512],
                             vsb[:, m * 128:(m + 1) * 128],
                             wosum[:, nb * 512:(nb + 1) * 512],
                             start=True, stop=False, skip_group_check=True)
                for nb in range(NB):
                    mm = t.matmul(psum[:, nb * 512:(nb + 1) * 512],
                                  onesr[0:1, 0:128],
                                  bob[0:1, nb * 512:(nb + 1) * 512],
                                  start=False, stop=(nb == NB - 1),
                                  skip_group_check=True)
                return mm

            emit_out_ln_residual(bb, x_in, x_out, pfx + "E", emit_mm)


def emit_l1_ffn(bb, x_in, x_out, pfx="f1"):
    nc = bb.nc
    io = bb.io
    xt = io[pfx + "_xt"]
    acc = io[pfx + "_acc"]
    NHM = HID1 // 128
    TC = 1024
    NCHUNK = S // TC

    emit_preln_xt(bb, x_in, xt, pfx + "A")
    _l1_w2_fold(bb, pfx, NHM)

    with ExitStack() as st0:
        e0 = st0.enter_context
        h128b = e0(nc.sbuf_tensor(pfx + "h128", [128, 128], BF16))
        hA = e0(nc.sbuf_tensor(pfx + "hA", [128, NHM, TC], BF16))
        sic = bb.slot("F1dic")
        with nc.Block() as blk:
            @blk.sync
            def _(g):
                sic.put(g, h128b[:, :], io["c_h128"][:, :])
                sic.wait(g)

        for chunk in range(NCHUNK):
            _l1_ffn_chunk(bb, chunk, xt, acc, h128b, hA, TC, NHM, pfx)

    emit_scr_out_phase(bb, x_in, x_out, acc, pfx + "E")


def _l1_w2_fold(bb, pfx, NHM):
    nc = bb.nc
    io = bb.io
    un = pfx + bb.u()
    vw = io[pfx + "_w2T"].rearrange("(b x) d -> b x d", b=64)
    wf = io[pfx + "_w2f"].rearrange("(a x) d -> a x d", a=64)
    XH = 64
    NSUB = 2
    NFB = XH // NSUB
    CS = D

    with ExitStack() as st:
        e = st.enter_context
        h64b = e(nc.sbuf_tensor(un + "h64", [128, 128], BF16))
        mv = e(nc.sbuf_tensor(un + "mv", [128, 2, NSUB, CS], BF16))
        res = e(nc.sbuf_tensor(un + "res", [128, 2, NSUB, CS], BF16))
        pfo = e(nc.psum_tensor(un + "pfo", [128, 2, CS], F32))
        sldA = bb.slot("F1dfl0")
        sldB = bb.slot("F1dfl1")
        sstA = bb.slot("F1dfs0")
        sstB = bb.slot("F1dfs1")
        shc = bb.slot("F1dfh")
        smf, sss, ssv = "BWFm", "BWFs", "BWFv"
        for _x in (smf, sss, ssv):
            bb.sem(_x)
        bmf, bss, bsv = bb.cnt[smf], bb.cnt[sss], bb.cnt[ssv]

        with nc.Block() as blk:
            @blk.gpsimd
            def _(g):
                for s in range(NFB):
                    if s >= 2:
                        g.wait_ge(bb.sem(smf), bmf + NSUB * (s - 1))
                    sldA.put(g, mv[0:64, s % 2, :, :],
                             vw[:, NSUB * s:NSUB * (s + 1), :])
                    sldB.put(g, mv[64:128, s % 2, :, :],
                             vw[:, XH + NSUB * s:XH + NSUB * (s + 1), :])

            @blk.scalar
            def _(s_):
                for s in range(NFB):
                    s_.wait_ge(bb.sem(smf), bmf + NSUB * s + 1)
                    if s >= 2:
                        sstA.wait_rel(s_, 16 * (s - 1))
                        sstB.wait_rel(s_, 16 * (s - 1))
                    s_.copy(res[:, s % 2, 0, :], pfo[:, 0, :]
                            ).then_inc(bb.sem(sss), 1)
                    bb.inc(sss)

            @blk.vector
            def _(v):
                for s in range(NFB):
                    v.wait_ge(bb.sem(smf), bmf + NSUB * s + 2)
                    if s >= 2:
                        sstA.wait_rel(v, 16 * (s - 1))
                        sstB.wait_rel(v, 16 * (s - 1))
                    v.tensor_copy(res[:, s % 2, 1, :], pfo[:, 1, :]
                                  ).then_inc(bb.sem(ssv), 1)
                    bb.inc(ssv)

            @blk.tensor
            def _(t):
                shc.wait(t)
                for j in range(NFB * NSUB):
                    s, sub = j // NSUB, j % NSUB
                    sldA.wait_rel(t, 16 * (s + 1))
                    sldB.wait_rel(t, 16 * (s + 1))
                    if j >= 2:
                        if j % 2 == 0:
                            t.wait_ge(bb.sem(sss), bss + j // 2)
                        else:
                            t.wait_ge(bb.sem(ssv), bsv + (j - 1) // 2)
                    for q in range(CS // 512):
                        mm = t.matmul(
                            pfo[:, j % 2, q * 512:(q + 1) * 512],
                            h64b[:, :],
                            mv[:, s % 2, sub, q * 512:(q + 1) * 512],
                            start=True, stop=True)
                    mm.then_inc(bb.sem(smf), 1)
                    bb.inc(smf)

            @blk.sync
            def _(g):
                shc.put(g, h64b[:, :], io["c_h64d"][:, :])
                for s in range(NFB):
                    g.wait_ge(bb.sem(sss), bss + s + 1)
                    g.wait_ge(bb.sem(ssv), bsv + s + 1)
                    sstA.put(g, wf[:, NSUB * s:NSUB * (s + 1), :],
                             res[0:64, s % 2, :, :])
                    sstB.put(g, wf[:, XH + NSUB * s:XH + NSUB * (s + 1), :],
                             res[64:128, s % 2, :, :])
                sstA.wait(g)
                sstB.wait(g)


def _l1_ffn_chunk(bb, chunk, xt, acc, h128b, hA, TC, NHM, pfx):
    nc = bb.nc
    io = bb.io
    un = pfx + bb.u()
    NH5 = TC // 512

    with ExitStack() as st:
        e = st.enter_context
        xtc = e(nc.sbuf_tensor(un + "xtc", [128, ND, TC], BF16))
        w1c = [e(nc.sbuf_tensor(f"{un}w1c{i}", [128, ND, 128], BF16))
               for i in range(2)]
        w3c = [e(nc.sbuf_tensor(f"{un}w3c{i}", [128, ND, 128], BF16))
               for i in range(2)]
        s1b = [e(nc.sbuf_tensor(f"{un}s1b{i}", [128, TC], BF16))
               for i in range(2)]
        pu = e(nc.psum_tensor(un + "pu", [128, 2, 2, TC], F32))
        sxc = bb.slot("F1dxc")
        sw1 = [bb.slot("F1dw1%d" % i) for i in range(2)]
        sw3 = [bb.slot("F1dw3%d" % i) for i in range(2)]
        sm, ss, sv = "BUm", "BUs", "BUv"
        for _x in (sm, ss, sv):
            bb.sem(_x)
        bm, bs, bv = bb.cnt[sm], bb.cnt[ss], bb.cnt[sv]

        with nc.Block() as blk:
            @blk.sync
            def _(g):
                for t2 in range(TC // 256):
                    sxc.put(g, xtc[:, :, t2 * 256:(t2 + 1) * 256],
                            xt[:, chunk * (TC // 256) + t2].rearrange(
                                "k p f -> p k f"))

            @blk.gpsimd
            def _(g):
                for hm in range(NHM):
                    if hm >= 2:
                        g.wait_ge(bb.sem(sm), bm + 2 * (hm - 1))
                    i = hm % 2
                    sw1[i].put(g, w1c[i][:, :, :],
                               io[pfx + "_w1T"][:, hm * 128:(hm + 1) * 128
                                                ].rearrange("(kt p) n -> p kt n",
                                                            p=128))
                    sw3[i].put(g, w3c[i][:, :, :],
                               io[pfx + "_w3T"][:, hm * 128:(hm + 1) * 128
                                                ].rearrange("(kt p) n -> p kt n",
                                                            p=128))

            @blk.tensor
            def _(t):
                sxc.wait(t)
                for hm in range(NHM):
                    i = hm % 2
                    sw1[i].wait_rel(t, 16 * (hm // 2 + 1))
                    sw3[i].wait_rel(t, 16 * (hm // 2 + 1))
                    if hm >= 2:
                        t.wait_ge(bb.sem(sv), bv + hm - 1)
                    for h5 in range(NH5):
                        for k in range(ND):
                            mm = t.matmul(
                                pu[:, i, 0, h5 * 512:(h5 + 1) * 512],
                                w1c[i][:, k, :],
                                xtc[:, k, h5 * 512:(h5 + 1) * 512],
                                start=(k == 0), stop=(k == ND - 1))
                    mm.then_inc(bb.sem(sm), 1)
                    bb.inc(sm)
                    for h5 in range(NH5):
                        for k in range(ND):
                            mm = t.matmul(
                                pu[:, i, 1, h5 * 512:(h5 + 1) * 512],
                                w3c[i][:, k, :],
                                xtc[:, k, h5 * 512:(h5 + 1) * 512],
                                start=(k == 0), stop=(k == ND - 1))
                    mm.then_inc(bb.sem(sm), 1)
                    bb.inc(sm)

            @blk.scalar
            def _(s):
                for hm in range(NHM):
                    i = hm % 2
                    s.wait_ge(bb.sem(sm), bm + 2 * hm + 1)
                    s.activation(s1b[i][:, :], pu[:, i, 0, :], AF.Silu
                                 ).then_inc(bb.sem(ss), 1)
                    bb.inc(ss)

            @blk.vector
            def _(v):
                for hm in range(NHM):
                    i = hm % 2
                    v.wait_ge(bb.sem(ss), bs + hm + 1)
                    v.wait_ge(bb.sem(sm), bm + 2 * hm + 2)
                    v.tensor_tensor(hA[:, hm, :], s1b[i][:, :], pu[:, i, 1, :],
                                    op=ALU.mult).then_inc(bb.sem(sv), 1)
                    bb.inc(sv)

    with ExitStack() as st:
        pf = st.enter_context(nc.psum_tensor(un + "pf", [128, 2, TC], F32))
        sm, ss = "BFm", "BFs"
        bb.sem(sm), bb.sem(ss)
        bm, bs = bb.cnt[sm], bb.cnt[ss]
        with nc.Block() as blk:
            @blk.tensor
            def _(t):
                for pm in range(NHM):
                    if pm >= 2:
                        t.wait_ge(bb.sem(ss), bs + pm - 1)
                    for h5 in range(NH5):
                        mm = t.matmul(pf[:, pm % 2, h5 * 512:(h5 + 1) * 512],
                                      h128b[:, :],
                                      hA[:, pm, h5 * 512:(h5 + 1) * 512],
                                      start=True, stop=True)
                    mm.then_inc(bb.sem(sm), 1)
                    bb.inc(sm)

            @blk.scalar
            def _(s):
                for pm in range(NHM):
                    s.wait_ge(bb.sem(sm), bm + pm + 1)
                    s.copy(hA[:, pm, :], pf[:, pm % 2, :]
                           ).then_inc(bb.sem(ss), 1)
                    bb.inc(ss)

    with ExitStack() as st:
        e = st.enter_context
        w2c = [e(nc.sbuf_tensor(f"{un}w2c{i}", [128, NHM, 128], BF16))
               for i in range(2)]
        orow = [e(nc.sbuf_tensor(f"{un}or{i}", [128, TC], F32))
                for i in range(2)]
        pd = e(nc.psum_tensor(un + "pd", [128, 2, TC], F32))
        sw2 = [bb.slot("F1dw2%d" % i) for i in range(2)]
        sor = [bb.slot("F1dor%d" % i) for i in range(2)]
        sm, ss = "BDm", "BDs"
        bb.sem(sm), bb.sem(ss)
        bm, bs = bb.cnt[sm], bb.cnt[ss]
        w2f = io[pfx + "_w2f"]

        with nc.Block() as blk:
            @blk.sync
            def _(g):
                for i in range(2):
                    sw2[i].put(g, w2c[i][:, :, :],
                               w2f[:, i * 128:(i + 1) * 128].rearrange(
                                   "(kt p) n -> p kt n", p=128))
                for md in range(ND):
                    i = md % 2
                    g.wait_ge(bb.sem(ss), bs + md + 1)
                    sor[i].put(g, acc[md, :, chunk * TC:(chunk + 1) * TC],
                               orow[i][:, :])
                    if md + 2 < ND:
                        g.wait_ge(bb.sem(sm), bm + md + 1)
                        sw2[i].put(g, w2c[i][:, :, :],
                                   w2f[:, (md + 2) * 128:(md + 3) * 128
                                       ].rearrange("(kt p) n -> p kt n",
                                                   p=128))
                sor[0].wait(g)
                sor[1].wait(g)

            @blk.tensor
            def _(t):
                for md in range(ND):
                    i = md % 2
                    sw2[i].wait_rel(t, 16 * (md // 2 + 1))
                    if md >= 2:
                        t.wait_ge(bb.sem(ss), bs + md - 1)
                    for h5 in range(NH5):
                        for k in range(NHM):
                            mm = t.matmul(pd[:, i, h5 * 512:(h5 + 1) * 512],
                                          w2c[i][:, k, :],
                                          hA[:, k, h5 * 512:(h5 + 1) * 512],
                                          start=(k == 0), stop=(k == NHM - 1))
                    mm.then_inc(bb.sem(sm), 1)
                    bb.inc(sm)

            @blk.scalar
            def _(s):
                for md in range(ND):
                    i = md % 2
                    s.wait_ge(bb.sem(sm), bm + md + 1)
                    if md >= 2:
                        sor[i].wait_rel(s, 16 * (md // 2))
                    s.copy(orow[i][:, :], pd[:, i, :]).then_inc(bb.sem(ss), 1)
                    bb.inc(ss)


def emit_l0_moe(bb, x_in, x_out, pfx="f0"):
    nc = bb.nc
    io = bb.io
    acc = io[pfx + "_acc"]
    NHT = HID0 // 128

    with ExitStack() as st0:
        e0 = st0.enter_context
        xtr = e0(nc.sbuf_tensor(pfx + "xtr", [128, ND, S], BF16))

        emit_preln_xt(bb, x_in, None, pfx + "A", xtr=xtr)

        gT = e0(nc.sbuf_tensor(pfx + "gT", [8, S], F32))
        onesf = e0(nc.sbuf_tensor(pfx + "onesf", [1, 128], F32))
        idf = e0(nc.sbuf_tensor(pfx + "idf", [128, 128], F32))
        id8 = e0(nc.sbuf_tensor(pfx + "id8", [8, 8], F32))
        gb = e0(nc.sbuf_tensor(pfx + "gb", [128, S], F32))
        grow = e0(nc.sbuf_tensor(pfx + "grow", [1, S], F32))
        heT = e0(nc.sbuf_tensor(pfx + "heT", [128, NHT, S], BF16))
        tmp = e0(nc.sbuf_tensor(pfx + "tmp", [128, 2, S], F32))
        s1b = e0(nc.sbuf_tensor(pfx + "s1b", [128, S], BF16))
        w1c = [e0(nc.sbuf_tensor(f"{pfx}w1c{i}", [128, ND, 128], BF16))
               for i in range(2)]
        w3c = [e0(nc.sbuf_tensor(f"{pfx}w3c{i}", [128, ND, 128], BF16))
               for i in range(2)]
        w2c = [e0(nc.sbuf_tensor(f"{pfx}w2c{i}", [128, NHT, 128], BF16))
               for i in range(2)]
        orow = [e0(nc.sbuf_tensor(f"{pfx}or{i}", [128, S], F32))
                for i in range(2)]
        sor = [bb.slot("F0dacc%d" % i) for i in range(2)]
        sw1 = [bb.slot("F0dw1%d" % i) for i in range(2)]
        sw3 = [bb.slot("F0dw3%d" % i) for i in range(2)]
        sw2 = [bb.slot("F0dw2%d" % i) for i in range(2)]
        scn = bb.slot("F0dcn")
        with nc.Block() as blk:
            @blk.sync
            def _(g):
                scn.put(g, onesf[:, :], io["c_ones_row_f"][:, :])
                scn.put(g, idf[:, :], io["c_identf"][:, :])
                scn.put(g, id8[:, :], io["c_ident8f"][:, :])
                scn.wait(g)

        _moe_gate(bb, xtr, gT, idf, onesf, id8, gb, grow,
                  w1c, w3c, sw1, sw3, pfx)

        for ex in range(NE + 1):
            _moe_expert(bb, ex, xtr, gT, onesf, id8, acc, sor, pfx, NHT,
                        gb, grow, heT, tmp, s1b, w1c, w3c, w2c, orow,
                        sw1, sw3, sw2)

    emit_scr_out_phase(bb, x_in, x_out, acc, pfx + "E")


def _f0_w13(io, pfx, ex, hm, which):
    if ex == NE:
        return io[pfx + ("_sw1" if which == 1 else "_sw3")
                  ][:, hm * 128:(hm + 1) * 128]
    return io[pfx + ("_w1" if which == 1 else "_w3")
              ][ex, :, hm * 128:(hm + 1) * 128]


def _moe_gate(bb, xtr, gT, idf, onesf, id8, gb, grow, w1c, w3c, sw1, sw3,
              pfx):
    nc = bb.nc
    io = bb.io
    un = pfx + bb.u()
    with ExitStack() as st:
        e = st.enter_context
        gwb = e(nc.sbuf_tensor(un + "gwb", [128, ND, 8], BF16))
        ebias = e(nc.sbuf_tensor(un + "ebias", [128, 8], F32))
        ge = e(nc.sbuf_tensor(un + "ge", [128, 2, 8], F32))
        gst = e(nc.sbuf_tensor(un + "gst", [128, 2, 8], F32))
        gsc = e(nc.sbuf_tensor(un + "gsc", [128, 2, 8], F32))
        gpb = e(nc.sbuf_tensor(un + "gpb", [128, 2, 8], F32))
        gw2 = e(nc.sbuf_tensor(un + "gw2", [128, 2, 8], F32))
        gall = e(nc.sbuf_tensor(un + "gall", [128, NT, 8], F32))
        pg = e(nc.psum_tensor(un + "pg", [128, 2, 8], F32))
        pgt = e(nc.psum_tensor(un + "pgt", [8, 128], F32))
        pgb = e(nc.psum_tensor(un + "pgb", [128, S], F32))
        sgw = bb.slot("F0dgw")
        sm, ss, sv = "BGm", "BGs", pfx + "Gv"
        for _x in (sm, ss, sv):
            bb.sem(_x)
        bm, bs, bv = bb.cnt[sm], bb.cnt[ss], bb.cnt[sv]

        with nc.Block() as blk:
            @blk.gpsimd
            def _(g):
                sgw.put(g, ebias[:, :], io[pfx + "_ebias"][:, :])
                sgw.put(g, gwb[:, :, :],
                        io[pfx + "_gate"].rearrange("(kt p) n -> p kt n", p=128))

            @blk.tensor
            def _(t):
                sgw.wait(t)
                for m in range(NT):
                    if m >= 2:
                        t.wait_ge(bb.sem(ss), bs + m - 1)
                    for k in range(ND):
                        mm = t.matmul(pg[:, m % 2, :],
                                      xtr[:, k, m * 128:(m + 1) * 128],
                                      gwb[:, k, :],
                                      start=(k == 0), stop=(k == ND - 1))
                    mm.then_inc(bb.sem(sm), 1)
                    bb.inc(sm)

            @blk.scalar
            def _(s):
                for m in range(NT):
                    i = m % 2
                    s.wait_ge(bb.sem(sm), bm + m + 1)
                    if m >= 2:
                        s.wait_ge(bb.sem(sv), bv + m - 1)
                    s.activation(ge[:, i, :], pg[:, i, :], AF.Exp,
                                 accum_out=gst[:, i, 0:1]).then_inc(bb.sem(ss), 1)
                    bb.inc(ss)

            @blk.vector
            def _(v):
                for m in range(NT):
                    i = m % 2
                    v.wait_ge(bb.sem(ss), bs + m + 1)
                    v.reciprocal(gst[:, i, 1:2], gst[:, i, 0:1])
                    v.drain()
                    v.tensor_scalar(gsc[:, i, :], ge[:, i, :], gst[:, i, 1:2],
                                    None, op0=ALU.mult)
                    v.drain()
                    v.tensor_tensor(gpb[:, i, :], gsc[:, i, :], ebias[:, :],
                                    op=ALU.add)
                    v.drain()
                    v.reduce_max(gst[:, i, 2:3], gpb[:, i, :], axis=AX.X)
                    v.drain()
                    v.tensor_scalar(gw2[:, i, :], gpb[:, i, :], gst[:, i, 2:3],
                                    None, op0=ALU.is_equal)
                    v.drain()
                    v.scalar_tensor_tensor(gw2[:, i, :], gw2[:, i, :], -1e30,
                                           gpb[:, i, :], op0=ALU.mult,
                                           op1=ALU.add)
                    v.drain()
                    v.reduce_max(gst[:, i, 3:4], gw2[:, i, :], axis=AX.X)
                    v.drain()
                    v.tensor_scalar(gw2[:, i, :], gpb[:, i, :], gst[:, i, 3:4],
                                    None, op0=ALU.is_ge)
                    v.drain()
                    v.tensor_tensor(gall[:, m, :], gsc[:, i, :], gw2[:, i, :],
                                    op=ALU.mult).then_inc(bb.sem(sv), 1)
                    bb.inc(sv)

        bm, bs, bv = bb.cnt[sm], bb.cnt[ss], bb.cnt[sv]
        with nc.Block() as blk:
            @blk.tensor
            def _(t):
                for m in range(NT):
                    if m >= 1:
                        t.wait_ge(bb.sem(sv), bv + m)
                    t.transpose(pgt[:, :], gall[:, m, :], idf[:, :]
                                ).then_inc(bb.sem(sm), 1)
                    bb.inc(sm)
                t.wait_ge(bb.sem(sv), bv + NT)
                for nb in range(NB):
                    mm = t.matmul(pgb[0:1, nb * 512:(nb + 1) * 512],
                                  id8[:, 0:1], gT[0:8, nb * 512:(nb + 1) * 512],
                                  start=True, stop=True, skip_group_check=True)
                mm.then_inc(bb.sem(sm), 1)
                bb.inc(sm)
                t.wait_ge(bb.sem(ss), bs + 1)
                for nb in range(NB):
                    mm = t.matmul(pgb[:, nb * 512:(nb + 1) * 512],
                                  onesf[0:1, :], grow[0:1, nb * 512:(nb + 1) * 512],
                                  start=True, stop=True, skip_group_check=True)
                mm.then_inc(bb.sem(sm), 1)
                bb.inc(sm)

            @blk.vector
            def _(v):
                for m in range(NT):
                    v.wait_ge(bb.sem(sm), bm + m + 1)
                    v.tensor_copy(gT[0:8, m * 128:(m + 1) * 128], pgt[0:8, :]
                                  ).then_inc(bb.sem(sv), 1)
                    bb.inc(sv)

            @blk.scalar
            def _(s):
                s.wait_ge(bb.sem(sm), bm + NT + 1)
                s.copy(grow[:, :], pgb[0:1, :]).then_inc(bb.sem(ss), 1)
                bb.inc(ss)
                s.wait_ge(bb.sem(sm), bm + NT + 2)
                s.copy(gb[:, :], pgb[:, :]).then_inc(bb.sem(ss), 1)
                bb.inc(ss)

            @blk.gpsimd
            def _(g):
                for hm in range(2):
                    i = hm % 2
                    sw1[i].put(g, w1c[i][:, :, :],
                               _f0_w13(io, pfx, 0, hm, 1).rearrange(
                                   "(kt p) n -> p kt n", p=128))
                    sw3[i].put(g, w3c[i][:, :, :],
                               _f0_w13(io, pfx, 0, hm, 3).rearrange(
                                   "(kt p) n -> p kt n", p=128))


def _moe_expert(bb, ex, xtr, gT, onesf, id8, acc, sor, pfx, NHT,
                gb, grow, heT, tmp, s1b, w1c, w3c, w2c, orow, sw1, sw3, sw2):
    nc = bb.nc
    io = bb.io
    un = pfx + bb.u()
    shared = (ex == NE)
    VPH = 1 if shared else 3

    def w2_ap(md):
        if shared:
            return io[pfx + "_sw2"][:, md * 128:(md + 1) * 128]
        return io[pfx + "_w2"][ex, :, md * 128:(md + 1) * 128]

    with ExitStack() as stu:
        pu = stu.enter_context(nc.psum_tensor(un + "pu", [128, 2, S], F32))
        sm, ss, sv = "BUm", "BUs", "BUv"
        for _x in (sm, ss, sv):
            bb.sem(_x)
        bm, bs, bv = bb.cnt[sm], bb.cnt[ss], bb.cnt[sv]

        with nc.Block() as blk:
            @blk.gpsimd
            def _(g):
                for hm in range(2, NHT):
                    g.wait_ge(bb.sem(sm), bm + 2 * (hm - 1))
                    i = hm % 2
                    sw1[i].put(g, w1c[i][:, :, :],
                               _f0_w13(io, pfx, ex, hm, 1).rearrange(
                                   "(kt p) n -> p kt n", p=128))
                    sw3[i].put(g, w3c[i][:, :, :],
                               _f0_w13(io, pfx, ex, hm, 3).rearrange(
                                   "(kt p) n -> p kt n", p=128))

            @blk.tensor
            def _(t):
                for hm in range(NHT):
                    i = hm % 2
                    sw1[i].wait(t, 16 * (7 * ex + hm // 2 + 1))
                    sw3[i].wait(t, 16 * (7 * ex + hm // 2 + 1))
                    if hm >= 1:
                        t.wait_ge(bb.sem(sv), bv + VPH * hm)
                        if shared:
                            t.wait_ge(bb.sem(ss), bs + hm)
                    for k in range(ND):
                        for nb in range(NB):
                            mm = t.matmul(
                                pu[:, 0, nb * 512:(nb + 1) * 512],
                                w1c[i][:, k, :],
                                xtr[:, k, nb * 512:(nb + 1) * 512],
                                start=(k == 0), stop=(k == ND - 1),
                                skip_group_check=True)
                    mm.then_inc(bb.sem(sm), 1)
                    bb.inc(sm)
                    for k in range(ND):
                        for nb in range(NB):
                            mm = t.matmul(
                                pu[:, 1, nb * 512:(nb + 1) * 512],
                                w3c[i][:, k, :],
                                xtr[:, k, nb * 512:(nb + 1) * 512],
                                start=(k == 0), stop=(k == ND - 1),
                                skip_group_check=True)
                    mm.then_inc(bb.sem(sm), 1)
                    bb.inc(sm)

            @blk.scalar
            def _(s):
                for hm in range(NHT):
                    if shared:
                        s.wait_ge(bb.sem(sm), bm + 2 * hm + 1)
                        if hm >= 1:
                            s.wait_ge(bb.sem(sv), bv + VPH * hm)
                        s.activation(s1b[:, :], pu[:, 0, :], AF.Silu
                                     ).then_inc(bb.sem(ss), 1)
                    else:
                        s.wait_ge(bb.sem(sv), bv + VPH * hm + 1)
                        s.activation(s1b[:, :], tmp[:, 0, :], AF.Silu
                                     ).then_inc(bb.sem(ss), 1)
                    bb.inc(ss)

            @blk.vector
            def _(v):
                for hm in range(NHT):
                    if shared:
                        v.wait_ge(bb.sem(sm), bm + 2 * hm + 2)
                        v.wait_ge(bb.sem(ss), bs + hm + 1)
                        v.tensor_tensor(heT[:, hm, :], s1b[:, :], pu[:, 1, :],
                                        op=ALU.mult).then_inc(bb.sem(sv), 1)
                        bb.inc(sv)
                    else:
                        v.wait_ge(bb.sem(sm), bm + 2 * hm + 1)
                        v.tensor_tensor(tmp[:, 0, :], pu[:, 0, :], gb[:, :],
                                        op=ALU.mult).then_inc(bb.sem(sv), 1)
                        bb.inc(sv)
                        v.wait_ge(bb.sem(sm), bm + 2 * hm + 2)
                        v.tensor_tensor(tmp[:, 1, :], pu[:, 1, :], gb[:, :],
                                        op=ALU.mult).then_inc(bb.sem(sv), 1)
                        bb.inc(sv)
                        v.wait_ge(bb.sem(ss), bs + hm + 1)
                        v.tensor_tensor(heT[:, hm, :], s1b[:, :], tmp[:, 1, :],
                                        op=ALU.mult).then_inc(bb.sem(sv), 1)
                        bb.inc(sv)

    with ExitStack() as std:
        pd = std.enter_context(nc.psum_tensor(un + "pd", [128, 2, S], F32))
        sm, ss = "BDm", "BDs"
        bb.sem(sm), bb.sem(ss)
        bm0, bs0 = bb.cnt[sm], bb.cnt[ss]
        nxt = ex + 1
        do_gb = (nxt < NE)
        NGB = 2 if do_gb else 0
        bm, bs = bm0 + NGB, bs0 + NGB
        acc_base = [sor[0].n, sor[1].n]

        with nc.Block() as blk:
            @blk.tensor
            def _(t):
                if do_gb:
                    for nb in range(NB):
                        mm = t.matmul(pd[0:1, 0, nb * 512:(nb + 1) * 512],
                                      id8[:, nxt:nxt + 1],
                                      gT[0:8, nb * 512:(nb + 1) * 512],
                                      start=True, stop=True,
                                      skip_group_check=True)
                    mm.then_inc(bb.sem(sm), 1)
                    bb.inc(sm)
                    t.wait_ge(bb.sem(ss), bs0 + 1)
                    for nb in range(NB):
                        mm = t.matmul(pd[:, 0, nb * 512:(nb + 1) * 512],
                                      onesf[0:1, :],
                                      grow[0:1, nb * 512:(nb + 1) * 512],
                                      start=True, stop=True,
                                      skip_group_check=True)
                    mm.then_inc(bb.sem(sm), 1)
                    bb.inc(sm)
                for md in range(ND):
                    i = md % 2
                    sw2[i].wait(t, 16 * (8 * ex + md // 2 + 1))
                    if md == 0 and do_gb:
                        t.wait_ge(bb.sem(ss), bs0 + 2)
                    if md >= 2:
                        t.wait_ge(bb.sem(ss), bs + md - 1)
                    for k in range(NHT):
                        for nb in range(NB):
                            mm = t.matmul(
                                pd[:, i, nb * 512:(nb + 1) * 512],
                                w2c[i][:, k, :],
                                heT[:, k, nb * 512:(nb + 1) * 512],
                                start=(k == 0), stop=(k == NHT - 1),
                                skip_group_check=True)
                    mm.then_inc(bb.sem(sm), 1)
                    bb.inc(sm)

            @blk.scalar
            def _(s):
                if do_gb:
                    s.wait_ge(bb.sem(sm), bm0 + 1)
                    s.copy(grow[:, :], pd[0:1, 0, :]).then_inc(bb.sem(ss), 1)
                    bb.inc(ss)
                    s.wait_ge(bb.sem(sm), bm0 + 2)
                    s.copy(gb[:, :], pd[:, 0, :]).then_inc(bb.sem(ss), 1)
                    bb.inc(ss)
                for md in range(ND):
                    i = md % 2
                    s.wait_ge(bb.sem(sm), bm + md + 1)
                    if md < 2:
                        sor[i].wait(s, acc_base[i])
                    else:
                        sor[i].wait(s, acc_base[i] + 16 * (md // 2))
                    s.copy(orow[i][:, :], pd[:, i, :]).then_inc(bb.sem(ss), 1)
                    bb.inc(ss)

            @blk.gpsimd
            def _(g):
                for md in range(ND + 2):
                    if md < ND:
                        if md >= 2:
                            g.wait_ge(bb.sem(sm), bm + md - 1)
                        i = md % 2
                        sw2[i].put(g, w2c[i][:, :, :],
                                   w2_ap(md).rearrange("(kt p) n -> p kt n",
                                                       p=128))
                    if md >= 2:
                        if md == 2:
                            sor[0].wait(g, acc_base[0])
                            sor[1].wait(g, acc_base[1])
                        md_ = md - 2
                        g.wait_ge(bb.sem(ss), bs + md_ + 1)
                        i = md_ % 2
                        if ex == 0:
                            sor[i].put(g, acc[md_], orow[i][:, :])
                        else:
                            sor[i].put(g, acc[md_], orow[i][:, :],
                                       accum_op=ALU.add)
                if ex < NE:
                    for hm in range(2):
                        i = hm % 2
                        sw1[i].put(g, w1c[i][:, :, :],
                                   _f0_w13(io, pfx, nxt, hm, 1).rearrange(
                                       "(kt p) n -> p kt n", p=128))
                        sw3[i].put(g, w3c[i][:, :, :],
                                   _f0_w13(io, pfx, nxt, hm, 3).rearrange(
                                       "(kt p) n -> p kt n", p=128))
                else:
                    sor[0].wait(g)
                    sor[1].wait(g)


def _prep_all(inputs):
    perm = _rope_perm()
    w = _prep_l0_attn(inputs, perm)
    f32 = lambda k: np.asarray(inputs[k], np.float32)
    w["f0_gate"] = np.ascontiguousarray(f32("L0_ffn_gate_w"))
    w["f0_w1"] = np.ascontiguousarray(f32("L0_ffn_w1"))
    w["f0_w2"] = np.ascontiguousarray(f32("L0_ffn_w2"))
    w["f0_w3"] = np.ascontiguousarray(f32("L0_ffn_w3"))
    w["f0_sw1"] = np.ascontiguousarray(f32("L0_ffn_sw1"))
    w["f0_sw2"] = np.ascontiguousarray(f32("L0_ffn_sw2"))
    w["f0_sw3"] = np.ascontiguousarray(f32("L0_ffn_sw3"))
    w["f0_ebias"] = np.ascontiguousarray(
        np.broadcast_to(f32("L0_ffn_expert_bias")[None, :], (128, NE)))
    w["m1_wvT"] = np.ascontiguousarray(f32("L1_attn_Wv").T)
    w["m1_woT"] = np.ascontiguousarray(f32("L1_attn_Wo").T)
    w["m1_bv"] = np.ascontiguousarray(f32("L1_attn_bv")[None, :])
    w["m1_bo"] = np.ascontiguousarray(f32("L1_attn_bo")[None, :])
    w["f1_w1T"] = np.ascontiguousarray(f32("L1_ffn_W1").T)
    w["f1_w3T"] = np.ascontiguousarray(f32("L1_ffn_W3").T)
    w["f1_w2T"] = np.ascontiguousarray(f32("L1_ffn_W2").T)
    return w


_SHAPES = {
    "a0_wqT": [D, D], "a0_wkT": [D, D], "a0_wvT": [D, D], "a0_woT": [D, D],
    "f0_gate": [D, NE], "f0_w1": [NE, D, HID0], "f0_w2": [NE, HID0, D],
    "f0_w3": [NE, D, HID0], "f0_sw1": [D, HID0], "f0_sw2": [HID0, D],
    "f0_sw3": [D, HID0], "f0_ebias": [128, NE],
    "m1_wvT": [D, D], "m1_woT": [D, D], "m1_bv": [1, D], "m1_bo": [1, D],
    "f1_w1T": [D, HID1], "f1_w3T": [D, HID1], "f1_w2T": [HID1, D],
}

_BLOCK_WEIGHTS = {
    "a0": ["a0_wqT", "a0_wkT", "a0_wvT", "a0_woT"],
    "f0": ["f0_gate", "f0_w1", "f0_w2", "f0_w3", "f0_sw1", "f0_sw2", "f0_sw3",
           "f0_ebias"],
    "m1": ["m1_wvT", "m1_woT", "m1_bv", "m1_bo"],
    "f1": ["f1_w1T", "f1_w3T", "f1_w2T"],
}

_EMITTERS = {"a0": emit_l0_attn, "f0": emit_l0_moe, "m1": emit_l1_mix,
             "f1": emit_l1_ffn}


def _declare_block_io(nc, bb, pfx):
    for nm in _BLOCK_WEIGHTS[pfx]:
        bb.io[nm] = nc.dram_tensor(nm, _SHAPES[nm], F32,
                                   kind="ExternalInput").ap()
    bb.io[pfx + "_xt"] = nc.dram_tensor(pfx + "_xt", [ND, NT2, 128, 256], BF16,
                                        kind="Internal").ap()
    if pfx == "a0":
        bb.io["a0_vscr"] = nc.dram_tensor("a0_vscr", [NH, NT, 128, 128], BF16,
                                          kind="Internal").ap()
    if pfx in ("f0", "f1"):
        bb.io[pfx + "_acc"] = nc.dram_tensor(pfx + "_acc", [ND, 128, S], F32,
                                             kind="Internal").ap()
    if pfx == "f1":
        bb.io["f1_w2f"] = nc.dram_tensor("f1_w2f", [HID1, D], BF16,
                                         kind="Internal").ap()


def build_block(pfx):
    nc = bass.Bass("TRN2", target_bir_lowering=False)
    with ExitStack() as ctx:
        bb = BB(nc, ctx)
        x_in = nc.dram_tensor("x", [S, D], F32, kind="ExternalInput").ap()
        x_out = nc.dram_tensor("y", [S, D], F32, kind="ExternalOutput").ap()
        _declare_common(nc, bb)
        _declare_block_io(nc, bb, pfx)
        _EMITTERS[pfx](bb, x_in, x_out, pfx)
    return nc


_NC_CACHE = {}


def kernel_unfused(**inputs):
    x = np.asarray(inputs["L0_attn_x"], np.float32)
    ncores = x.shape[0]
    wmap = _prep_all(inputs)
    cmap = {"c_" + k: v for k, v in _consts().items()}

    cur = [np.ascontiguousarray(x[c]) for c in range(ncores)]
    for pfx in ["a0", "f0", "m1", "f1"]:
        if pfx not in _NC_CACHE:
            _NC_CACHE[pfx] = build_block(pfx)
        nc = _NC_CACHE[pfx]
        in_maps = []
        for c in range(ncores):
            m = {"x": cur[c]}
            for nm in _BLOCK_WEIGHTS[pfx]:
                m[nm] = wmap[nm]
            m.update(cmap)
            in_maps.append(m)
        res = run_bass_kernel_spmd(nc, in_maps, core_ids=list(range(ncores)))
        cur = [np.asarray(res.results[c]["y"]) for c in range(ncores)]
    return np.stack(cur).astype(np.float32)


def build_fused():
    nc = bass.Bass("TRN2", target_bir_lowering=False)
    with ExitStack() as ctx:
        bb = BB(nc, ctx)
        x_in = nc.dram_tensor("x", [S, D], F32, kind="ExternalInput").ap()
        y = nc.dram_tensor("y", [S, D], F32, kind="ExternalOutput").ap()
        x1 = nc.dram_tensor("x1", [S, D], F32, kind="Internal").ap()
        x2 = nc.dram_tensor("x2", [S, D], F32, kind="Internal").ap()
        x3 = nc.dram_tensor("x3", [S, D], F32, kind="Internal").ap()
        _declare_common(nc, bb)
        for pfx in ("a0", "f0", "m1", "f1"):
            _declare_block_io(nc, bb, pfx)
        emit_l0_attn(bb, x_in, x1, "a0")
        emit_l0_moe(bb, x1, x2, "f0")
        emit_l1_mix(bb, x2, x3, "m1")
        emit_l1_ffn(bb, x3, y, "f1")
    return nc


def kernel(**inputs):
    x = np.asarray(inputs["L0_attn_x"], np.float32)
    ncores = x.shape[0]
    wmap = _prep_all(inputs)
    cmap = {"c_" + k: v for k, v in _consts().items()}

    if "fused" not in _NC_CACHE:
        _NC_CACHE["fused"] = build_fused()
    nc = _NC_CACHE["fused"]
    in_maps = []
    for c in range(ncores):
        m = {"x": np.ascontiguousarray(x[c])}
        m.update(wmap)
        m.update(cmap)
        in_maps.append(m)
    res = run_bass_kernel_spmd(nc, in_maps, core_ids=list(range(ncores)))
    return np.stack([np.asarray(res.results[c]["y"]) for c in range(ncores)]
                    ).astype(np.float32)
